# version 12
# baseline (speedup 1.0000x reference)
import math
import numpy as np
import ml_dtypes
import concourse.bass as bass
import concourse.mybir as mybir
from concourse.bass_utils import run_bass_kernel_spmd

F32 = mybir.dt.float32
BF16 = mybir.dt.bfloat16
I32 = mybir.dt.int32
AF = mybir.ActivationFunctionType
ALU = mybir.AluOpType
AX = mybir.AxisListType

TOK, MT, D, KC = 1024, 8, 2048, 16
H, DH, NB, E, F, HID = 16, 128, 4, 8, 2048, 8192

EPS_LN = 1e-5
EPS_RMS = 1e-6
GC_ = 1.0 / math.e
GL_ = 0.5 - math.log(4.0 / 3.0)
GU_ = 0.5
INV2PI = float(np.float32(1.0 / (2 * math.pi)))
PI = float(np.float32(math.pi))
PI_2 = float(np.float32(math.pi / 2))
T2PI_HI = float(np.float32(2 * math.pi))
T2PI_LO = float(2 * np.float64(math.pi) - np.float64(np.float32(2 * math.pi)))
ISQD = float(np.float32(1.0 / math.sqrt(DH)))
MAGIC = 16384.5
NMAGIC = -16384.0

_DEV = False
_BLKN = 64
_BLKMERGE = {"vector", "scalar"}


_DP_OPS = {
    "tensor_copy", "tensor_tensor", "tensor_add", "tensor_sub", "tensor_mul",
    "tensor_max", "tensor_scalar", "tensor_single_scalar", "tensor_scalar_add",
    "tensor_scalar_sub", "tensor_scalar_mul", "tensor_scalar_max",
    "tensor_scalar_min", "tensor_relu", "tensor_reduce", "reduce_max",
    "reduce_sum", "scalar_tensor_tensor", "reciprocal", "memset", "memzero",
    "select", "transpose", "activation", "sign", "square", "sqrt", "mul",
    "copy", "add", "dma_start",
}
_DP_ALWAYS = {"reciprocal"}


def _ap_name(a):
    try:
        n = a.name
        return n() if callable(n) else n
    except Exception:
        return None


class DrainProxy:

    def __init__(self, eng, state):
        self._e = eng
        self._w, self._r = state

    def _flush(self):
        self._e.drain()
        self._w.clear()
        self._r.clear()

    def __getattr__(self, n):
        a = getattr(self._e, n)
        if n not in _DP_OPS:
            return a

        def wrap(*args, **kw):
            names = []
            for x in list(args) + list(kw.values()):
                nm = _ap_name(x)
                if nm:
                    names.append(nm)
            outn = names[0] if names else None
            ins = set(names[1:])
            if (ins & self._w) or (outn in self._w) or (outn in self._r):
                self._flush()
            r = a(*args, **kw)
            if n in _DP_ALWAYS:
                self._flush()
            else:
                if outn:
                    self._w.add(outn)
                self._r |= ins
            return r
        return wrap


class TProxy:

    def __init__(self, eng):
        self._e = eng
        self.last = None

    def __getattr__(self, n):
        a = getattr(self._e, n)
        if n in ("matmul", "transpose"):
            def wrap(*args, **kw):
                r = a(*args, **kw)
                self.last = r
                return r
            return wrap
        return a


class Ctr:
    def __init__(self, ctx, nc, name):
        self.sem = ctx.enter_context(nc.semaphore(name))
        self.n = 0

    def inc(self, instr, k=1):
        instr.then_inc(self.sem, k)
        self.n += k
        return instr

    def dinc(self, instr):
        return self.inc(instr, 16)

    def wait(self, eng, n=None):
        eng.wait_ge(self.sem, self.n if n is None else n)


class Slot:
    def __init__(self, ap):
        self.ap = ap
        self.rd = None

    def guard(self, eng):
        if self.rd is not None:
            c, n = self.rd
            eng.wait_ge(c.sem, n)

    def consumed(self, ctr):
        self.rd = (ctr, ctr.n)


def build(dev_taps=()):
    nc = bass.Bass("TRN2", target_bir_lowering=False, detect_race_conditions=False)
    di = {}

    def din(name, shape, dt=F32):
        di[name] = nc.dram_tensor(name, shape, dt, kind="ExternalInput")
        return di[name]

    x0 = din("x0", [TOK, D])
    wqkv = din("wqkv", [128, KC * 3 * D])
    wgate = din("wgate", [128, KC * D])
    wout = din("wout", [128, KC * D])
    wg = din("wg", [E, 128, KC * F])
    wu = din("wu", [E, 128, KC * F])
    wd = din("wd", [E, 128, KC * D])
    ga = din("ga", [2, 128, KC * 64])
    gbp = din("gbp", [2, 128, F])
    ua = din("ua", [2, 128, KC * 64])
    ubp = din("ubp", [2, 128, F])
    da = din("da", [2, 128, KC * 64])
    dbp = din("dbp", [2, 128, D])
    wr = din("wr", [128, KC * E])
    wq1 = din("wq1", [128, KC * D])
    wk1 = din("wk1", [128, KC * D])
    wv1 = din("wv1", [128, KC * D])
    wo1 = din("wo1", [128, KC * D])
    wgt1 = din("wgt1", [128, KC * HID])
    wvt1 = din("wvt1", [128, KC * HID])
    wot1 = din("wot1", [128, 64 * D])
    gb_r = din("gb_r", [128, HID])
    vb_r = din("vb_r", [128, HID])
    ob_r = din("ob_r", [128, D])
    l0nw_r = din("l0nw_r", [128, D])
    qnw_r = din("qnw_r", [128, D])
    knw_r = din("knw_r", [128, D])
    slope_r = din("slope_r", [128, H])
    temp_r = din("temp_r", [128, 1])
    gw_sl = din("gw_sl", [256, HID])
    vw_sl = din("vw_sl", [256, HID])
    ow_sl = din("ow_sl", [1024, D])
    posb = din("posb", [128, 1])
    chunkid = din("chunkid", [1, 1], I32)
    ldist = din("ldist", [128, 4])
    lmask = din("lmask", [128, 4])
    arq = din("arq", [128, 2])
    ark = din("ark", [128, 2])
    mask8_r = din("mask8_r", [128, MT * 1152], BF16)
    invf_hi = din("invf_hi", [128, 64])
    invf_lo = din("invf_lo", [128, 64])

    out = nc.dram_tensor("out", [TOK, D], F32, kind="ExternalOutput")

    cc_a_in = nc.dram_tensor("cc_a_in", [H * 128, 128], F32)
    cc_a_out = nc.dram_tensor("cc_a_out", [4 * H * 128, 128], F32)
    cc_kv_in = nc.dram_tensor("cc_kv_in", [2 * TOK, D], BF16)
    cc_kv_out = nc.dram_tensor("cc_kv_out", [4 * 2 * TOK, D], BF16)
    cc_g_in = nc.dram_tensor("cc_g_in", [128, 4], F32)
    cc_g_out = nc.dram_tensor("cc_g_out", [128, 4], F32)
    x1d = nc.dram_tensor("x1d", [TOK, D], F32)
    s0d = nc.dram_tensor("s0d", [TOK, D], BF16)
    s1d = nc.dram_tensor("s1d", [TOK, D], BF16)
    midd = nc.dram_tensor("midd", [TOK, HID], BF16)
    qd_d = nc.dram_tensor("qd_d", [TOK, D], BF16)
    attn1d = nc.dram_tensor("attn1d", [TOK, D], BF16)
    kd_d = nc.dram_tensor("kd_d", [TOK, D], BF16)
    vd_d = nc.dram_tensor("vd_d", [TOK, D], BF16)
    accd = nc.dram_tensor("accd", [TOK, D], F32)

    taps = {}
    if _DEV:
        for tname, shape, tdt in dev_taps:
            taps[tname] = nc.dram_tensor("tap_" + tname, list(shape), tdt,
                                         kind="ExternalOutput")

    from contextlib import ExitStack
    with ExitStack() as ctx:
        def sb(name, shape, dt):
            return ctx.enter_context(nc.sbuf_tensor(name, shape, dt))

        dg = Ctr(ctx, nc, "dg")
        dh_ = Ctr(ctx, nc, "dh")
        sv = Ctr(ctx, nc, "sv")
        ss = Ctr(ctx, nc, "ss")
        st = Ctr(ctx, nc, "st")
        sg = Ctr(ctx, nc, "sg")
        cc = Ctr(ctx, nc, "cc")

        _blkh = [nc.Block(), 0]
        _blkh[0].__enter__()

        def _newblk():
            _blkh[0].__exit__(None, None, None)
            _blkh[0] = nc.Block()
            _blkh[0].__enter__()
            _blkh[1] = 0

        def _emit(kind, fn):
            if kind in _BLKMERGE:
                _blkh[1] += 1
                if _blkh[1] > _BLKN:
                    _newblk()
                    _blkh[1] = 1
                getattr(_blkh[0], kind)(fn)
            else:
                if _blkh[1]:
                    _newblk()
                getattr(_blkh[0], kind)(fn)
                _newblk()

        def _endblk():
            _blkh[0].__exit__(None, None, None)
        ctx.callback(_endblk)

        def G(fn):
            _emit("gpsimd", fn)

        vstate = (set(), set())
        sstate = (set(), set())

        def V(fn):
            _emit("vector", lambda v: fn(DrainProxy(v, vstate)))

        def S(fn):
            _emit("scalar", lambda s: fn(DrainProxy(s, sstate)))

        def T(fn):
            _emit("tensor", lambda t: fn(TProxy(t)))

        def Y(fn):
            _emit("sync", fn)

        pbanks = [Slot(ctx.enter_context(
            nc.psum_tensor("pb%d" % i, [128, 512], F32))) for i in range(8)]

        def fence():
            def fg(g):
                g.wait_ge(sv.sem, sv.n)
                g.wait_ge(ss.sem, ss.n)
                g.wait_ge(st.sem, st.n)
                g.wait_ge(dh_.sem, dh_.n)
                g.wait_ge(dg.sem, dg.n)
                g.sem_inc(sg.sem, 1)
                sg.n += 1
            G(fg)

        def tap_out(name, src_dram=None, src_sb=None, rows=None):
            if not _DEV or name not in taps:
                return
            def tg(g, name=name):
                sv.wait(g); ss.wait(g); st.wait(g)
                if src_sb is not None:
                    dg.dinc(g.dma_start(out=taps[name].ap(), in_=src_sb))
                else:
                    dg.dinc(g.dma_start(out=taps[name].ap(), in_=src_dram))
            G(tg)

        slope_s = sb("slope_s", [128, H], F32)
        temp_s = sb("temp_s", [128, 1], F32)
        invT = sb("invT", [128, 1], F32)
        posb_s = sb("posb_s", [128, 1], F32)
        chid_s = sb("chid_s", [1, 1], I32)
        ldist_s = sb("ldist_s", [128, 4], F32)
        lmask_s = sb("lmask_s", [128, 4], F32)
        arq_s = sb("arq_s", [128, 2], F32)
        ark_s = sb("ark_s", [128, 2], F32)
        invfh_s = sb("invfh_s", [128, 64], F32)
        invfl_s = sb("invfl_s", [128, 64], F32)
        io_p = sb("io_p", [128, 1], I32)
        io_pf = sb("io_pf", [128, 1], F32)
        bd_s = sb("bd_s", [128, H], F32)
        bdp_s = sb("bdp_s", [128, H], F32)
        qd_s = sb("qd_s", [128, 2 * H], F32)
        kd_s = sb("kd_s", [128, 2 * H], F32)
        coef_s = sb("coef_s", [128, 4 * H], F32)
        mu0 = sb("mu0", [128, MT], F32)
        rstd0 = sb("rstd0", [128, MT], F32)
        mu1 = sb("mu1", [128, MT], F32)
        rstd1 = sb("rstd1", [128, MT], F32)
        gx1 = sb("gx1", [128, MT], F32)
        sx1 = sb("sx1", [128, MT], F32)
        nsc1g = sb("nsc1g", [128, MT], F32)
        nsc1v = sb("nsc1v", [128, MT], F32)
        gx2 = sb("gx2", [128, MT], F32)
        sx2 = sb("sx2", [128, MT], F32)
        nsc3 = sb("nsc3", [128, MT], F32)
        thr = {}
        nthr = {}
        gamv = {}
        for nm in ("g", "v", "o"):
            thr[nm] = sb("thr_" + nm, [128, 1], F32)
            nthr[nm] = sb("nthr_" + nm, [128, 1], F32)
            gamv[nm] = sb("gam_" + nm, [128, 1], F32)
        wcols = sb("wcols", [128, MT * E], F32)
        colA = sb("colA", [128, 1], F32)
        colB = sb("colB", [128, 1], F32)
        colC = sb("colC", [128, 1], F32)
        colD = sb("colD", [128, 1], F32)
        cgc = sb("cgc", [128, 1], F32)
        cpi = sb("cpi", [128, 1], F32)
        cpi2 = sb("cpi2", [128, 1], F32)
        ncos8 = sb("ncos8", [128, MT * 64], F32)
        nsin8 = sb("nsin8", [128, MT * 64], F32)
        ones_bf = sb("ones_bf", [128, 128], BF16)
        onesf = sb("onesf", [128, 128], F32)
        sqs = sb("sqs", [128, D], F32)

        def setup_g(g):
            for dst, src in [(slope_s, slope_r), (temp_s, temp_r),
                             (posb_s, posb), (ldist_s, ldist),
                             (lmask_s, lmask), (arq_s, arq), (ark_s, ark),
                             (invfh_s, invf_hi), (invfl_s, invf_lo)]:
                dg.dinc(g.dma_start(out=dst[:, :], in_=src[:, :]))
            dg.dinc(g.dma_start(out=chid_s[:, :], in_=chunkid[:, :]))
            g.iota(io_p[:, :], [[0, 1]], channel_multiplier=1)
            g.sem_inc(sg.sem, 1)
            sg.n += 1
        G(setup_g)

        def setup_v(v):
            dg.wait(v)
            sg.wait(v)
            v.tensor_copy(io_pf[:, :], io_p[:, :])
            v.memset(ones_bf[:, :], 1.0)
            v.memset(onesf[:, :], 1.0)
            v.memset(cgc[:, :], -GC_)
            v.memset(cpi[:, :], -PI)
            v.memset(cpi2[:, :], -PI_2)
            v.reciprocal(invT[:, :], temp_s[:, :])
            v.drain()
            for h in range(H):
                sc = slope_s[:, h:h + 1]
                v.tensor_scalar(bd_s[:, h:h + 1], sc, -256.0, None, ALU.mult)
                v.tensor_scalar(qd_s[:, 2 * h:2 * h + 2], arq_s[:, :], sc,
                                -1.0, ALU.mult, ALU.mult)
                v.tensor_scalar(kd_s[:, 2 * h:2 * h + 2], ark_s[:, :], sc,
                                -1.0, ALU.mult, ALU.mult)
                v.tensor_scalar(coef_s[:, 4 * h:4 * h + 4], ldist_s[:, :], sc,
                                -1.0, ALU.mult, ALU.mult)
            sv.inc(v.engine_nop())
        V(setup_v)

        def setup_s(s):
            sv.wait(s)
            s.activation(bd_s[:, :], bd_s[:, :], AF.Exp)
            s.activation(qd_s[:, :], qd_s[:, :], AF.Exp)
            s.activation(kd_s[:, :], kd_s[:, :], AF.Exp)
            ss.inc(s.activation(coef_s[:, :], coef_s[:, :], AF.Exp))
        S(setup_s)

        def setup_v2(v):
            ss.wait(v)
            for h in range(H):
                v.tensor_mul(coef_s[:, 4 * h:4 * h + 4],
                             coef_s[:, 4 * h:4 * h + 4], lmask_s[:, :])
            sv.inc(v.engine_nop())
        V(setup_v2)

        _ps1 = ExitStack()
        gwbuf = _ps1.enter_context(nc.sbuf_tensor("gwbuf", [128, 2 * HID], F32))
        gcol = _ps1.enter_context(nc.sbuf_tensor("gcol", [128, 4], F32))
        if True:
            for j, (src, width) in enumerate([(gw_sl, 2 * HID),
                                              (vw_sl, 2 * HID),
                                              (ow_sl, 8 * D)]):
                def g_load(g, src=src, width=width, j=j):
                    if j > 0:
                        sv.wait(g)
                    na = width // HID if width >= HID else 1
                    na = src.shape[0] // 128
                    dg.dinc(g.dma_start(
                        out=gwbuf[:, 0:width].rearrange("p (a n) -> p a n",
                                                        a=na),
                        in_=src.ap().rearrange("(a p) n -> p a n", p=128)))
                G(g_load)

                def v_red(v, j=j, width=width):
                    dg.wait(v)
                    sv.inc(v.tensor_reduce(gcol[:, j:j + 1], gwbuf[:, 0:width],
                                           AX.X, ALU.add,
                                           apply_absolute_value=True))
                V(v_red)

            def v_fin(v):
                v.memset(gcol[:, 3:4], 0.0)
                sv.inc(v.engine_nop())
            V(v_fin)

            def g_exp(g):
                sv.wait(g)
                dg.dinc(g.dma_start(out=cc_g_in[:, :], in_=gcol[:, :]))
            G(g_exp)

        _ps1.close()
        _ps2 = ExitStack()
        xn_fm0 = _ps2.enter_context(nc.sbuf_tensor("xn_fm0", [128, MT * D], BF16))
        if True:
            def xn0_lhs(m, k):
                return xn_fm0[:, (m * KC + k) * 128:(m * KC + k) * 128 + 128]

            _ps3 = ExitStack()
            xt0 = _ps3.enter_context(nc.sbuf_tensor("xt0", [128, 2 * D], F32))
            xnt = _ps3.enter_context(nc.sbuf_tensor("xnt", [128, 2 * D], BF16))
            if True:
                for m in range(MT):
                    pb = m % 2

                    def p1_g(g, m=m, pb=pb):
                        if m >= 2:
                            dh_.wait(g, 16 * (m - 1))
                        dg.dinc(g.dma_start(out=xt0[:, pb * D:(pb + 1) * D],
                                            in_=x0[128 * m:128 * (m + 1), :]))
                    G(p1_g)

                    def p1_s(s, m=m, pb=pb):
                        dg.wait(s)
                        ss.inc(s.activation(sqs[:, :], xt0[:, pb * D:(pb + 1) * D], AF.Square,
                                            accum_out=colB[:, 0:1]))
                    S(p1_s)

                    def p1_v(v, m=m, pb=pb):
                        dg.wait(v)
                        ss.wait(v)
                        v.tensor_reduce(colA[:, :], xt0[:, pb * D:(pb + 1) * D], AX.X, ALU.add)
                        v.tensor_scalar(mu0[:, m:m + 1], colA[:, :], 1.0 / D,
                                        None, ALU.mult)
                        v.tensor_mul(colC[:, :], mu0[:, m:m + 1],
                                     mu0[:, m:m + 1])
                        v.tensor_scalar(colD[:, :], colB[:, :], 1.0 / D, None,
                                        ALU.mult)
                        v.tensor_sub(colC[:, :], colD[:, :], colC[:, :])
                        sv.inc(v.tensor_scalar(colC[:, :], colC[:, :], EPS_LN,
                                               None, ALU.add))
                    V(p1_v)

                    def p1_s2(s, m=m):
                        sv.wait(s)
                        ss.inc(s.activation(colC[:, :], colC[:, :], AF.Sqrt))
                    S(p1_s2)

                    def p1_v2(v, m=m, pb=pb):
                        ss.wait(v)
                        v.reciprocal(rstd0[:, m:m + 1], colC[:, :])
                        v.drain()
                        sv.inc(v.scalar_tensor_tensor(
                            xnt[:, pb * D:(pb + 1) * D], xt0[:, pb * D:(pb + 1) * D], mu0[:, m:m + 1],
                            rstd0[:, m:m + 1].broadcast_to((128, D)),
                            ALU.subtract, ALU.mult))
                    V(p1_v2)

                    def p1_y(sy, m=m, pb=pb):
                        sv.wait(sy)
                        o3 = xn_fm0.ap().rearrange("p (q c) -> p q c", c=128)[
                            :, KC * m:KC * (m + 1), :]
                        dh_.dinc(sy.dma_start(out=o3, in_=xnt[:, pb * D:(pb + 1) * D],
                                              transpose=True))
                    Y(p1_y)

            _ps3.close()
            def emit_proj(wsrc_fn, n_chunks, lhs_fn, evac_fn, wbuf, kc=KC,
                          extra_mm=None, nslots=(0, 1, 2, 3, 4, 5, 6, 7),
                          dbl=True):
                wuse = [0, 0]
                for n in range(n_chunks):
                    wb = (n % 2) if dbl else 0

                    def wload(g, n=n, wb=wb):
                        if dbl and n >= 2:
                            st.wait(g, wuse[wb])
                        elif not dbl and n >= 1:
                            st.wait(g, wuse[0])
                        if dbl:
                            dst = wbuf.ap().rearrange("p (w q) -> p w q",
                                                      w=2)[:, wb]
                        else:
                            dst = wbuf.ap()
                        dg.dinc(g.dma_start(out=dst, in_=wsrc_fn(n)))
                    G(wload)

                    for m in range(MT):
                        slot = pbanks[nslots[m % len(nslots)]]

                        def mm(t, n=n, m=m, wb=wb, slot=slot):
                            if m == 0:
                                dg.wait(t)
                            slot.guard(t)
                            if dbl:
                                wv = wbuf.ap().rearrange("p (w q) -> p w q",
                                                         w=2)[:, wb]
                            else:
                                wv = wbuf.ap()
                            wv = wv.rearrange("p (k n) -> p k n", k=kc)
                            last = (extra_mm is None)
                            for k in range(kc):
                                t.matmul(slot.ap[:, 0:512], lhs_fn(m, k),
                                         wv[:, k], start=(k == 0),
                                         stop=(k == kc - 1 and last))
                            if extra_mm is not None:
                                extra_mm(t, m, n, slot)
                            st.inc(t.last)
                        T(mm)
                        wuse[wb] = st.n
                        evac_fn(m, n, slot)

            def tiled_src(handle, kc):
                def fn(n):
                    return handle.ap().rearrange("p (n r) -> p n r",
                                                 r=kc * 512)[:, n]
                return fn

            if _DEV and "cols" in taps:
                def ctap(g):
                    sv.wait(g)
                    dg.dinc(g.dma_start(
                        out=taps["cols"].ap()[:, 0:MT], in_=mu0[:, :]))
                    dg.dinc(g.dma_start(
                        out=taps["cols"].ap()[:, MT:2 * MT], in_=rstd0[:, :]))
                G(ctap)

            if _DEV and "xnf" in taps:
                def xtap(g):
                    sv.wait(g)
                    dh_.wait(g)
                    dg.dinc(g.dma_start(out=taps["xnf"].ap(),
                                        in_=xn_fm0[:, :]))
                G(xtap)

            from contextlib import nullcontext
            qst = ExitStack()
            qkv_st = qst.enter_context(
                nc.sbuf_tensor("qkv_st", [128, 2 * 512], BF16))
            _ps4 = ExitStack()
            if True:
                _ps5 = ExitStack()
                if True:
                    qkv_dram = [qd_d, kd_d, vd_d]
                    _ps6 = ExitStack()
                    wbufA = _ps6.enter_context(nc.sbuf_tensor("wbufA", [128, 2 * KC * 512], BF16))
                    if True:
                        def t_wt(t):
                            dh_.wait(t)
                        T(t_wt)

                        def evac_qkv(m, n, slot):
                            sh = (n * MT + m) % 2

                            def ev(s, m=m, n=n, slot=slot, sh=sh):
                                st.wait(s)
                                if n * MT + m >= 2:
                                    dh_.wait(s, dh_.n)
                                s.activation(
                                    qkv_st[:, sh * 512:sh * 512 + 512],
                                    slot.ap[:, 0:512], AF.Silu)
                                ss.inc(s.nop())
                                which, ncol = divmod(n, 4)
                                dh_.dinc(s.dma_start(
                                    out=qkv_dram[which][
                                        128 * m:128 * (m + 1),
                                        512 * ncol:512 * (ncol + 1)],
                                    in_=qkv_st[:, sh * 512:sh * 512 + 512]))
                            S(ev)
                            slot.consumed(ss)
                        emit_proj(tiled_src(wqkv, KC), 12, xn0_lhs, evac_qkv,
                                  wbufA)

                    _ps6.close()
                    def g_ar(g):
                        g.wait_ge(dg.sem, dg.n)
                        cc.inc(g.collective_compute(
                            "AllReduce", ALU.add,
                            replica_groups=[[0, 1, 2, 3, 4, 5, 6, 7]],
                            ins=[cc_g_in.ap().opt()],
                            outs=[cc_g_out.ap().opt()]))
                    G(g_ar)
                    qst.close()

                    aost = ExitStack()
                    attn_o = aost.enter_context(
                        nc.sbuf_tensor("attn_o", [128, MT * D], BF16))
                    kvst3 = ExitStack()
                    kvloc = kvst3.enter_context(
                        nc.sbuf_tensor("kvloc", [128, H * 128], F32))
                    kvloc_bf = kvst3.enter_context(
                        nc.sbuf_tensor("kvloc_bf", [128, H * 128], BF16))
                    ps_s = [pbanks[0], pbanks[1]]
                    ps_oi = [pbanks[2], pbanks[3]]
                    ps_ii = [pbanks[4], pbanks[5]]
                    ps_cc = pbanks[6]

                    _ps7 = ExitStack()
                    dmask = _ps7.enter_context(nc.sbuf_tensor("dmask", [128, H * 2 * 256], BF16))
                    idx2 = _ps7.enter_context(nc.sbuf_tensor("idx2", [128, 2 * 256], I32))
                    idx2f = _ps7.enter_context(nc.sbuf_tensor("idx2f", [128, 2 * 256], F32))
                    m01 = _ps7.enter_context(nc.sbuf_tensor("m01", [128, 256], F32))
                    dtmp = _ps7.enter_context(nc.sbuf_tensor("dtmp", [128, 256], F32))
                    qk_fm = _ps7.enter_context(nc.sbuf_tensor("qk_fm", [128, 2 * 4096], BF16))
                    stbf = _ps7.enter_context(nc.sbuf_tensor("stbf", [128, 2 * 256], BF16))
                    vkd2 = _ps7.enter_context(nc.sbuf_tensor("vkd2", [128, 256], BF16))
                    q_bt = _ps7.enter_context(nc.sbuf_tensor("q_bt", [128, 2 * D], BF16))
                    k_bt = _ps7.enter_context(nc.sbuf_tensor("k_bt", [128, 2 * D], BF16))
                    v_bt = _ps7.enter_context(nc.sbuf_tensor("v_bt", [128, 2 * D], BF16))
                    itmp = _ps7.enter_context(nc.sbuf_tensor("itmp", [128, 128], F32))
                    if True:

                        def p3_iota(g):
                            g.iota(idx2[:, 0:256], [[1, 256]], base=0,
                                   channel_multiplier=-1)
                            g.iota(idx2[:, 256:512], [[1, 256]], base=-128,
                                   channel_multiplier=-1)
                            g.sem_inc(sg.sem, 1)
                            sg.n += 1
                        G(p3_iota)

                        def p3_prep(v):
                            sg.wait(v)
                            ss.wait(v)
                            v.tensor_copy(idx2f[:, 0:256], idx2[:, 0:256])
                            v.tensor_copy(idx2f[:, 256:512], idx2[:, 256:512])
                            v.memset(kvloc[:, :], 0.0)
                            v.memset(kvloc_bf[:, :], 0.0)
                            sv.inc(v.engine_nop())
                        V(p3_prep)

                        for h in range(H):
                            for ko in range(2):
                                def dm_v(v, h=h, ko=ko):
                                    v.tensor_scalar(m01[:, :], idx2f[:, ko * 256:ko * 256 + 256],
                                                    0.0, None, ALU.is_ge)
                                    sv.inc(v.tensor_scalar(
                                        dtmp[:, :], idx2f[:, ko * 256:ko * 256 + 256],
                                        slope_s[:, h:h + 1], -1.0, ALU.mult,
                                        ALU.mult))
                                V(dm_v)

                                def dm_s(s, h=h, ko=ko):
                                    sv.wait(s)
                                    ss.inc(s.activation(dtmp[:, :], dtmp[:, :],
                                                        AF.Exp))
                                S(dm_s)

                                def dm_v2(v, h=h, ko=ko):
                                    ss.wait(v)
                                    off = (h * 2 + ko) * 256
                                    sv.inc(v.tensor_mul(
                                        dmask[:, off:off + 256], dtmp[:, :],
                                        m01[:, :]))
                                V(dm_v2)

                        for blk in range(NB):
                            def p3_ld(g, blk=blk):
                                if blk == 0:
                                    dh_.wait(g)
                                else:
                                    st.wait(g)
                                    sv.wait(g)
                                r0 = 256 * blk
                                for src, dst in ((qd_d, q_bt), (kd_d, k_bt),
                                                 (vd_d, v_bt)):
                                    dg.dinc(g.dma_start(
                                        out=dst.ap().rearrange(
                                            "p (t c) -> p t c", t=2),
                                        in_=src.ap()[r0:r0 + 256, :].rearrange(
                                            "(t p) c -> p t c", p=128)))
                            G(p3_ld)

                            def p3_tr(sy, blk=blk):
                                sv.wait(sy)
                                dg.wait(sy)
                                if blk > 0:
                                    st.wait(sy)
                                for mo in range(2):
                                    ap3 = qk_fm.ap().rearrange(
                                        "p (q c) -> p q c", c=128)
                                    dh_.dinc(sy.dma_start(
                                        out=ap3[:, 32 * mo:32 * mo + 16, :],
                                        in_=q_bt[:, mo * D:(mo + 1) * D],
                                        transpose=True))
                                    dh_.dinc(sy.dma_start(
                                        out=ap3[:, 32 * mo + 16:32 * mo + 32, :],
                                        in_=k_bt[:, mo * D:(mo + 1) * D],
                                        transpose=True))
                            Y(p3_tr)

                            def qfm(h, mo):
                                base = mo * 4096 + h * 128
                                return qk_fm[:, base:base + 128]

                            def kfm(h, mo):
                                base = mo * 4096 + 2048 + h * 128
                                return qk_fm[:, base:base + 128]

                            for h in range(H):
                                def p3_mm_s(t, blk=blk, h=h):
                                    if h == 0:
                                        dh_.wait(t)
                                    qrhs = qk_fm.ap().rearrange(
                                        "p (mo r) -> p mo r", mo=2)[
                                        :, :, h * 128:h * 128 + 128]
                                    for ko in range(2):
                                        ps_s[ko].guard(t)
                                        t.matmul(ps_s[ko].ap[:, 0:256],
                                                 kfm(h, ko), qrhs, start=True,
                                                 stop=True)
                                    st.inc(t.last)
                                T(p3_mm_s)

                                def p3_vm(v, blk=blk, h=h):
                                    st.wait(v)
                                    off = (h * 2) * 256
                                    v.tensor_mul(stbf[:, 0:256],
                                                 ps_s[0].ap[:, 0:256],
                                                 dmask[:, off:off + 256])
                                    v.tensor_mul(stbf[:, 256:512],
                                                 ps_s[1].ap[:, 0:256],
                                                 dmask[:, off + 256:off + 512])
                                    for ko in range(2):
                                        vsl = v_bt[:, ko * D + h * 128:
                                                   ko * D + h * 128 + 128]
                                        v.tensor_scalar(
                                            vkd2[:, ko * 128:ko * 128 + 128],
                                            vsl,
                                            kd_s[:, 2 * h + ko:2 * h + ko + 1],
                                            None, ALU.mult)
                                    sv.inc(v.engine_nop())
                                    ps_s[0].consumed(sv)
                                    ps_s[1].consumed(sv)
                                V(p3_vm)

                                def p3_mm2(t, blk=blk, h=h):
                                    sv.wait(t)
                                    for qo in range(2):
                                        ps_oi[qo].guard(t)
                                        ps_ii[qo].guard(t)
                                        for ko in range(2):
                                            vsl = v_bt[:, ko * D + h * 128:
                                                       ko * D + h * 128 + 128]
                                            t.matmul(
                                                ps_oi[qo].ap[:, 0:128],
                                                stbf[:, ko * 256 + qo * 128:
                                                     ko * 256 + qo * 128 + 128],
                                                vsl, start=(ko == 0),
                                                stop=(ko == 1))
                                        t.matmul(ps_ii[qo].ap[:, 0:128],
                                                 qfm(h, qo),
                                                 kvloc_bf[:, h * 128:
                                                          h * 128 + 128],
                                                 start=True, stop=True)
                                    ps_cc.guard(t)
                                    for ko in range(2):
                                        ksl = k_bt[:, ko * D + h * 128:
                                                   ko * D + h * 128 + 128]
                                        t.matmul(ps_cc.ap[:, 0:128], ksl,
                                                 vkd2[:, ko * 128:ko * 128 + 128],
                                                 start=(ko == 0), stop=(ko == 1))
                                    st.inc(t.last)
                                T(p3_mm2)

                                def p3_ev(v, blk=blk, h=h):
                                    st.wait(v)
                                    for qo in range(2):
                                        m = 2 * blk + qo
                                        asl = attn_o[:, m * D + h * 128:
                                                     m * D + h * 128 + 128]
                                        v.tensor_scalar(
                                            itmp[:, :], ps_ii[qo].ap[:, 0:128],
                                            qd_s[:, 2 * h + qo:2 * h + qo + 1],
                                            None, ALU.mult)
                                        v.tensor_add(asl,
                                                     ps_oi[qo].ap[:, 0:128],
                                                     itmp[:, :])
                                    ksl = kvloc[:, h * 128:h * 128 + 128]
                                    v.scalar_tensor_tensor(
                                        ksl, ksl, bd_s[:, h:h + 1],
                                        ps_cc.ap[:, 0:128], ALU.mult, ALU.add)
                                    v.tensor_copy(
                                        kvloc_bf[:, h * 128:h * 128 + 128], ksl)
                                    sv.inc(v.engine_nop())
                                    for sl in (ps_oi[0], ps_oi[1], ps_ii[0],
                                               ps_ii[1], ps_cc):
                                        sl.consumed(sv)
                                V(p3_ev)

                    _ps7.close()
                    def p35_g(g):
                        sv.wait(g)
                        o3 = cc_a_in.ap().rearrange("(h d) e -> d h e", h=H)
                        i3 = kvloc.ap().rearrange("p (h e) -> p h e", h=H)
                        dg.dinc(g.dma_start(out=o3, in_=i3))
                    G(p35_g)

                    def p35_y(g):
                        g.wait_ge(dg.sem, dg.n)
                        cc.inc(g.collective_compute(
                            "AllGather", ALU.bypass,
                            replica_groups=[[0, 1, 2, 3], [4, 5, 6, 7]],
                            ins=[cc_a_in.ap().opt()],
                            outs=[cc_a_out.ap().opt()]))
                    G(p35_y)
                    kvst3.close()

                _ps5.close()
                _ps8 = ExitStack()
                g0 = _ps8.enter_context(nc.sbuf_tensor("g0", [128, MT * D], BF16))
                if True:
                    _ps9 = ExitStack()
                    wbufB = _ps9.enter_context(nc.sbuf_tensor("wbufB", [128, 2 * KC * 512], BF16))
                    if True:
                        def evac_g0(m, n, slot):
                            def ev(s, m=m, n=n, slot=slot):
                                st.wait(s)
                                ss.inc(s.activation(
                                    g0[:, m * D + n * 512:m * D + n * 512 + 512],
                                    slot.ap[:, 0:512], AF.Sigmoid))
                            S(ev)
                            slot.consumed(ss)
                        emit_proj(tiled_src(wgate, KC), 4, xn0_lhs, evac_g0,
                                  wbufB)

                    _ps9.close()
                    _ps10 = ExitStack()
                    tmpA = _ps10.enter_context(nc.sbuf_tensor("tmpA", [128, H * 128], F32))
                    kvin = _ps10.enter_context(nc.sbuf_tensor("kvin", [128, H * 128], F32))
                    kvsc = _ps10.enter_context(nc.sbuf_tensor("kvsc", [128, H * 128], BF16))
                    qf_blk = _ps10.enter_context(nc.sbuf_tensor("qf_blk", [128, 2 * D], BF16))
                    qb5 = _ps10.enter_context(nc.sbuf_tensor("qb5", [128, 2 * D], BF16))
                    itmp2 = _ps10.enter_context(nc.sbuf_tensor("itmp2", [128, 128], F32))
                    if True:

                        def p5_v0(v):
                            cc.wait(v)
                            v.memset(kvin[:, :], 0.0)
                            v.memset(bdp_s[:, :], 1.0)
                            sv.inc(v.engine_nop())
                        V(p5_v0)

                        for cp in range(4):
                            def p5_g(g, cp=cp):
                                sv.wait(g)
                                i3 = cc_a_out.ap()[cp * H * 128:
                                                   (cp + 1) * H * 128, :]
                                i3 = i3.rearrange("(h d) e -> d h e", h=H)
                                o3 = tmpA.ap().rearrange("p (h e) -> p h e",
                                                         h=H)
                                dg.dinc(g.dma_start(out=o3, in_=i3))
                            G(p5_g)

                            def p5_v(v, cp=cp):
                                dg.wait(v)
                                for h in range(H):
                                    v.scalar_tensor_tensor(
                                        kvin[:, h * 128:h * 128 + 128],
                                        tmpA[:, h * 128:h * 128 + 128],
                                        coef_s[:, 4 * h + cp:4 * h + cp + 1],
                                        kvin[:, h * 128:h * 128 + 128],
                                        ALU.mult, ALU.add)
                                sv.inc(v.engine_nop())
                            V(p5_v)

                        for blk in range(NB):
                            def p5_vb(v, blk=blk):
                                if blk > 0:
                                    v.tensor_mul(bdp_s[:, :], bdp_s[:, :],
                                                 bd_s[:, :])
                                for h in range(H):
                                    v.tensor_scalar(
                                        kvsc[:, h * 128:h * 128 + 128],
                                        kvin[:, h * 128:h * 128 + 128],
                                        bdp_s[:, h:h + 1], None, ALU.mult)
                                sv.inc(v.engine_nop())
                            V(p5_vb)

                            def p5_ld(g, blk=blk):
                                sv.wait(g)
                                if blk > 0:
                                    st.wait(g)
                                r0 = 256 * blk
                                dg.dinc(g.dma_start(
                                    out=qb5.ap().rearrange("p (t c) -> p t c",
                                                           t=2),
                                    in_=qd_d.ap()[r0:r0 + 256, :].rearrange(
                                        "(t p) c -> p t c", p=128)))
                            G(p5_ld)

                            def p5_tr(sy, blk=blk):
                                dg.wait(sy)
                                if blk > 0:
                                    st.wait(sy)
                                for mo in range(2):
                                    o3 = qf_blk.ap().rearrange(
                                        "p (q c) -> p q c", c=128)[
                                        :, KC * mo:KC * mo + KC, :]
                                    dh_.dinc(sy.dma_start(
                                        out=o3,
                                        in_=qb5[:, mo * D:(mo + 1) * D],
                                        transpose=True))
                            Y(p5_tr)

                            for h in range(H):
                                def p5_mm(t, blk=blk, h=h):
                                    if h == 0:
                                        dh_.wait(t)
                                        sv.wait(t)
                                    for qo in range(2):
                                        ps_ii[qo].guard(t)
                                        t.matmul(
                                            ps_ii[qo].ap[:, 0:128],
                                            qf_blk[:, qo * D + h * 128:
                                                   qo * D + h * 128 + 128],
                                            kvsc[:, h * 128:h * 128 + 128],
                                            start=True, stop=True)
                                    st.inc(t.last)
                                T(p5_mm)

                                def p5_ev(v, blk=blk, h=h):
                                    st.wait(v)
                                    for qo in range(2):
                                        m = 2 * blk + qo
                                        asl = attn_o[:, m * D + h * 128:
                                                     m * D + h * 128 + 128]
                                        v.tensor_scalar(
                                            itmp2[:, :],
                                            ps_ii[qo].ap[:, 0:128],
                                            qd_s[:, 2 * h + qo:2 * h + qo + 1],
                                            None, ALU.mult)
                                        v.tensor_add(asl, itmp2[:, :], asl)
                                    sv.inc(v.engine_nop())
                                    ps_ii[0].consumed(sv)
                                    ps_ii[1].consumed(sv)
                                V(p5_ev)
                    _ps10.close()
                    tap_out("attn0", src_sb=attn_o[:, :])

                    _ps11 = ExitStack()
                    nwg = _ps11.enter_context(nc.sbuf_tensor("nwg", [128, D], BF16))
                    z_fm = _ps11.enter_context(nc.sbuf_tensor("z_fm", [128, MT * D], BF16))
                    l0nw_s = _ps11.enter_context(
                        nc.sbuf_tensor("l0nw_s", [128, D], F32))
                    if True:
                        def p6_ld(g):
                            sv.wait(g)
                            dg.dinc(g.dma_start(out=l0nw_s[:, :],
                                                in_=l0nw_r[:, :]))
                        G(p6_ld)
                        for m in range(MT):
                            def p6_s(s, m=m):
                                sv.wait(s)
                                if m > 0:
                                    dh_.wait(s)
                                ss.inc(s.activation(
                                    sqs[:, :], attn_o[:, m * D:(m + 1) * D],
                                    AF.Square, accum_out=colB[:, 0:1]))
                            S(p6_s)

                            def p6_v(v, m=m):
                                ss.wait(v)
                                sv.inc(v.tensor_scalar(colC[:, :], colB[:, :],
                                                       1.0 / D, EPS_RMS,
                                                       ALU.mult, ALU.add))
                            V(p6_v)

                            def p6_s2(s, m=m):
                                sv.wait(s)
                                ss.inc(s.activation(colC[:, :], colC[:, :],
                                                    AF.Sqrt))
                            S(p6_s2)

                            def p6_v2(v, m=m):
                                ss.wait(v)
                                if m == 0:
                                    dg.wait(v)
                                v.reciprocal(colD[:, :], colC[:, :])
                                v.drain()
                                v.tensor_mul(nwg[:, :],
                                             g0[:, m * D:(m + 1) * D],
                                             l0nw_s[:, :])
                                v.scalar_tensor_tensor(
                                    attn_o[:, m * D:(m + 1) * D],
                                    attn_o[:, m * D:(m + 1) * D],
                                    colD[:, 0:1], nwg[:, :], ALU.mult, ALU.mult)
                                sv.inc(v.engine_nop())
                            V(p6_v2)

                            def p6_y(sy, m=m):
                                sv.wait(sy)
                                o3 = z_fm.ap().rearrange("p (q c) -> p q c",
                                                         c=128)[
                                    :, KC * m:KC * (m + 1), :]
                                dh_.dinc(sy.dma_start(
                                    out=o3, in_=attn_o[:, m * D:(m + 1) * D],
                                    transpose=True))
                            Y(p6_y)

                        def t_wt2(t):
                            dh_.wait(t)
                        T(t_wt2)
                        wbufC = _ps11.enter_context(
                            nc.sbuf_tensor("wbufC", [128, 2 * KC * 512], BF16))
                        s0t = _ps11.enter_context(
                            nc.sbuf_tensor("s0t", [128, 512], BF16))

                        def z_lhs(m, k):
                            return z_fm[:, (m * KC + k) * 128:
                                        (m * KC + k) * 128 + 128]

                        def evac_s0(m, n, slot):
                            def ev(v, m=m, n=n, slot=slot):
                                st.wait(v)
                                dg.wait(v)
                                sv.inc(v.tensor_copy(s0t[:, :],
                                                     slot.ap[:, 0:512]))
                            V(ev)
                            slot.consumed(sv)

                            def stg(g, m=m, n=n):
                                sv.wait(g)
                                dg.dinc(g.dma_start(
                                    out=s0d[128 * m:128 * (m + 1),
                                            512 * n:512 * (n + 1)],
                                    in_=s0t[:, :]))
                            G(stg)
                        emit_proj(tiled_src(wout, KC), 4, z_lhs, evac_s0,
                                  wbufC)
                    _ps11.close()
                _ps8.close()
            _ps4.close()

            aost.close()
            fence()
            _ps12 = ExitStack()
            wbufD = _ps12.enter_context(nc.sbuf_tensor("wbufD", [128, 2 * KC * 512], BF16))
            wbufE = _ps12.enter_context(nc.sbuf_tensor("wbufE", [128, 2 * KC * 512], BF16))
            mid_fm = _ps12.enter_context(nc.sbuf_tensor("mid_fm", [128, MT * F], BF16))
            midc = _ps12.enter_context(nc.sbuf_tensor("midc", [128, 2 * 512], BF16))
            silg = _ps12.enter_context(nc.sbuf_tensor("silg", [128, 512], BF16))
            rtile = _ps12.enter_context(nc.sbuf_tensor("rtile", [128, E], F32))
            raux = _ps12.enter_context(nc.sbuf_tensor("raux", [128, E], F32))
            raux2 = _ps12.enter_context(nc.sbuf_tensor("raux2", [128, E], F32))
            zaux = _ps12.enter_context(nc.sbuf_tensor("zaux", [128, 2 * E], F32))
            t1g = _ps12.enter_context(nc.sbuf_tensor("t1g", [128, MT * 128], BF16))
            t1u = _ps12.enter_context(nc.sbuf_tensor("t1u", [128, MT * 128], BF16))
            t1gf = _ps12.enter_context(nc.sbuf_tensor("t1gf", [128, MT * 128], BF16))
            t1uf = _ps12.enter_context(nc.sbuf_tensor("t1uf", [128, MT * 128], BF16))
            s0tt = _ps12.enter_context(nc.sbuf_tensor("s0tt", [128, D], BF16))
            x0tt = _ps12.enter_context(nc.sbuf_tensor("x0tt", [128, D], F32))
            dstg = _ps12.enter_context(nc.sbuf_tensor("dstg", [128, 2 * 512], F32))
            accb = _ps12.enter_context(nc.sbuf_tensor("accb", [128, D], F32))
            fftmp = _ps12.enter_context(nc.sbuf_tensor("fftmp", [128, D], F32))
            wr_s = _ps12.enter_context(nc.sbuf_tensor("wr_s", [128, KC * E], BF16))
            lag = _ps12.enter_context(nc.sbuf_tensor("lag", [128, KC * 64], BF16))
            lau = _ps12.enter_context(nc.sbuf_tensor("lau", [128, KC * 64], BF16))
            lad = _ps12.enter_context(nc.sbuf_tensor("lad", [128, KC * 64], BF16))
            lbg = _ps12.enter_context(nc.sbuf_tensor("lbg", [128, F], BF16))
            lbu = _ps12.enter_context(nc.sbuf_tensor("lbu", [128, F], BF16))
            lbd = _ps12.enter_context(nc.sbuf_tensor("lbd", [128, D], BF16))
            if True:
                def p7_ld(g):
                    sg.wait(g)
                    dg.dinc(g.dma_start(out=wr_s[:, :], in_=wr[:, :]))
                G(p7_ld)

                for m in range(MT):
                    slot = pbanks[m % 2]

                    def r_mm(t, m=m, slot=slot):
                        slot.guard(t)
                        if m == 0:
                            dg.wait(t)
                        wv = wr_s.ap().rearrange("p (k n) -> p k n", k=KC)
                        for k in range(KC):
                            t.matmul(slot.ap[:, 0:E], xn0_lhs(m, k), wv[:, k],
                                     start=(k == 0), stop=(k == KC - 1))
                        st.inc(t.last)
                    T(r_mm)

                    def r_s(s, m=m, slot=slot):
                        st.wait(s)
                        ss.inc(s.activation(rtile[:, :], slot.ap[:, 0:E],
                                            AF.Sigmoid, scale=invT[:, 0:1]))
                    S(r_s)
                    slot.consumed(ss)

                    def r_s2(s, m=m):
                        ss.inc(s.activation(raux[:, :], rtile[:, :], AF.Abs,
                                            bias=cgc[:, 0:1]))
                    S(r_s2)

                    def r_v1(v, m=m):
                        ss.wait(v)
                        z1 = zaux[:, 0:E]
                        z2 = zaux[:, E:2 * E]
                        v.tensor_scalar(z1, rtile[:, :], GL_, None, ALU.is_ge)
                        v.tensor_scalar(z2, rtile[:, :], GU_, None, ALU.is_le)
                        sv.inc(v.tensor_mul(z1, z1, z2))
                    V(r_v1)

                    def r_s3(s, m=m):
                        sv.wait(s)
                        s.activation(raux2[:, :], raux[:, :], AF.Exp,
                                     scale=-10.0)
                        ss.inc(s.activation(raux[:, :], raux[:, :], AF.Exp,
                                            scale=-1.0 / 0.3))
                    S(r_s3)

                    def r_v2(v, m=m):
                        ss.wait(v)
                        wsl = wcols[:, m * E:(m + 1) * E]
                        z1 = zaux[:, 0:E]
                        z2 = zaux[:, E:2 * E]
                        v.tensor_mul(wsl, raux2[:, :], z1)
                        v.tensor_reduce(colA[:, :], wsl, AX.X, ALU.add)
                        v.tensor_scalar(colB[:, :], colA[:, :], 1e-8, None,
                                        ALU.is_lt)
                        v.tensor_reduce(colC[:, :], raux[:, :], AX.X, ALU.max)
                        v.tensor_scalar(z2, raux[:, :], colC[:, 0:1], None,
                                        ALU.is_ge)
                        v.scalar_tensor_tensor(raux2[:, :], z2, -2.0,
                                               raux[:, :], ALU.mult, ALU.add)
                        v.tensor_reduce(colC[:, :], raux2[:, :], AX.X, ALU.max)
                        v.tensor_scalar(raux2[:, :], raux2[:, :], colC[:, 0:1],
                                        None, ALU.is_ge)
                        v.tensor_add(z2, z2, raux2[:, :])
                        v.tensor_mul(raux[:, :], raux[:, :], z2)
                        v.tensor_reduce(colC[:, :], raux[:, :], AX.X, ALU.add)
                        v.tensor_scalar_max(colC[:, :], colC[:, :], 1e-8)
                        v.reciprocal(colD[:, :], colC[:, :])
                        v.drain()
                        v.tensor_scalar(raux[:, :], raux[:, :], colD[:, 0:1],
                                        None, ALU.mult)
                        v.tensor_scalar(colC[:, :], colB[:, :], -1.0, 1.0,
                                        ALU.mult, ALU.add)
                        v.tensor_scalar(wsl, wsl, colC[:, 0:1], None, ALU.mult)
                        v.scalar_tensor_tensor(wsl, raux[:, :], colB[:, 0:1],
                                               wsl, ALU.mult, ALU.add)
                        v.tensor_reduce(colC[:, :], wsl, AX.X, ALU.add)
                        v.tensor_scalar_max(colC[:, :], colC[:, :], 1e-8)
                        v.reciprocal(colD[:, :], colC[:, :])
                        v.drain()
                        v.tensor_scalar(wsl, wsl, colD[:, 0:1], None, ALU.mult)
                        sv.inc(v.engine_nop())
                    V(r_v2)

                def mid_lhs(m, k):
                    return mid_fm[:, (m * KC + k) * 128:(m * KC + k) * 128 + 128]

                pl = pbanks[6]
                for e in range(E):
                    lora = e < 2
                    if lora:
                        def l_ld(g, e=e):
                            st.wait(g)
                            for dst, srcv in ((lag, ga), (lau, ua), (lad, da)):
                                dg.dinc(g.dma_start(out=dst[:, :],
                                                    in_=srcv.ap()[e]))
                            for dst, srcv in ((lbg, gbp), (lbu, ubp),
                                              (lbd, dbp)):
                                dg.dinc(g.dma_start(out=dst[:, :],
                                                    in_=srcv.ap()[e]))
                        G(l_ld)

                        def l_v0(v, e=e):
                            dh_.wait(v)
                            v.memset(t1g[:, :], 0.0)
                            v.memset(t1u[:, :], 0.0)
                            sv.inc(v.engine_nop())
                        V(l_v0)
                        for m in range(MT):
                            for asrc, tdst in ((lag, t1g), (lau, t1u)):
                                def l_mm(t, m=m, e=e, asrc=asrc):
                                    if m == 0:
                                        dg.wait(t)
                                    pl.guard(t)
                                    src = asrc.ap().rearrange(
                                        "p (k n) -> p k n", k=KC)
                                    for k in range(KC):
                                        t.matmul(pl.ap[:, 0:64],
                                                 xn0_lhs(m, k), src[:, k],
                                                 start=(k == 0),
                                                 stop=(k == KC - 1))
                                    st.inc(t.last)
                                T(l_mm)

                                def l_ev(v, m=m, tdst=tdst):
                                    st.wait(v)
                                    sv.inc(v.tensor_copy(
                                        tdst[:, m * 128:m * 128 + 64],
                                        pl.ap[:, 0:64]))
                                V(l_ev)
                                pl.consumed(sv)

                        def l_tr(sy, e=e):
                            sv.wait(sy)
                            for tsrc, tdst in ((t1g, t1gf), (t1u, t1uf)):
                                o3 = tdst.ap().rearrange("p (q c) -> p q c",
                                                         c=128)
                                dh_.dinc(sy.dma_start(out=o3, in_=tsrc[:, :],
                                                      transpose=True))
                        Y(l_tr)

                        def t_wt3(t, e=e):
                            dh_.wait(t)
                        T(t_wt3)

                    for n in range(4):
                        wb = n % 2

                        def wload_gu(g, e=e, n=n, wb=wb):
                            if n >= 2:
                                st.wait(g)
                            for wsrc, wbuf in ((wg, wbufD), (wu, wbufE)):
                                dst = wbuf.ap().rearrange("p (w q) -> p w q",
                                                          w=2)[:, wb]
                                src = wsrc.ap()[e].rearrange(
                                    "p (n r) -> p n r", r=KC * 512)[:, n]
                                dg.dinc(g.dma_start(out=dst, in_=src))
                        G(wload_gu)

                        for m in range(MT):
                            sg_ = pbanks[m % 2]
                            su_ = pbanks[2 + m % 2]

                            def gu_mm(t, e=e, n=n, m=m, wb=wb, sg_=sg_,
                                      su_=su_, lora=lora):
                                if m == 0:
                                    dg.wait(t)
                                for slot, wbuf, lf, bsrc in (
                                        (sg_, wbufD, t1gf, lbg),
                                        (su_, wbufE, t1uf, lbu)):
                                    slot.guard(t)
                                    wv = wbuf.ap().rearrange(
                                        "p (w q) -> p w q", w=2)[:, wb]
                                    wv = wv.rearrange("p (k n) -> p k n", k=KC)
                                    for k in range(KC):
                                        t.matmul(slot.ap[:, 0:512],
                                                 xn0_lhs(m, k), wv[:, k],
                                                 start=(k == 0),
                                                 stop=(k == KC - 1 and
                                                       not lora))
                                    if lora:
                                        t.matmul(
                                            slot.ap[:, 0:512],
                                            lf[:, m * 128:m * 128 + 128],
                                            bsrc[:, 512 * n:512 * (n + 1)],
                                            start=False, stop=True)
                                st.inc(t.last)
                            T(gu_mm)

                            mb = m % 2

                            def gu_ev_s(s, e=e, n=n, m=m, sg_=sg_):
                                st.wait(s)
                                ss.inc(s.activation(silg[:, :],
                                                    sg_.ap[:, 0:512], AF.Silu))
                            S(gu_ev_s)
                            sg_.consumed(ss)

                            def gu_ev_v(v, e=e, n=n, m=m, mb=mb, su_=su_):
                                ss.wait(v)
                                if not (e == 0 and n == 0 and m < 2):
                                    dh_.wait(v, dh_.n)
                                sv.inc(v.tensor_mul(midc[:, mb * 512:mb * 512 + 512], silg[:, :],
                                                    su_.ap[:, 0:512]))
                            V(gu_ev_v)
                            su_.consumed(sv)

                            def gu_tr(sy, e=e, n=n, m=m, mb=mb):
                                sv.wait(sy)
                                o3 = mid_fm.ap().rearrange(
                                    "p (q c) -> p q c", c=128)[
                                    :, m * KC + 4 * n:m * KC + 4 * n + 4, :]
                                dh_.dinc(sy.dma_start(out=o3,
                                                      in_=midc[:, mb * 512:mb * 512 + 512],
                                                      transpose=True))
                            Y(gu_tr)

                    if lora:
                        def t2_v0(v, e=e):
                            dh_.wait(v)
                            v.memset(t1g[:, :], 0.0)
                            sv.inc(v.engine_nop())
                        V(t2_v0)
                        for m in range(MT):
                            def t2_mm(t, m=m, e=e):
                                if m == 0:
                                    dh_.wait(t)
                                pl.guard(t)
                                src = lad.ap().rearrange(
                                    "p (k n) -> p k n", k=KC)
                                for k in range(KC):
                                    t.matmul(pl.ap[:, 0:64], mid_lhs(m, k),
                                             src[:, k], start=(k == 0),
                                             stop=(k == KC - 1))
                                st.inc(t.last)
                            T(t2_mm)

                            def t2_ev(v, m=m):
                                st.wait(v)
                                sv.inc(v.tensor_copy(
                                    t1g[:, m * 128:m * 128 + 64],
                                    pl.ap[:, 0:64]))
                            V(t2_ev)
                            pl.consumed(sv)

                        def t2_tr(sy, e=e):
                            sv.wait(sy)
                            o3 = t1gf.ap().rearrange("p (q c) -> p q c", c=128)
                            dh_.dinc(sy.dma_start(out=o3, in_=t1g[:, :],
                                                  transpose=True))
                        Y(t2_tr)

                    for n in range(4):
                        wb = n % 2

                        def wload_d(g, e=e, n=n, wb=wb):
                            if n >= 2:
                                st.wait(g)
                            dst = wbufD.ap().rearrange("p (w q) -> p w q",
                                                       w=2)[:, wb]
                            src = wd.ap()[e].rearrange(
                                "p (n r) -> p n r", r=KC * 512)[:, n]
                            dg.dinc(g.dma_start(out=dst, in_=src))
                        G(wload_d)

                        for m in range(MT):
                            sd_ = pbanks[4 + m % 2]

                            def d_mm(t, e=e, n=n, m=m, wb=wb, sd_=sd_,
                                     lora=lora):
                                if m == 0:
                                    dg.wait(t)
                                    if n == 0:
                                        dh_.wait(t)
                                sd_.guard(t)
                                wv = wbufD.ap().rearrange("p (w q) -> p w q",
                                                          w=2)[:, wb]
                                wv = wv.rearrange("p (k n) -> p k n", k=KC)
                                for k in range(KC):
                                    t.matmul(sd_.ap[:, 0:512], mid_lhs(m, k),
                                             wv[:, k], start=(k == 0),
                                             stop=(k == KC - 1 and not lora))
                                if lora:
                                    t.matmul(sd_.ap[:, 0:512],
                                             t1gf[:, m * 128:m * 128 + 128],
                                             lbd[:, 512 * n:512 * (n + 1)],
                                             start=False, stop=True)
                                st.inc(t.last)
                            T(d_mm)

                            dsh = (n * MT + m) % 2

                            def d_ev(v, e=e, n=n, m=m, sd_=sd_, dsh=dsh):
                                st.wait(v)
                                dg.wait(v)
                                sv.inc(v.tensor_scalar(
                                    dstg[:, dsh * 512:dsh * 512 + 512],
                                    sd_.ap[:, 0:512],
                                    wcols[:, m * E + e:m * E + e + 1], None,
                                    ALU.mult))
                            V(d_ev)
                            sd_.consumed(sv)

                            def d_evg(g, e=e, n=n, m=m, dsh=dsh):
                                sv.wait(g)
                                dg.dinc(g.dma_start(
                                    out=accd[128 * m:128 * (m + 1),
                                             512 * n:512 * (n + 1)],
                                    in_=dstg[:, dsh * 512:dsh * 512 + 512],
                                    accum_op=(ALU.bypass if e == 0
                                              else ALU.add)))
                            G(d_evg)

                for m in range(MT):
                    def c_g(g, m=m):
                        sv.wait(g)
                        dg.dinc(g.dma_start(out=s0tt[:, :],
                                            in_=s0d[128 * m:128 * (m + 1), :]))
                        dg.dinc(g.dma_start(out=x0tt[:, :],
                                            in_=x0[128 * m:128 * (m + 1), :]))
                        dg.dinc(g.dma_start(out=accb[:, :],
                                            in_=accd[128 * m:128 * (m + 1), :]))
                    G(c_g)

                    def c_s(s, m=m):
                        dg.wait(s)
                        ss.inc(s.activation(accb[:, :], accb[:, :], AF.Silu))
                    S(c_s)

                    def c_v(v, m=m):
                        dg.wait(v)
                        ss.wait(v)
                        v.tensor_mul(fftmp[:, :], s0tt[:, :], accb[:, :])
                        sv.inc(v.tensor_add(accb[:, :], fftmp[:, :],
                                            x0tt[:, :]))
                    V(c_v)

                    def c_g2(g, m=m):
                        sv.wait(g)
                        dg.dinc(g.dma_start(out=x1d[128 * m:128 * (m + 1), :],
                                            in_=accb[:, :]))
                    G(c_g2)
            _ps12.close()
            tap_out("x1", src_dram=x1d.ap())
        _ps2.close()

        xfst = ExitStack()
        xn1_fm = xfst.enter_context(nc.sbuf_tensor("xn1_fm", [128, MT * D],
                                                   BF16))

        def xn1_lhs(m, k):
            return xn1_fm[:, (m * KC + k) * 128:(m * KC + k) * 128 + 128]

        _ps13 = ExitStack()
        xnt1 = _ps13.enter_context(nc.sbuf_tensor("xnt1", [128, D], BF16))
        xt8 = _ps13.enter_context(nc.sbuf_tensor("xt8", [128, D], F32))
        if True:
            for m in range(MT):
                def p8_g(g, m=m):
                    sv.wait(g)
                    dg.dinc(g.dma_start(out=xt8[:, :],
                                        in_=x1d[128 * m:128 * (m + 1), :]))
                G(p8_g)

                def p8_s(s, m=m):
                    sv.wait(s)
                    dg.wait(s)
                    if m > 0:
                        dh_.wait(s)
                    ss.inc(s.activation(sqs[:, :], xt8[:, :],
                                        AF.Square, accum_out=colB[:, 0:1]))
                S(p8_s)

                def p8_v(v, m=m):
                    ss.wait(v)
                    v.tensor_reduce(colA[:, :], xt8[:, :], AX.X, ALU.add)
                    v.tensor_scalar(mu1[:, m:m + 1], colA[:, :], 1.0 / D,
                                    None, ALU.mult)
                    v.tensor_mul(colC[:, :], mu1[:, m:m + 1], mu1[:, m:m + 1])
                    v.tensor_scalar(colD[:, :], colB[:, :], 1.0 / D, None,
                                    ALU.mult)
                    v.tensor_sub(colC[:, :], colD[:, :], colC[:, :])
                    sv.inc(v.tensor_scalar(colC[:, :], colC[:, :], EPS_LN,
                                           None, ALU.add))
                V(p8_v)

                def p8_s2(s, m=m):
                    sv.wait(s)
                    ss.inc(s.activation(colC[:, :], colC[:, :], AF.Sqrt))
                S(p8_s2)

                def p8_v2(v, m=m):
                    ss.wait(v)
                    v.reciprocal(rstd1[:, m:m + 1], colC[:, :])
                    v.drain()
                    v.scalar_tensor_tensor(
                        sqs[:, :], xt8[:, :], mu1[:, m:m + 1],
                        rstd1[:, m:m + 1].broadcast_to((128, D)),
                        ALU.subtract, ALU.mult)
                    v.tensor_reduce(colA[:, :], sqs[:, :], AX.X, ALU.max,
                                    apply_absolute_value=True)
                    v.tensor_scalar_max(gx1[:, m:m + 1], colA[:, :], 1e-5)
                    v.reciprocal(colB[:, :], gx1[:, m:m + 1])
                    v.drain()
                    v.tensor_scalar(sx1[:, m:m + 1], colB[:, :], 127.0, None,
                                    ALU.mult)
                    sv.inc(v.tensor_copy(xnt1[:, :], sqs[:, :]))
                V(p8_v2)

                def p8_y(sy, m=m):
                    sv.wait(sy)
                    o3 = xn1_fm.ap().rearrange("p (q c) -> p q c", c=128)[
                        :, KC * m:KC * (m + 1), :]
                    dh_.dinc(sy.dma_start(out=o3, in_=xnt1[:, :],
                                          transpose=True))
                Y(p8_y)

        _ps13.close()
        def p8_rel(t):
            dh_.wait(t)
        T(p8_rel)

        _ps14 = ExitStack()
        gws = _ps14.enter_context(nc.sbuf_tensor("gws", [128, 4], F32))
        if True:
            def gf_g(g):
                cc.wait(g, 1)
                dg.dinc(g.dma_start(out=gws[:, :], in_=cc_g_out[:, :]))
            G(gf_g)

            def gf_t(t):
                dg.wait(t)
                pbanks[7].guard(t)
                t.matmul(pbanks[7].ap[:, 0:4], onesf[:, :], gws[:, :],
                         start=True, stop=True)
                st.inc(t.last)
            T(gf_t)

            def gf_v(v):
                st.wait(v)
                for j, nm in enumerate(("g", "v", "o")):
                    v.tensor_scalar(gamv[nm][:, :], pbanks[7].ap[:, j:j + 1],
                                    1.0 / (D * HID), None, ALU.mult)
                    v.tensor_scalar_max(gamv[nm][:, :], gamv[nm][:, :], 1e-5)
                    v.tensor_scalar(thr[nm][:, :], gamv[nm][:, :], 0.5, None,
                                    ALU.mult)
                    v.tensor_scalar(nthr[nm][:, :], gamv[nm][:, :], -0.5, None,
                                    ALU.mult)
                sv.inc(v.engine_nop())
            V(gf_v)
            pbanks[7].consumed(sv)

        _ps14.close()
        def p8b_v(v):
            for m in range(MT):
                v.tensor_mul(colA[:, :], gx1[:, m:m + 1], gamv["g"][:, :])
                v.tensor_scalar(nsc1g[:, m:m + 1], colA[:, :], -1.0 / 127.0,
                                None, ALU.mult)
                v.tensor_mul(colA[:, :], gx1[:, m:m + 1], gamv["v"][:, :])
                v.tensor_scalar(nsc1v[:, m:m + 1], colA[:, :], -1.0 / 127.0,
                                None, ALU.mult)
            sv.inc(v.engine_nop())
        V(p8b_v)

        kvst = ExitStack()
        v1_tm = kvst.enter_context(nc.sbuf_tensor("v1_tm", [128, MT * D], BF16))
        q_fm1 = kvst.enter_context(nc.sbuf_tensor("q_fm1", [128, H * TOK], BF16))
        k_fm1 = kvst.enter_context(nc.sbuf_tensor("k_fm1", [128, H * TOK], BF16))
        nwst = ExitStack()
        qnw_s = nwst.enter_context(nc.sbuf_tensor("qnw_s", [128, D], BF16))
        knw_s = nwst.enter_context(nc.sbuf_tensor("knw_s", [128, D], BF16))

        def p9_ld(g):
            sv.wait(g)
            dg.dinc(g.dma_start(out=qnw_s[:, :], in_=qnw_r[:, :]))
            dg.dinc(g.dma_start(out=knw_s[:, :], in_=knw_r[:, :]))
        G(p9_ld)

        _ps15 = ExitStack()
        angb = _ps15.enter_context(nc.sbuf_tensor("angb", [128, 64], F32))
        rr = _ps15.enter_context(nc.sbuf_tensor("rr", [128, 64], F32))
        ki = _ps15.enter_context(nc.sbuf_tensor("ki", [128, 64], I32))
        kif = _ps15.enter_context(nc.sbuf_tensor("kif", [128, 64], F32))
        rra = _ps15.enter_context(nc.sbuf_tensor("rra", [128, 64], F32))
        if True:
            for m in range(MT):
                def rp_v(v, m=m):
                    if m > 0:
                        ss.wait(v)
                    v.tensor_scalar(colA[:, :], io_pf[:, :], float(128 * m),
                                    None, ALU.add)
                    v.tensor_add(colA[:, :], colA[:, :], posb_s[:, :])
                    v.drain()
                    v.tensor_scalar(angb[:, :], invfh_s[:, :], colA[:, 0:1],
                                    None, ALU.mult)
                    v.scalar_tensor_tensor(angb[:, :], invfl_s[:, :],
                                           colA[:, 0:1], angb[:, :], ALU.mult,
                                           ALU.add)
                    v.tensor_scalar(ki[:, :], angb[:, :], INV2PI, None,
                                    ALU.mult)
                    v.tensor_copy(kif[:, :], ki[:, :])
                    v.scalar_tensor_tensor(rr[:, :], kif[:, :], -T2PI_HI,
                                           angb[:, :], ALU.mult, ALU.add)
                    v.scalar_tensor_tensor(rr[:, :], kif[:, :], -T2PI_LO,
                                           rr[:, :], ALU.mult, ALU.add)
                    sv.inc(v.engine_nop())
                V(rp_v)

                def rp_s(s, m=m):
                    sv.wait(s)
                    s.activation(nsin8[:, m * 64:(m + 1) * 64],
                                 rr[:, :], AF.Sin, bias=0.0, scale=-1.0)
                    s.activation(rra[:, :], rr[:, :], AF.Abs)
                    ss.inc(s.activation(ncos8[:, m * 64:(m + 1) * 64],
                                        rra[:, :], AF.Sin, bias=cpi2[:, 0:1]))
                S(rp_s)

        _ps15.close()
        tap_out("ncos", src_sb=ncos8[:, :])
        tap_out("nsin", src_sb=nsin8[:, :])
        def rope_tile(v, tm, m, h, nw_s, rsc):
            x1a = tm[:, m * D + h * 128:m * D + h * 128 + 64]
            x2a = tm[:, m * D + h * 128 + 64:m * D + h * 128 + 128]
            ncm = ncos8[:, m * 64:(m + 1) * 64]
            nsm = nsin8[:, m * 64:(m + 1) * 64]
            v.scalar_tensor_tensor(x1a, x1a, rsc,
                                   nw_s[:, h * 128:h * 128 + 64], ALU.mult,
                                   ALU.mult)
            v.scalar_tensor_tensor(x2a, x2a, rsc,
                                   nw_s[:, h * 128 + 64:h * 128 + 128],
                                   ALU.mult, ALU.mult)
            v.tensor_scalar(x1a, x1a, -1.0, None, ALU.mult)
            v.tensor_scalar(x2a, x2a, -1.0, None, ALU.mult)
            t1 = sqs[:, 0:64]
            t2 = sqs[:, 64:128]
            t3 = sqs[:, 128:192]
            t4 = sqs[:, 192:256]
            v.tensor_mul(t1, x1a, ncm)
            v.tensor_mul(t2, x2a, nsm)
            v.tensor_mul(t3, x2a, ncm)
            v.tensor_mul(t4, x1a, nsm)
            v.tensor_sub(x1a, t1, t2)
            v.tensor_add(x2a, t3, t4)

        for which, (wsrc, nw_s) in enumerate([(wq1, qnw_s), (wk1, knw_s)]):
            _ps16 = ExitStack()
            qk1_tm = _ps16.enter_context(nc.sbuf_tensor(
                "qk1_tm%d" % which, [128, MT * D], BF16))
            wbufF = _ps16.enter_context(nc.sbuf_tensor(
                "wbufF%d" % which, [128, KC * 512], BF16))
            if True:
                def evac_cp(m, n, slot):
                    def ev(v, m=m, n=n, slot=slot):
                        st.wait(v)
                        sv.inc(v.tensor_copy(
                            qk1_tm[:, m * D + n * 512:m * D + n * 512 + 512],
                            slot.ap[:, 0:512]))
                    V(ev)
                    slot.consumed(sv)

                def t_wt4(t):
                    dh_.wait(t)
                T(t_wt4)
                emit_proj(tiled_src(wsrc, KC), 4, xn1_lhs, evac_cp, wbufF,
                          dbl=False)

                for m in range(MT):
                    for h in range(H):
                        def nr_s(s, m=m, h=h, qk1_tm=qk1_tm):
                            if m == 0 and h == 0:
                                sv.wait(s)
                            ss.inc(s.activation(
                                sqs[:, 256:384],
                                qk1_tm[:, m * D + h * 128:m * D + h * 128 + 128],
                                AF.Square, accum_out=colB[:, 0:1]))
                        S(nr_s)

                        def nr_v(v, m=m, h=h):
                            ss.wait(v)
                            sv.inc(v.tensor_scalar(colC[:, :], colB[:, :],
                                                   1.0 / DH, EPS_RMS, ALU.mult,
                                                   ALU.add))
                        V(nr_v)

                        def nr_s2(s, m=m, h=h):
                            sv.wait(s)
                            ss.inc(s.activation(colC[:, :], colC[:, :],
                                                AF.Sqrt))
                        S(nr_s2)

                        def nr_v2(v, m=m, h=h, qk1_tm=qk1_tm, nw_s=nw_s):
                            ss.wait(v)
                            if m == 0 and h == 0:
                                dg.wait(v)
                            v.reciprocal(colD[:, :], colC[:, :])
                            v.drain()
                            rope_tile(v, qk1_tm, m, h, nw_s, colD[:, 0:1])
                            sv.inc(v.engine_nop())
                        V(nr_v2)

                if which == 1:
                    def kc_g(g, qk1_tm=qk1_tm):
                        sv.wait(g)
                        o3 = cc_kv_in.ap()[0:TOK, :].rearrange(
                            "(t p) c -> p t c", p=128)
                        i3 = qk1_tm.ap().rearrange("p (t c) -> p t c", t=MT)
                        dg.dinc(g.dma_start(out=o3, in_=i3))
                    G(kc_g)

                dstf = q_fm1 if which == 0 else k_fm1

                def qk_tr(sy, qk1_tm=qk1_tm, dstf=dstf, which=which):
                    sv.wait(sy)
                    if which == 1:
                        dg.wait(sy)
                    for t_ in range(MT):
                        o3 = dstf.ap().rearrange("p (h q) -> p h q", h=H)[
                            :, :, 128 * t_:128 * t_ + 128]
                        dh_.dinc(sy.dma_start(
                            out=o3, in_=qk1_tm[:, t_ * D:(t_ + 1) * D],
                            transpose=True))
                Y(qk_tr)

                def t_wt5(t):
                    dh_.wait(t)
                T(t_wt5)

            _ps16.close()
        nwst.close()
        fence()
        _ps17 = ExitStack()
        wbufG = _ps17.enter_context(nc.sbuf_tensor("wbufG", [128, 2 * KC * 512], BF16))
        if True:
            def evac_v1(m, n, slot):
                def ev(v, m=m, n=n, slot=slot):
                    st.wait(v)
                    sv.inc(v.tensor_copy(
                        v1_tm[:, m * D + n * 512:m * D + n * 512 + 512],
                        slot.ap[:, 0:512]))
                V(ev)
                slot.consumed(sv)
            emit_proj(tiled_src(wv1, KC), 4, xn1_lhs, evac_v1, wbufG)

        _ps17.close()
        def vc_g(g):
            sv.wait(g)
            o3 = cc_kv_in.ap()[TOK:2 * TOK, :].rearrange("(t p) c -> p t c",
                                                         p=128)
            i3 = v1_tm.ap().rearrange("p (t c) -> p t c", t=MT)
            dg.dinc(g.dma_start(out=o3, in_=i3))
        G(vc_g)

        def kv_ag(g):
            g.wait_ge(dg.sem, dg.n)
            cc.inc(g.collective_compute(
                "AllGather", ALU.bypass,
                replica_groups=[[0, 1, 2, 3], [4, 5, 6, 7]],
                ins=[cc_kv_in.ap().opt()], outs=[cc_kv_out.ap().opt()]))
        G(kv_ag)

        if _DEV and "q1f" in taps:
            def q1tap(g):
                sv.wait(g); dh_.wait(g); cc.wait(g)
                dg.dinc(g.dma_start(out=taps["q1f"].ap(), in_=q_fm1[:, :]))
                dg.dinc(g.dma_start(out=taps["k1f"].ap(), in_=k_fm1[:, :]))
                dg.dinc(g.dma_start(out=taps["v1t"].ap(), in_=v1_tm[:, :]))
            G(q1tap)

        a1st = ExitStack()
        attn1 = a1st.enter_context(nc.sbuf_tensor("attn1", [128, MT * D],
                                                  BF16))

        _ps18 = ExitStack()
        mask8 = _ps18.enter_context(nc.sbuf_tensor("mask8", [128, MT * 1152], BF16))
        khalo = _ps18.enter_context(nc.sbuf_tensor("khalo", [128, TOK], BF16))
        vhalo = _ps18.enter_context(nc.sbuf_tensor("vhalo", [128, MT * 128], BF16))
        pbf = _ps18.enter_context(nc.sbuf_tensor("pbf", [128, 1152], BF16))
        pt = _ps18.enter_context(nc.sbuf_tensor("pt", [128, 9 * 128], BF16))
        if True:
            def mk_g(g):
                sv.wait(g)
                dg.dinc(g.dma_start(out=mask8[:, :], in_=mask8_r[:, :]))
            G(mk_g)

            ps_sc = [pbanks[0], pbanks[1], pbanks[2]]
            ps_pv = pbanks[3]

            creg = [None]
            for h in range(H):
                def halo_y(sy, h=h):
                    if h == 0:
                        cc.wait(sy)
                    sv.wait(sy)
                    st.wait(sy)
                    if creg[0] is None:
                        creg[0] = sy.alloc_register("cid")
                        sy.reg_load(creg[0], chid_s[0:1, 0:1])
                    cid = creg[0]
                    if True:
                        for p_ in range(4):
                            base = max(p_ - 1, 0) * 2 * TOK
                            with sy.If_eq(cid, p_):
                                dh_.dinc(sy.dma_start(
                                    out=khalo[:, :],
                                    in_=cc_kv_out.ap()[base:base + TOK,
                                                       h * 128:(h + 1) * 128],
                                    transpose=True))
                                dh_.dinc(sy.dma_start(
                                    out=vhalo.ap().rearrange(
                                        "p (t c) -> p t c", t=MT),
                                    in_=cc_kv_out.ap()[
                                        base + TOK:base + 2 * TOK,
                                        h * 128:(h + 1) * 128].rearrange(
                                        "(t p) c -> p t c", p=128)))
                    dh_.n += 0
                Y(halo_y)
                dh_.n -= 3 * 32

                for t_ in range(MT):
                    hl = TOK - 128 * t_

                    def sc_mm(t, h=h, t_=t_, hl=hl):
                        if t_ == 0:
                            dh_.wait(t)
                        for ci, wlo in enumerate((0, 512, 1024)):
                            wid = 512 if ci < 2 else 128
                            ps_sc[ci].guard(t)
                            segs = []
                            j0 = wlo
                            while j0 < wlo + wid:
                                if j0 < hl:
                                    seg = min(hl - j0, wlo + wid - j0)
                                    src = khalo[:, 128 * t_ + j0:
                                                128 * t_ + j0 + seg]
                                else:
                                    seg = wlo + wid - j0
                                    src = k_fm1[:, h * TOK + (j0 - hl):
                                                h * TOK + (j0 - hl) + seg]
                                segs.append((j0 - wlo, seg, src))
                                j0 += seg
                            for off, seg, src in segs:
                                t.matmul(ps_sc[ci].ap[:, off:off + seg],
                                         q_fm1[:, h * TOK + 128 * t_:
                                               h * TOK + 128 * t_ + 128],
                                         src, start=True, stop=True)
                        st.inc(t.last)
                    T(sc_mm)

                    def sm_v(v, h=h, t_=t_):
                        st.wait(v)
                        v.tensor_reduce(colA[:, :], ps_sc[0].ap[:, 0:512],
                                        AX.X, ALU.max, negate=True)
                        v.tensor_reduce(colB[:, :], ps_sc[1].ap[:, 0:512],
                                        AX.X, ALU.max, negate=True)
                        v.tensor_reduce(colC[:, :], ps_sc[2].ap[:, 0:128],
                                        AX.X, ALU.max, negate=True)
                        v.tensor_tensor(colA[:, :], colA[:, :], colB[:, :],
                                        ALU.min)
                        v.tensor_tensor(colA[:, :], colA[:, :], colC[:, :],
                                        ALU.min)
                        sv.inc(v.tensor_scalar(colD[:, :], colA[:, :], ISQD,
                                               None, ALU.mult))
                    V(sm_v)

                    def sm_s(s, h=h, t_=t_):
                        sv.wait(s)
                        dh_.wait(s, dh_.n)
                        s.activation(pbf[:, 0:512], ps_sc[0].ap[:, 0:512],
                                     AF.Exp, bias=colD[:, 0:1], scale=ISQD)
                        s.activation(pbf[:, 512:1024], ps_sc[1].ap[:, 0:512],
                                     AF.Exp, bias=colD[:, 0:1], scale=ISQD)
                        ss.inc(s.activation(pbf[:, 1024:1152],
                                            ps_sc[2].ap[:, 0:128], AF.Exp,
                                            bias=colD[:, 0:1], scale=ISQD))
                    S(sm_s)
                    for sl in ps_sc:
                        sl.consumed(ss)

                    def pm_v(v, h=h, t_=t_):
                        ss.wait(v)
                        if h == 0 and t_ == 0:
                            dg.wait(v)
                        v.tensor_mul(pbf[:, :], pbf[:, :],
                                     mask8[:, t_ * 1152:(t_ + 1) * 1152])
                        v.tensor_reduce(colA[:, :], pbf[:, :], AX.X, ALU.add)
                        v.reciprocal(colB[:, :], colA[:, :])
                        v.drain()
                        sv.inc(v.engine_nop())
                    V(pm_v)

                    def pt_y(sy, h=h, t_=t_):
                        sv.wait(sy)
                        o3 = pt.ap().rearrange("p (q c) -> p q c", c=128)
                        dh_.dinc(sy.dma_start(out=o3, in_=pbf[:, :],
                                              transpose=True))
                    Y(pt_y)

                    def pv_mm(t, h=h, t_=t_, hl=hl):
                        dh_.wait(t)
                        ps_pv.guard(t)
                        for kb in range(9):
                            j0 = 128 * kb
                            if j0 < hl:
                                vsl = vhalo[:, (t_ + kb) * 128:
                                            (t_ + kb) * 128 + 128]
                            else:
                                mo = kb - (MT - t_)
                                vsl = v1_tm[:, mo * D + h * 128:
                                            mo * D + h * 128 + 128]
                            t.matmul(ps_pv.ap[:, 0:128],
                                     pt[:, kb * 128:kb * 128 + 128], vsl,
                                     start=(kb == 0), stop=(kb == 8))
                        st.inc(t.last)
                    T(pv_mm)

                    def pv_ev(v, h=h, t_=t_):
                        st.wait(v)
                        sv.inc(v.tensor_scalar(
                            attn1[:, t_ * D + h * 128:t_ * D + h * 128 + 128],
                            ps_pv.ap[:, 0:128], colB[:, 0:1], None, ALU.mult))
                    V(pv_ev)
                    ps_pv.consumed(sv)

        _ps18.close()
        def a1_st(g):
            sv.wait(g)
            dg.dinc(g.dma_start(
                out=attn1d.ap().rearrange("(t p) c -> p t c", p=128),
                in_=attn1.ap().rearrange("p (t c) -> p t c", t=MT)))
        G(a1_st)

        def a1_rel(t):
            dg.wait(t)
        T(a1_rel)
        a1st.close()
        fence()

        _ps19 = ExitStack()
        a1_fm = _ps19.enter_context(nc.sbuf_tensor("a1_fm", [128, MT * D], BF16))
        wbufH = _ps19.enter_context(nc.sbuf_tensor("wbufH", [128, KC * 512], BF16))
        s1t = _ps19.enter_context(nc.sbuf_tensor("s1t", [128, 512], BF16))
        if True:
            def a1_tr(sy):
                dg.wait(sy)
                sg.wait(sy)
                for m in range(MT):
                    o3 = a1_fm.ap().rearrange("p (q c) -> p q c", c=128)[
                        :, KC * m:KC * (m + 1), :]
                    dh_.dinc(sy.dma_start(
                        out=o3, in_=attn1d.ap()[128 * m:128 * (m + 1), :],
                        transpose=True))
            Y(a1_tr)

            def t_wt6(t):
                dh_.wait(t)
            T(t_wt6)

            def a1_lhs(m, k):
                return a1_fm[:, (m * KC + k) * 128:(m * KC + k) * 128 + 128]

            def evac_s1(m, n, slot):
                def ev(v, m=m, n=n, slot=slot):
                    st.wait(v)
                    dg.wait(v)
                    sv.inc(v.tensor_copy(s1t[:, :], slot.ap[:, 0:512]))
                V(ev)
                slot.consumed(sv)

                def stg(g, m=m, n=n):
                    sv.wait(g)
                    dg.dinc(g.dma_start(
                        out=s1d[128 * m:128 * (m + 1), 512 * n:512 * (n + 1)],
                        in_=s1t[:, :]))
                G(stg)
            emit_proj(tiled_src(wo1, KC), 4, a1_lhs, evac_s1, wbufH,
                      dbl=False)

        _ps19.close()
        kvst.close()
        xfst.close()
        fence()
        _ps20 = ExitStack()
        xq_fm = _ps20.enter_context(nc.sbuf_tensor("xq_fm", [128, MT * D], BF16))
        xti = _ps20.enter_context(nc.sbuf_tensor("xti", [128, D], I32))
        xtb = _ps20.enter_context(nc.sbuf_tensor("xtb", [128, D], BF16))
        x1tt = _ps20.enter_context(nc.sbuf_tensor("x1tt", [128, D], F32))
        if True:
            for m in range(MT):
                def qa_g(g, m=m):
                    if m > 0:
                        sv.wait(g)
                    dg.dinc(g.dma_start(out=x1tt[:, :],
                                        in_=x1d[128 * m:128 * (m + 1), :]))
                G(qa_g)

                def qa_v(v, m=m):
                    dg.wait(v)
                    if m > 0:
                        dh_.wait(v)
                    v.scalar_tensor_tensor(
                        sqs[:, :], x1tt[:, :], mu1[:, m:m + 1],
                        rstd1[:, m:m + 1].broadcast_to((128, D)),
                        ALU.subtract, ALU.mult)
                    v.tensor_scalar(xti[:, :], sqs[:, :], sx1[:, m:m + 1],
                                    None, ALU.mult)
                    sv.inc(v.tensor_copy(xtb[:, :], xti[:, :]))
                V(qa_v)

                def qa_y(sy, m=m):
                    sv.wait(sy)
                    o3 = xq_fm.ap().rearrange("p (q c) -> p q c", c=128)[
                        :, KC * m:KC * (m + 1), :]
                    dh_.dinc(sy.dma_start(out=o3, in_=xtb[:, :],
                                          transpose=True))
                Y(qa_y)

            def xq_lhs(m, k):
                return xq_fm[:, (m * KC + k) * 128:(m * KC + k) * 128 + 128]

            _ps21 = ExitStack()
            wfq = _ps21.enter_context(nc.sbuf_tensor("wfq", [128, KC * 512], F32))
            wqn = _ps21.enter_context(nc.sbuf_tensor("wqn", [128, 2 * KC * 512], BF16))
            gfull = _ps21.enter_context(nc.sbuf_tensor("gfull", [128, MT * 512], F32))
            sgf = _ps21.enter_context(nc.sbuf_tensor("sgf", [128, MT * 512], BF16))
            bsl = _ps21.enter_context(nc.sbuf_tensor("bsl", [128, 512], F32))
            mct = _ps21.enter_context(nc.sbuf_tensor("mct", [128, 2 * 512], BF16))
            if True:
                def t_wt7(t):
                    dh_.wait(t)
                T(t_wt7)

                def p13_v0(v):
                    v.memset(gx2[:, :], 0.0)
                    sv.inc(v.engine_nop())
                V(p13_v0)

                for n in range(HID // 512):
                    for wi, (wsrc, nm, bias_r, nsc) in enumerate(
                            [(wgt1, "g", gb_r, nsc1g), (wvt1, "v", vb_r,
                                                        nsc1v)]):
                        wb = (2 * n + wi) % 2

                        def qw_g(g, n=n, wi=wi, wsrc=wsrc, bias_r=bias_r):
                            if n + wi > 0:
                                sv.wait(g)
                            src = wsrc.ap().rearrange(
                                "p (n r) -> p n r", r=KC * 512)[:, n]
                            dg.dinc(g.dma_start(out=wfq.ap(), in_=src))
                            dg.dinc(g.dma_start(
                                out=bsl[:, :],
                                in_=bias_r[:, 512 * n:512 * (n + 1)]))
                        G(qw_g)

                        def qw_v(v, n=n, wi=wi, nm=nm, wb=wb):
                            dg.wait(v)
                            if 2 * n + wi >= 2:
                                st.wait(v)
                            v.tensor_scalar(wqn[:, wb * KC * 512:(wb + 1) * KC * 512], wfq[:, :],
                                            thr[nm][:, 0:1], None, ALU.is_ge)
                            v.scalar_tensor_tensor(wqn[:, wb * KC * 512:(wb + 1) * KC * 512], wfq[:, :],
                                                   nthr[nm][:, 0:1],
                                                   wqn[:, wb * KC * 512:(wb + 1) * KC * 512], ALU.is_le,
                                                   ALU.subtract)
                            sv.inc(v.engine_nop())
                        V(qw_v)

                        for m in range(MT):
                            slot = pbanks[4 + m % 2] if wi else pbanks[m % 2]

                            def bq_mm(t, n=n, wi=wi, m=m, wb=wb, slot=slot):
                                if m == 0:
                                    sv.wait(t)
                                slot.guard(t)
                                wv = wqn[:, wb * KC * 512:(wb + 1) * KC * 512].rearrange(
                                    "p (k n) -> p k n", k=KC)
                                for k in range(KC):
                                    t.matmul(slot.ap[:, 0:512], xq_lhs(m, k),
                                             wv[:, k], start=(k == 0),
                                             stop=(k == KC - 1))
                                st.inc(t.last)
                            T(bq_mm)

                            if wi == 0:
                                def g_ev(v, n=n, m=m, slot=slot):
                                    st.wait(v)
                                    sv.inc(v.scalar_tensor_tensor(
                                        gfull[:, m * 512:(m + 1) * 512],
                                        slot.ap[:, 0:512],
                                        nsc1g[:, m:m + 1], bsl[:, :],
                                        ALU.mult, ALU.add))
                                V(g_ev)
                                slot.consumed(sv)

                                def g_sil(s, n=n, m=m):
                                    sv.wait(s)
                                    ss.inc(s.activation(
                                        sgf[:, m * 512:(m + 1) * 512],
                                        gfull[:, m * 512:(m + 1) * 512],
                                        AF.Silu))
                                S(g_sil)
                            else:
                                mb = m % 2

                                def v_ev(v, n=n, m=m, mb=mb, slot=slot):
                                    st.wait(v)
                                    ss.wait(v)
                                    if not (n == 0 and m < 2):
                                        dg.wait(v, dg.n)
                                    v.scalar_tensor_tensor(
                                        sqs[:, 0:512], slot.ap[:, 0:512],
                                        nsc1v[:, m:m + 1], bsl[:, :],
                                        ALU.mult, ALU.add)
                                    v.tensor_mul(mct[:, mb * 512:mb * 512 + 512],
                                                 sgf[:, m * 512:(m + 1) * 512],
                                                 sqs[:, 0:512])
                                    v.tensor_reduce(colA[:, :], mct[:, mb * 512:mb * 512 + 512],
                                                    AX.X, ALU.max,
                                                    apply_absolute_value=True)
                                    sv.inc(v.tensor_tensor(
                                        gx2[:, m:m + 1], gx2[:, m:m + 1],
                                        colA[:, :], ALU.max))
                                V(v_ev)
                                slot.consumed(sv)

                                def mc_g(g, n=n, m=m, mb=mb):
                                    sv.wait(g)
                                    dg.dinc(g.dma_start(
                                        out=midd[128 * m:128 * (m + 1),
                                                 512 * n:512 * (n + 1)],
                                        in_=mct[:, mb * 512:mb * 512 + 512]))
                                G(mc_g)

                def p13b_v(v):
                    sv.wait(v) if False else None
                    for m in range(MT):
                        v.tensor_scalar_max(gx2[:, m:m + 1], gx2[:, m:m + 1],
                                            1e-5)
                        v.reciprocal(colA[:, :], gx2[:, m:m + 1])
                        v.drain()
                        v.tensor_scalar(sx2[:, m:m + 1], colA[:, :], 127.0,
                                        None, ALU.mult)
                        v.tensor_mul(colA[:, :], gx2[:, m:m + 1],
                                     gamv["o"][:, :])
                        v.tensor_scalar(nsc3[:, m:m + 1], colA[:, :],
                                        -1.0 / 127.0, None, ALU.mult)
                    sv.inc(v.engine_nop())
                V(p13b_v)

            _ps21.close()
        _ps20.close()
        _ps22 = ExitStack()
        midq_fm = _ps22.enter_context(nc.sbuf_tensor("midq_fm", [128, 4 * HID], BF16))
        mbq = _ps22.enter_context(nc.sbuf_tensor("mbq", [128, HID], BF16))
        mbi = _ps22.enter_context(nc.sbuf_tensor("mbi", [128, D], I32))
        mbb = _ps22.enter_context(nc.sbuf_tensor("mbb", [128, HID], BF16))
        wfq2 = _ps22.enter_context(nc.sbuf_tensor("wfq2", [128, KC * 512], F32))
        wqn2 = _ps22.enter_context(nc.sbuf_tensor("wqn2", [128, 2 * KC * 512], BF16))
        obsl = _ps22.enter_context(nc.sbuf_tensor("obsl", [128, 512], F32))
        s1tt = _ps22.enter_context(nc.sbuf_tensor("s1tt", [128, 512], BF16))
        x1tt2 = _ps22.enter_context(nc.sbuf_tensor("x1tt2", [128, 512], F32))
        ot = _ps22.enter_context(nc.sbuf_tensor("ot", [128, 512], F32))
        tf2 = _ps22.enter_context(nc.sbuf_tensor("tf2", [128, 512], F32))
        if True:
            for half in range(2):
                for mi in range(4):
                    m = 4 * half + mi

                    def mq_g(g, m=m, mi=mi, half=half):
                        sv.wait(g)
                        if half > 0 and mi == 0:
                            st.wait(g)
                        dg.dinc(g.dma_start(
                            out=mbq[:, :],
                            in_=midd[128 * m:128 * (m + 1), :]))
                    G(mq_g)

                    def mq_v(v, m=m, mi=mi, half=half):
                        dg.wait(v)
                        if half + mi > 0:
                            dh_.wait(v)
                        for cq in range(HID // D):
                            v.tensor_scalar(mbi[:, :],
                                            mbq[:, cq * D:(cq + 1) * D],
                                            sx2[:, m:m + 1], None, ALU.mult)
                            v.tensor_copy(mbb[:, cq * D:(cq + 1) * D],
                                          mbi[:, :])
                        sv.inc(v.engine_nop())
                    V(mq_v)

                    def mq_y(sy, m=m, mi=mi):
                        sv.wait(sy)
                        o3 = midq_fm.ap().rearrange("p (q c) -> p q c",
                                                    c=128)[
                            :, 64 * mi:64 * (mi + 1), :]
                        dh_.dinc(sy.dma_start(out=o3, in_=mbb[:, :],
                                              transpose=True))
                    Y(mq_y)

                def mq_lhs(mi, kk):
                    return midq_fm[:, (mi * 64 + kk) * 128:
                                   (mi * 64 + kk) * 128 + 128]

                def t_wt8(t, half=half):
                    dh_.wait(t)
                T(t_wt8)

                for n in range(4):
                    for kg in range(4):
                        wb = kg % 2

                        def dw_g(g, n=n, kg=kg, wb=wb, half=half):
                            if half + n + kg > 0:
                                sv.wait(g)
                            src = wot1.ap().rearrange(
                                "p (c r) -> p c r", r=KC * 512)[:, n * 4 + kg]
                            dg.dinc(g.dma_start(out=wfq2.ap(), in_=src))
                        G(dw_g)

                        def dw_v(v, n=n, kg=kg, wb=wb, half=half):
                            dg.wait(v)
                            if half + n + kg >= 1:
                                st.wait(v)
                            v.tensor_scalar(wqn2[:, wb * KC * 512:(wb + 1) * KC * 512], wfq2[:, :],
                                            thr["o"][:, 0:1], None, ALU.is_ge)
                            v.scalar_tensor_tensor(wqn2[:, wb * KC * 512:(wb + 1) * KC * 512], wfq2[:, :],
                                                   nthr["o"][:, 0:1],
                                                   wqn2[:, wb * KC * 512:(wb + 1) * KC * 512], ALU.is_le,
                                                   ALU.subtract)
                            sv.inc(v.engine_nop())
                        V(dw_v)

                        for mi in range(4):
                            slot = pbanks[mi]

                            def dn_mm(t, n=n, kg=kg, mi=mi, wb=wb, slot=slot):
                                if mi == 0:
                                    sv.wait(t)
                                if kg == 0:
                                    slot.guard(t)
                                wv = wqn2[:, wb * KC * 512:(wb + 1) * KC * 512].rearrange(
                                    "p (k n) -> p k n", k=KC)
                                for k in range(KC):
                                    t.matmul(slot.ap[:, 0:512],
                                             mq_lhs(mi, kg * 16 + k),
                                             wv[:, k],
                                             start=(kg == 0 and k == 0),
                                             stop=(kg == 3 and k == KC - 1))
                                st.inc(t.last)
                            T(dn_mm)

                    for mi in range(4):
                        m = 4 * half + mi
                        slot = pbanks[mi]

                        def fin_g(g, n=n, m=m):
                            sv.wait(g)
                            dg.dinc(g.dma_start(
                                out=obsl[:, :],
                                in_=ob_r[:, 512 * n:512 * (n + 1)]))
                            dg.dinc(g.dma_start(
                                out=s1tt[:, :],
                                in_=s1d[128 * m:128 * (m + 1),
                                        512 * n:512 * (n + 1)]))
                            dg.dinc(g.dma_start(
                                out=x1tt2[:, :],
                                in_=x1d[128 * m:128 * (m + 1),
                                        512 * n:512 * (n + 1)]))
                        G(fin_g)

                        def fin_v(v, n=n, m=m, slot=slot):
                            st.wait(v)
                            dg.wait(v)
                            sv.inc(v.scalar_tensor_tensor(
                                tf2[:, :], slot.ap[:, 0:512],
                                nsc3[:, m:m + 1], obsl[:, :], ALU.mult,
                                ALU.add))
                        V(fin_v)
                        slot.consumed(sv)

                        def fin_s(s, n=n, m=m):
                            sv.wait(s)
                            ss.inc(s.activation(tf2[:, :], tf2[:, :],
                                                AF.Silu))
                        S(fin_s)

                        def fin_v2(v, n=n, m=m):
                            ss.wait(v)
                            v.tensor_mul(ot[:, :], tf2[:, :], s1tt[:, :])
                            sv.inc(v.tensor_add(ot[:, :], ot[:, :],
                                                x1tt2[:, :]))
                        V(fin_v2)

                        def fin_g2(g, n=n, m=m):
                            sv.wait(g)
                            dg.dinc(g.dma_start(
                                out=out[128 * m:128 * (m + 1),
                                        512 * n:512 * (n + 1)],
                                in_=ot[:, :]))
                        G(fin_g2)

        _ps22.close()
        if _DEV:
            dram_map = {"qd": qd_d, "kd": kd_d, "vd": vd_d, "s0": s0d,
                        "acc": accd, "x1": x1d, "a1": attn1d, "s1": s1d,
                        "mid": midd, "ccg": cc_g_out, "cca": cc_a_out,
                        "cckv": cc_kv_out}
            def tg(g):
                sv.wait(g); ss.wait(g); st.wait(g); dh_.wait(g)
                for tname in taps:
                    if tname in dram_map:
                        dg.dinc(g.dma_start(out=taps[tname].ap(),
                                            in_=dram_map[tname].ap()))
            G(tg)

        def fin_wait(g):
            g.wait_ge(dg.sem, dg.n)
            g.wait_ge(dh_.sem, dh_.n)
        G(fin_wait)

    return nc


_NC_CACHE = {}


def _pretile(w):
    K, N = w.shape
    return np.ascontiguousarray(
        w.reshape(K // 128, 128, N).transpose(1, 0, 2).reshape(128, -1))


def _pretile_c(w, nb=512):
    K, N = w.shape
    kc = K // 128
    t = w.reshape(kc, 128, N // nb, nb).transpose(1, 2, 0, 3)
    return np.ascontiguousarray(t.reshape(128, -1))


def _pretile_g(w, kgs=4, nb=512):
    K, N = w.shape
    kc = K // 128
    kk = kc // kgs
    t = w.reshape(kgs, kk, 128, N // nb, nb).transpose(2, 3, 0, 1, 4)
    return np.ascontiguousarray(t.reshape(128, -1))


def kernel(L0_attn_hidden_states, L0_attn_slope_rate, L0_attn_w_qkv,
           L0_attn_w_gate, L0_attn_w_out, L0_attn_norm_w, L0_ffn_Wg,
           L0_ffn_Wu, L0_ffn_Wd, L0_ffn_Ga, L0_ffn_Gb, L0_ffn_Ua, L0_ffn_Ub,
           L0_ffn_Da, L0_ffn_Db, L0_ffn_Wr, L0_ffn_temperature, L1_attn_Wq,
           L1_attn_Wk, L1_attn_Wv, L1_attn_Wo, L1_attn_q_norm_w,
           L1_attn_k_norm_w, L1_ffn_gate_w, L1_ffn_gate_b, L1_ffn_val_w,
           L1_ffn_val_b, L1_ffn_out_w, L1_ffn_out_b):
    f32 = np.float32
    asf = lambda a: np.ascontiguousarray(a, dtype=f32)

    x = asf(L0_attn_hidden_states)
    rep = lambda vec: np.ascontiguousarray(
        np.tile(np.asarray(vec, f32).reshape(1, -1), (128, 1)))

    shared = {
        "wqkv": _pretile_c(asf(L0_attn_w_qkv).T),
        "wgate": _pretile_c(asf(L0_attn_w_gate).T),
        "wout": _pretile_c(asf(L0_attn_w_out).T),
        "wg": np.stack([_pretile_c(asf(L0_ffn_Wg[e])) for e in range(E)]),
        "wu": np.stack([_pretile_c(asf(L0_ffn_Wu[e])) for e in range(E)]),
        "wd": np.stack([_pretile_c(asf(L0_ffn_Wd[e])) for e in range(E)]),
        "ga": np.stack([_pretile(asf(L0_ffn_Ga[l])) for l in range(2)]),
        "ua": np.stack([_pretile(asf(L0_ffn_Ua[l])) for l in range(2)]),
        "da": np.stack([_pretile(asf(L0_ffn_Da[l])) for l in range(2)]),
        "gbp": np.stack([np.concatenate(
            [asf(L0_ffn_Gb[l]), np.zeros((64, F), f32)], axis=0)
            for l in range(2)]),
        "ubp": np.stack([np.concatenate(
            [asf(L0_ffn_Ub[l]), np.zeros((64, F), f32)], axis=0)
            for l in range(2)]),
        "dbp": np.stack([np.concatenate(
            [asf(L0_ffn_Db[l]), np.zeros((64, D), f32)], axis=0)
            for l in range(2)]),
        "wr": _pretile(asf(L0_ffn_Wr)),
        "wq1": _pretile_c(asf(L1_attn_Wq).T),
        "wk1": _pretile_c(asf(L1_attn_Wk).T),
        "wv1": _pretile_c(asf(L1_attn_Wv).T),
        "wo1": _pretile_c(asf(L1_attn_Wo).T),
        "wgt1": _pretile_c(asf(L1_ffn_gate_w).T),
        "wvt1": _pretile_c(asf(L1_ffn_val_w).T),
        "wot1": _pretile_g(asf(L1_ffn_out_w).T),
        "gb_r": rep(L1_ffn_gate_b),
        "vb_r": rep(L1_ffn_val_b),
        "ob_r": rep(L1_ffn_out_b),
        "l0nw_r": rep(L0_attn_norm_w),
        "qnw_r": rep(np.tile(np.asarray(L1_attn_q_norm_w, f32), H)),
        "knw_r": rep(np.tile(np.asarray(L1_attn_k_norm_w, f32), H)),
        "slope_r": rep(L0_attn_slope_rate),
        "temp_r": rep(L0_ffn_temperature),
        "arq": np.ascontiguousarray(
            np.arange(128, dtype=f32)[:, None] + np.array([1.0, 129.0], f32)),
        "ark": np.ascontiguousarray(
            np.array([255.0, 127.0], f32) - np.arange(128, dtype=f32)[:, None]),
        "invf_hi": None, "invf_lo": None,
    }
    inv64 = 1.0 / (10000.0 ** (np.arange(0, 128, 2, dtype=np.float64) / 128.0))
    hi = inv64.astype(f32)
    lo = (inv64 - hi.astype(np.float64)).astype(f32)
    shared["invf_hi"] = np.ascontiguousarray(np.tile(hi, (128, 1)))
    shared["invf_lo"] = np.ascontiguousarray(np.tile(lo, (128, 1)))
    shared.pop("invf_hi", None) if False else None
    ii = np.arange(128)[:, None]
    jj = np.arange(1152)[None, :]
    band = ((jj > ii) & (jj <= ii + 1024))

    gwT = asf(L1_ffn_gate_w).T
    vwT = asf(L1_ffn_val_w).T
    owT = asf(L1_ffn_out_w).T

    in_maps = []
    for c in range(8):
        b, p = divmod(c, 4)
        im = dict(shared)
        im["x0"] = np.ascontiguousarray(x[b, 1024 * p:1024 * (p + 1), :])
        im["posb"] = np.full((128, 1), 1024.0 * p, f32)
        im["chunkid"] = np.array([[p]], np.int32)
        dist = np.zeros((128, 4), f32)
        msk = np.zeros((128, 4), f32)
        for cp in range(4):
            if cp < p:
                dist[:, cp] = 256.0 * (4 * (p - 1 - cp))
                msk[:, cp] = 1.0
        im["ldist"] = dist
        im["lmask"] = msk
        m8 = np.zeros((128, MT * 1152), ml_dtypes.bfloat16)
        for t_ in range(MT):
            valid = (jj >= 1024 - 128 * t_ - 1024 * p)
            m8[:, t_ * 1152:(t_ + 1) * 1152] = (band & valid).astype(
                ml_dtypes.bfloat16)
        im["mask8_r"] = m8
        im["gw_sl"] = np.ascontiguousarray(gwT[256 * c:256 * (c + 1), :])
        im["vw_sl"] = np.ascontiguousarray(vwT[256 * c:256 * (c + 1), :])
        im["ow_sl"] = np.ascontiguousarray(owT[1024 * c:1024 * (c + 1), :])
        in_maps.append(im)

    key = "main"
    if key not in _NC_CACHE:
        _NC_CACHE[key] = build()
    nc = _NC_CACHE[key]
    res = run_bass_kernel_spmd(nc, in_maps, core_ids=list(range(8)))

    outp = np.empty((2, 4096, D), f32)
    for c in range(8):
        b, p = divmod(c, 4)
        outp[b, 1024 * p:1024 * (p + 1), :] = res.results[c]["out"]
    return outp



# version 14
# speedup vs baseline: 1.0085x; 1.0085x over previous
import math
import numpy as np
import ml_dtypes
import concourse.bass as bass
import concourse.mybir as mybir
from concourse.bass_utils import run_bass_kernel_spmd

F32 = mybir.dt.float32
BF16 = mybir.dt.bfloat16
I32 = mybir.dt.int32
AF = mybir.ActivationFunctionType
ALU = mybir.AluOpType
AX = mybir.AxisListType

TOK, MT, D, KC = 1024, 8, 2048, 16
H, DH, NB, E, F, HID = 16, 128, 4, 8, 2048, 8192

EPS_LN = 1e-5
EPS_RMS = 1e-6
GC_ = 1.0 / math.e
GL_ = 0.5 - math.log(4.0 / 3.0)
GU_ = 0.5
INV2PI = float(np.float32(1.0 / (2 * math.pi)))
PI = float(np.float32(math.pi))
PI_2 = float(np.float32(math.pi / 2))
T2PI_HI = float(np.float32(2 * math.pi))
T2PI_LO = float(2 * np.float64(math.pi) - np.float64(np.float32(2 * math.pi)))
ISQD = float(np.float32(1.0 / math.sqrt(DH)))
MAGIC = 16384.5
NMAGIC = -16384.0

_DEV = False
_BLKN = 64
_BLKMERGE = {"vector", "scalar"}


_DP_OPS = {
    "tensor_copy", "tensor_tensor", "tensor_add", "tensor_sub", "tensor_mul",
    "tensor_max", "tensor_scalar", "tensor_single_scalar", "tensor_scalar_add",
    "tensor_scalar_sub", "tensor_scalar_mul", "tensor_scalar_max",
    "tensor_scalar_min", "tensor_relu", "tensor_reduce", "reduce_max",
    "reduce_sum", "scalar_tensor_tensor", "reciprocal", "memset", "memzero",
    "select", "transpose", "activation", "sign", "square", "sqrt", "mul",
    "copy", "add", "dma_start",
}
_DP_ALWAYS = {"reciprocal"}


def _ap_name(a):
    try:
        n = a.name
        return n() if callable(n) else n
    except Exception:
        return None


class DrainProxy:

    def __init__(self, eng, state):
        self._e = eng
        self._w, self._r = state

    def _flush(self):
        self._e.drain()
        self._w.clear()
        self._r.clear()

    def __getattr__(self, n):
        a = getattr(self._e, n)
        if n not in _DP_OPS:
            return a

        def wrap(*args, **kw):
            names = []
            for x in list(args) + list(kw.values()):
                nm = _ap_name(x)
                if nm:
                    names.append(nm)
            outn = names[0] if names else None
            ins = set(names[1:])
            if (ins & self._w) or (outn in self._w) or (outn in self._r):
                self._flush()
            r = a(*args, **kw)
            if n in _DP_ALWAYS:
                self._flush()
            else:
                if outn:
                    self._w.add(outn)
                self._r |= ins
            return r
        return wrap


class TProxy:

    def __init__(self, eng):
        self._e = eng
        self.last = None

    def __getattr__(self, n):
        a = getattr(self._e, n)
        if n in ("matmul", "transpose"):
            def wrap(*args, **kw):
                r = a(*args, **kw)
                self.last = r
                return r
            return wrap
        return a


class Ctr:
    def __init__(self, ctx, nc, name):
        self.sem = ctx.enter_context(nc.semaphore(name))
        self.n = 0

    def inc(self, instr, k=1):
        instr.then_inc(self.sem, k)
        self.n += k
        return instr

    def dinc(self, instr):
        return self.inc(instr, 16)

    def wait(self, eng, n=None):
        eng.wait_ge(self.sem, self.n if n is None else n)


class Slot:
    def __init__(self, ap):
        self.ap = ap
        self.rd = None

    def guard(self, eng):
        if self.rd is not None:
            c, n = self.rd
            eng.wait_ge(c.sem, n)

    def consumed(self, ctr):
        self.rd = (ctr, ctr.n)


def build(dev_taps=()):
    nc = bass.Bass("TRN2", target_bir_lowering=False, detect_race_conditions=False)
    di = {}

    def din(name, shape, dt=F32):
        di[name] = nc.dram_tensor(name, shape, dt, kind="ExternalInput")
        return di[name]

    x0 = din("x0", [TOK, D])
    wqkv = din("wqkv", [128, KC * 3 * D])
    wgate = din("wgate", [128, KC * D])
    wout = din("wout", [128, KC * D])
    wg = din("wg", [E, 128, KC * F])
    wu = din("wu", [E, 128, KC * F])
    wd = din("wd", [E, 128, KC * D])
    ga = din("ga", [2, 128, KC * 64])
    gbp = din("gbp", [2, 128, F])
    ua = din("ua", [2, 128, KC * 64])
    ubp = din("ubp", [2, 128, F])
    da = din("da", [2, 128, KC * 64])
    dbp = din("dbp", [2, 128, D])
    wr = din("wr", [128, KC * E])
    wq1 = din("wq1", [128, KC * D])
    wk1 = din("wk1", [128, KC * D])
    wv1 = din("wv1", [128, KC * D])
    wo1 = din("wo1", [128, KC * D])
    wgt1 = din("wgt1", [128, KC * HID])
    wvt1 = din("wvt1", [128, KC * HID])
    wot1 = din("wot1", [128, 64 * D])
    gb_r = din("gb_r", [128, HID])
    vb_r = din("vb_r", [128, HID])
    ob_r = din("ob_r", [128, D])
    l0nw_r = din("l0nw_r", [128, D])
    qnw_r = din("qnw_r", [128, D])
    knw_r = din("knw_r", [128, D])
    slope_r = din("slope_r", [128, H])
    temp_r = din("temp_r", [128, 1])
    gw_sl = din("gw_sl", [256, HID])
    vw_sl = din("vw_sl", [256, HID])
    ow_sl = din("ow_sl", [1024, D])
    posb = din("posb", [128, 1])
    chunkid = din("chunkid", [1, 1], I32)
    ldist = din("ldist", [128, 4])
    lmask = din("lmask", [128, 4])
    arq = din("arq", [128, 2])
    ark = din("ark", [128, 2])
    mask8_r = din("mask8_r", [128, MT * 1152], BF16)
    invf_hi = din("invf_hi", [128, 64])
    invf_lo = din("invf_lo", [128, 64])

    out = nc.dram_tensor("out", [TOK, D], F32, kind="ExternalOutput")

    cc_a_in = nc.dram_tensor("cc_a_in", [H * 128, 128], F32)
    cc_a_out = nc.dram_tensor("cc_a_out", [4 * H * 128, 128], F32)
    cc_kv_in = nc.dram_tensor("cc_kv_in", [2 * TOK, D], BF16)
    cc_kv_out = nc.dram_tensor("cc_kv_out", [4 * 2 * TOK, D], BF16)
    cc_g_in = nc.dram_tensor("cc_g_in", [128, 4], F32)
    cc_g_out = nc.dram_tensor("cc_g_out", [128, 4], F32)
    x1d = nc.dram_tensor("x1d", [TOK, D], F32)
    s0d = nc.dram_tensor("s0d", [TOK, D], BF16)
    s1d = nc.dram_tensor("s1d", [TOK, D], BF16)
    midd = nc.dram_tensor("midd", [TOK, HID], BF16)
    qd_d = nc.dram_tensor("qd_d", [TOK, D], BF16)
    attn1d = nc.dram_tensor("attn1d", [TOK, D], BF16)
    kd_d = nc.dram_tensor("kd_d", [TOK, D], BF16)
    vd_d = nc.dram_tensor("vd_d", [TOK, D], BF16)
    accd = nc.dram_tensor("accd", [TOK, D], F32)

    taps = {}
    if _DEV:
        for tname, shape, tdt in dev_taps:
            taps[tname] = nc.dram_tensor("tap_" + tname, list(shape), tdt,
                                         kind="ExternalOutput")

    from contextlib import ExitStack
    with ExitStack() as ctx:
        def sb(name, shape, dt):
            return ctx.enter_context(nc.sbuf_tensor(name, shape, dt))

        dg = Ctr(ctx, nc, "dg")
        dh_ = Ctr(ctx, nc, "dh")
        sv = Ctr(ctx, nc, "sv")
        ss = Ctr(ctx, nc, "ss")
        st = Ctr(ctx, nc, "st")
        sg = Ctr(ctx, nc, "sg")
        cc = Ctr(ctx, nc, "cc")

        _blkh = [nc.Block(), 0]
        _blkh[0].__enter__()

        def _newblk():
            _blkh[0].__exit__(None, None, None)
            _blkh[0] = nc.Block()
            _blkh[0].__enter__()
            _blkh[1] = 0

        def _emit(kind, fn):
            if kind in _BLKMERGE:
                _blkh[1] += 1
                if _blkh[1] > _BLKN:
                    _newblk()
                    _blkh[1] = 1
                getattr(_blkh[0], kind)(fn)
            else:
                if _blkh[1]:
                    _newblk()
                getattr(_blkh[0], kind)(fn)
                _newblk()

        def _endblk():
            _blkh[0].__exit__(None, None, None)
        ctx.callback(_endblk)

        def G(fn):
            _emit("gpsimd", fn)

        vstate = (set(), set())
        sstate = (set(), set())

        def V(fn):
            _emit("vector", lambda v: fn(DrainProxy(v, vstate)))

        def S(fn):
            _emit("scalar", lambda s: fn(DrainProxy(s, sstate)))

        def T(fn):
            _emit("tensor", lambda t: fn(TProxy(t)))

        def Y(fn):
            _emit("sync", fn)

        pbanks = [Slot(ctx.enter_context(
            nc.psum_tensor("pb%d" % i, [128, 512], F32))) for i in range(8)]

        def fence():
            def fg(g):
                g.wait_ge(sv.sem, sv.n)
                g.wait_ge(ss.sem, ss.n)
                g.wait_ge(st.sem, st.n)
                g.wait_ge(dh_.sem, dh_.n)
                g.wait_ge(dg.sem, dg.n)
                g.sem_inc(sg.sem, 1)
                sg.n += 1
            G(fg)

        def tap_out(name, src_dram=None, src_sb=None, rows=None):
            if not _DEV or name not in taps:
                return
            def tg(g, name=name):
                sv.wait(g); ss.wait(g); st.wait(g)
                if src_sb is not None:
                    dg.dinc(g.dma_start(out=taps[name].ap(), in_=src_sb))
                else:
                    dg.dinc(g.dma_start(out=taps[name].ap(), in_=src_dram))
            G(tg)

        slope_s = sb("slope_s", [128, H], F32)
        temp_s = sb("temp_s", [128, 1], F32)
        invT = sb("invT", [128, 1], F32)
        posb_s = sb("posb_s", [128, 1], F32)
        chid_s = sb("chid_s", [1, 1], I32)
        ldist_s = sb("ldist_s", [128, 4], F32)
        lmask_s = sb("lmask_s", [128, 4], F32)
        arq_s = sb("arq_s", [128, 2], F32)
        ark_s = sb("ark_s", [128, 2], F32)
        invfh_s = sb("invfh_s", [128, 64], F32)
        invfl_s = sb("invfl_s", [128, 64], F32)
        io_p = sb("io_p", [128, 1], I32)
        io_pf = sb("io_pf", [128, 1], F32)
        bd_s = sb("bd_s", [128, H], F32)
        bdp_s = sb("bdp_s", [128, H], F32)
        qd_s = sb("qd_s", [128, 2 * H], F32)
        kd_s = sb("kd_s", [128, 2 * H], F32)
        coef_s = sb("coef_s", [128, 4 * H], F32)
        mu0 = sb("mu0", [128, MT], F32)
        rstd0 = sb("rstd0", [128, MT], F32)
        mu1 = sb("mu1", [128, MT], F32)
        rstd1 = sb("rstd1", [128, MT], F32)
        gx1 = sb("gx1", [128, MT], F32)
        sx1 = sb("sx1", [128, MT], F32)
        nsc1g = sb("nsc1g", [128, MT], F32)
        nsc1v = sb("nsc1v", [128, MT], F32)
        gx2 = sb("gx2", [128, MT], F32)
        sx2 = sb("sx2", [128, MT], F32)
        nsc3 = sb("nsc3", [128, MT], F32)
        thr = {}
        nthr = {}
        gamv = {}
        for nm in ("g", "v", "o"):
            thr[nm] = sb("thr_" + nm, [128, 1], F32)
            nthr[nm] = sb("nthr_" + nm, [128, 1], F32)
            gamv[nm] = sb("gam_" + nm, [128, 1], F32)
        wcols = sb("wcols", [128, MT * E], F32)
        colA = sb("colA", [128, 1], F32)
        colB = sb("colB", [128, 1], F32)
        colC = sb("colC", [128, 1], F32)
        colD = sb("colD", [128, 1], F32)
        cgc = sb("cgc", [128, 1], F32)
        cpi = sb("cpi", [128, 1], F32)
        cpi2 = sb("cpi2", [128, 1], F32)
        ncos8 = sb("ncos8", [128, MT * 64], F32)
        nsin8 = sb("nsin8", [128, MT * 64], F32)
        ones_bf = sb("ones_bf", [128, 128], BF16)
        onesf = sb("onesf", [128, 128], F32)
        sqs = sb("sqs", [128, D], F32)

        def setup_g(g):
            for dst, src in [(slope_s, slope_r), (temp_s, temp_r),
                             (posb_s, posb), (ldist_s, ldist),
                             (lmask_s, lmask), (arq_s, arq), (ark_s, ark),
                             (invfh_s, invf_hi), (invfl_s, invf_lo)]:
                dg.dinc(g.dma_start(out=dst[:, :], in_=src[:, :]))
            dg.dinc(g.dma_start(out=chid_s[:, :], in_=chunkid[:, :]))
            g.iota(io_p[:, :], [[0, 1]], channel_multiplier=1)
            g.sem_inc(sg.sem, 1)
            sg.n += 1
        G(setup_g)

        def setup_v(v):
            dg.wait(v)
            sg.wait(v)
            v.tensor_copy(io_pf[:, :], io_p[:, :])
            v.memset(ones_bf[:, :], 1.0)
            v.memset(onesf[:, :], 1.0)
            v.memset(cgc[:, :], -GC_)
            v.memset(cpi[:, :], -PI)
            v.memset(cpi2[:, :], -PI_2)
            v.reciprocal(invT[:, :], temp_s[:, :])
            v.drain()
            for h in range(H):
                sc = slope_s[:, h:h + 1]
                v.tensor_scalar(bd_s[:, h:h + 1], sc, -256.0, None, ALU.mult)
                v.tensor_scalar(qd_s[:, 2 * h:2 * h + 2], arq_s[:, :], sc,
                                -1.0, ALU.mult, ALU.mult)
                v.tensor_scalar(kd_s[:, 2 * h:2 * h + 2], ark_s[:, :], sc,
                                -1.0, ALU.mult, ALU.mult)
                v.tensor_scalar(coef_s[:, 4 * h:4 * h + 4], ldist_s[:, :], sc,
                                -1.0, ALU.mult, ALU.mult)
            sv.inc(v.engine_nop())
        V(setup_v)

        def setup_s(s):
            sv.wait(s)
            s.activation(bd_s[:, :], bd_s[:, :], AF.Exp)
            s.activation(qd_s[:, :], qd_s[:, :], AF.Exp)
            s.activation(kd_s[:, :], kd_s[:, :], AF.Exp)
            ss.inc(s.activation(coef_s[:, :], coef_s[:, :], AF.Exp))
        S(setup_s)

        def setup_v2(v):
            ss.wait(v)
            for h in range(H):
                v.tensor_mul(coef_s[:, 4 * h:4 * h + 4],
                             coef_s[:, 4 * h:4 * h + 4], lmask_s[:, :])
            sv.inc(v.engine_nop())
        V(setup_v2)

        _ps1 = ExitStack()
        gwbuf = _ps1.enter_context(nc.sbuf_tensor("gwbuf", [128, 2 * HID], F32))
        gcol = _ps1.enter_context(nc.sbuf_tensor("gcol", [128, 4], F32))
        if True:
            for j, (src, width) in enumerate([(gw_sl, 2 * HID),
                                              (vw_sl, 2 * HID),
                                              (ow_sl, 8 * D)]):
                def g_load(g, src=src, width=width, j=j):
                    if j > 0:
                        sv.wait(g)
                    na = width // HID if width >= HID else 1
                    na = src.shape[0] // 128
                    dg.dinc(g.dma_start(
                        out=gwbuf[:, 0:width].rearrange("p (a n) -> p a n",
                                                        a=na),
                        in_=src.ap().rearrange("(a p) n -> p a n", p=128)))
                G(g_load)

                def v_red(v, j=j, width=width):
                    dg.wait(v)
                    sv.inc(v.tensor_reduce(gcol[:, j:j + 1], gwbuf[:, 0:width],
                                           AX.X, ALU.add,
                                           apply_absolute_value=True))
                V(v_red)

            def v_fin(v):
                v.memset(gcol[:, 3:4], 0.0)
                sv.inc(v.engine_nop())
            V(v_fin)

            def g_exp(g):
                sv.wait(g)
                dg.dinc(g.dma_start(out=cc_g_in[:, :], in_=gcol[:, :]))
            G(g_exp)

        _ps1.close()
        _ps2 = ExitStack()
        xn_fm0 = _ps2.enter_context(nc.sbuf_tensor("xn_fm0", [128, MT * D], BF16))
        if True:
            def xn0_lhs(m, k):
                return xn_fm0[:, (m * KC + k) * 128:(m * KC + k) * 128 + 128]

            _ps3 = ExitStack()
            xt0 = _ps3.enter_context(nc.sbuf_tensor("xt0", [128, 2 * D], F32))
            xnt = _ps3.enter_context(nc.sbuf_tensor("xnt", [128, 2 * D], BF16))
            if True:
                for m in range(MT):
                    pb = m % 2

                    def p1_g(g, m=m, pb=pb):
                        if m >= 2:
                            dh_.wait(g, 16 * (m - 1))
                        dg.dinc(g.dma_start(out=xt0[:, pb * D:(pb + 1) * D],
                                            in_=x0[128 * m:128 * (m + 1), :]))
                    G(p1_g)

                    def p1_s(s, m=m, pb=pb):
                        dg.wait(s)
                        ss.inc(s.activation(sqs[:, :], xt0[:, pb * D:(pb + 1) * D], AF.Square,
                                            accum_out=colB[:, 0:1]))
                    S(p1_s)

                    def p1_v(v, m=m, pb=pb):
                        dg.wait(v)
                        ss.wait(v)
                        v.tensor_reduce(colA[:, :], xt0[:, pb * D:(pb + 1) * D], AX.X, ALU.add)
                        v.tensor_scalar(mu0[:, m:m + 1], colA[:, :], 1.0 / D,
                                        None, ALU.mult)
                        v.tensor_mul(colC[:, :], mu0[:, m:m + 1],
                                     mu0[:, m:m + 1])
                        v.tensor_scalar(colD[:, :], colB[:, :], 1.0 / D, None,
                                        ALU.mult)
                        v.tensor_sub(colC[:, :], colD[:, :], colC[:, :])
                        sv.inc(v.tensor_scalar(colC[:, :], colC[:, :], EPS_LN,
                                               None, ALU.add))
                    V(p1_v)

                    def p1_s2(s, m=m):
                        sv.wait(s)
                        ss.inc(s.activation(colC[:, :], colC[:, :], AF.Sqrt))
                    S(p1_s2)

                    def p1_v2(v, m=m, pb=pb):
                        ss.wait(v)
                        v.reciprocal(rstd0[:, m:m + 1], colC[:, :])
                        v.drain()
                        sv.inc(v.scalar_tensor_tensor(
                            xnt[:, pb * D:(pb + 1) * D], xt0[:, pb * D:(pb + 1) * D], mu0[:, m:m + 1],
                            rstd0[:, m:m + 1].broadcast_to((128, D)),
                            ALU.subtract, ALU.mult))
                    V(p1_v2)

                    def p1_y(sy, m=m, pb=pb):
                        sv.wait(sy)
                        o3 = xn_fm0.ap().rearrange("p (q c) -> p q c", c=128)[
                            :, KC * m:KC * (m + 1), :]
                        dh_.dinc(sy.dma_start(out=o3, in_=xnt[:, pb * D:(pb + 1) * D],
                                              transpose=True))
                    Y(p1_y)

            _ps3.close()
            def emit_proj(wsrc_fn, n_chunks, lhs_fn, evac_fn, wbuf, kc=KC,
                          extra_mm=None, nslots=(0, 1, 2, 3, 4, 5, 6, 7),
                          dbl=True):
                wuse = [0, 0]
                for n in range(n_chunks):
                    wb = (n % 2) if dbl else 0

                    def wload(g, n=n, wb=wb):
                        if dbl and n >= 2:
                            st.wait(g, wuse[wb])
                        elif not dbl and n >= 1:
                            st.wait(g, wuse[0])
                        if dbl:
                            dst = wbuf.ap().rearrange("p (w q) -> p w q",
                                                      w=2)[:, wb]
                        else:
                            dst = wbuf.ap()
                        dg.dinc(g.dma_start(out=dst, in_=wsrc_fn(n)))
                    G(wload)

                    for m in range(MT):
                        slot = pbanks[nslots[m % len(nslots)]]

                        def mm(t, n=n, m=m, wb=wb, slot=slot):
                            if m == 0:
                                dg.wait(t)
                            slot.guard(t)
                            if dbl:
                                wv = wbuf.ap().rearrange("p (w q) -> p w q",
                                                         w=2)[:, wb]
                            else:
                                wv = wbuf.ap()
                            wv = wv.rearrange("p (k n) -> p k n", k=kc)
                            last = (extra_mm is None)
                            for k in range(kc):
                                t.matmul(slot.ap[:, 0:512], lhs_fn(m, k),
                                         wv[:, k], start=(k == 0),
                                         stop=(k == kc - 1 and last))
                            if extra_mm is not None:
                                extra_mm(t, m, n, slot)
                            st.inc(t.last)
                        T(mm)
                        wuse[wb] = st.n
                        evac_fn(m, n, slot)

            def tiled_src(handle, kc):
                def fn(n):
                    return handle.ap().rearrange("p (n r) -> p n r",
                                                 r=kc * 512)[:, n]
                return fn

            if _DEV and "cols" in taps:
                def ctap(g):
                    sv.wait(g)
                    dg.dinc(g.dma_start(
                        out=taps["cols"].ap()[:, 0:MT], in_=mu0[:, :]))
                    dg.dinc(g.dma_start(
                        out=taps["cols"].ap()[:, MT:2 * MT], in_=rstd0[:, :]))
                G(ctap)

            if _DEV and "xnf" in taps:
                def xtap(g):
                    sv.wait(g)
                    dh_.wait(g)
                    dg.dinc(g.dma_start(out=taps["xnf"].ap(),
                                        in_=xn_fm0[:, :]))
                G(xtap)

            from contextlib import nullcontext
            qst = ExitStack()
            qkv_st = qst.enter_context(
                nc.sbuf_tensor("qkv_st", [128, 2 * 512], BF16))
            _ps4 = ExitStack()
            if True:
                _ps5 = ExitStack()
                if True:
                    qkv_dram = [qd_d, kd_d, vd_d]
                    _ps6 = ExitStack()
                    wbufA = _ps6.enter_context(nc.sbuf_tensor("wbufA", [128, 2 * KC * 512], BF16))
                    if True:
                        def t_wt(t):
                            dh_.wait(t)
                        T(t_wt)

                        def evac_qkv(m, n, slot):
                            sh = (n * MT + m) % 2

                            def ev(s, m=m, n=n, slot=slot, sh=sh):
                                st.wait(s)
                                if n * MT + m >= 2:
                                    dh_.wait(s, dh_.n)
                                s.activation(
                                    qkv_st[:, sh * 512:sh * 512 + 512],
                                    slot.ap[:, 0:512], AF.Silu)
                                ss.inc(s.nop())
                                which, ncol = divmod(n, 4)
                                dh_.dinc(s.dma_start(
                                    out=qkv_dram[which][
                                        128 * m:128 * (m + 1),
                                        512 * ncol:512 * (ncol + 1)],
                                    in_=qkv_st[:, sh * 512:sh * 512 + 512]))
                            S(ev)
                            slot.consumed(ss)
                        emit_proj(tiled_src(wqkv, KC), 12, xn0_lhs, evac_qkv,
                                  wbufA)

                    _ps6.close()
                    def g_ar(g):
                        g.wait_ge(dg.sem, dg.n)
                        cc.inc(g.collective_compute(
                            "AllReduce", ALU.add,
                            replica_groups=[[0, 1, 2, 3, 4, 5, 6, 7]],
                            ins=[cc_g_in.ap().opt()],
                            outs=[cc_g_out.ap().opt()]))
                    G(g_ar)
                    qst.close()

                    aost = ExitStack()
                    attn_o = aost.enter_context(
                        nc.sbuf_tensor("attn_o", [128, MT * D], BF16))
                    kvst3 = ExitStack()
                    kvloc = kvst3.enter_context(
                        nc.sbuf_tensor("kvloc", [128, H * 128], F32))
                    kvloc_bf = kvst3.enter_context(
                        nc.sbuf_tensor("kvloc_bf", [128, H * 128], BF16))
                    ps_s = [pbanks[0], pbanks[1]]
                    ps_oi = [pbanks[2], pbanks[3]]
                    ps_ii = [pbanks[4], pbanks[5]]
                    ps_cc = pbanks[6]

                    _ps7 = ExitStack()
                    dmask = _ps7.enter_context(nc.sbuf_tensor("dmask", [128, H * 2 * 256], BF16))
                    idx2 = _ps7.enter_context(nc.sbuf_tensor("idx2", [128, 2 * 256], I32))
                    idx2f = _ps7.enter_context(nc.sbuf_tensor("idx2f", [128, 2 * 256], F32))
                    m01 = _ps7.enter_context(nc.sbuf_tensor("m01", [128, 256], F32))
                    dtmp = _ps7.enter_context(nc.sbuf_tensor("dtmp", [128, 256], F32))
                    qk_fm = _ps7.enter_context(nc.sbuf_tensor("qk_fm", [128, 2 * 4096], BF16))
                    stbf = _ps7.enter_context(nc.sbuf_tensor("stbf", [128, 2 * 256], BF16))
                    vkd2 = _ps7.enter_context(nc.sbuf_tensor("vkd2", [128, 256], BF16))
                    q_bt = _ps7.enter_context(nc.sbuf_tensor("q_bt", [128, 2 * D], BF16))
                    k_bt = _ps7.enter_context(nc.sbuf_tensor("k_bt", [128, 2 * D], BF16))
                    v_bt = _ps7.enter_context(nc.sbuf_tensor("v_bt", [128, 2 * D], BF16))
                    itmp = _ps7.enter_context(nc.sbuf_tensor("itmp", [128, 128], F32))
                    if True:

                        def p3_iota(g):
                            g.iota(idx2[:, 0:256], [[1, 256]], base=0,
                                   channel_multiplier=-1)
                            g.iota(idx2[:, 256:512], [[1, 256]], base=-128,
                                   channel_multiplier=-1)
                            g.sem_inc(sg.sem, 1)
                            sg.n += 1
                        G(p3_iota)

                        def p3_prep(v):
                            sg.wait(v)
                            ss.wait(v)
                            v.tensor_copy(idx2f[:, 0:256], idx2[:, 0:256])
                            v.tensor_copy(idx2f[:, 256:512], idx2[:, 256:512])
                            v.memset(kvloc[:, :], 0.0)
                            v.memset(kvloc_bf[:, :], 0.0)
                            sv.inc(v.engine_nop())
                        V(p3_prep)

                        for h in range(H):
                            for ko in range(2):
                                def dm_v(v, h=h, ko=ko):
                                    v.tensor_scalar(m01[:, :], idx2f[:, ko * 256:ko * 256 + 256],
                                                    0.0, None, ALU.is_ge)
                                    sv.inc(v.tensor_scalar(
                                        dtmp[:, :], idx2f[:, ko * 256:ko * 256 + 256],
                                        slope_s[:, h:h + 1], -1.0, ALU.mult,
                                        ALU.mult))
                                V(dm_v)

                                def dm_s(s, h=h, ko=ko):
                                    sv.wait(s)
                                    ss.inc(s.activation(dtmp[:, :], dtmp[:, :],
                                                        AF.Exp))
                                S(dm_s)

                                def dm_v2(v, h=h, ko=ko):
                                    ss.wait(v)
                                    off = (h * 2 + ko) * 256
                                    sv.inc(v.tensor_mul(
                                        dmask[:, off:off + 256], dtmp[:, :],
                                        m01[:, :]))
                                V(dm_v2)

                        for blk in range(NB):
                            def p3_ld(g, blk=blk):
                                if blk == 0:
                                    dh_.wait(g)
                                else:
                                    st.wait(g)
                                    sv.wait(g)
                                r0 = 256 * blk
                                for src, dst in ((qd_d, q_bt), (kd_d, k_bt),
                                                 (vd_d, v_bt)):
                                    dg.dinc(g.dma_start(
                                        out=dst.ap().rearrange(
                                            "p (t c) -> p t c", t=2),
                                        in_=src.ap()[r0:r0 + 256, :].rearrange(
                                            "(t p) c -> p t c", p=128)))
                            G(p3_ld)

                            def p3_tr(sy, blk=blk):
                                sv.wait(sy)
                                dg.wait(sy)
                                if blk > 0:
                                    st.wait(sy)
                                for mo in range(2):
                                    ap3 = qk_fm.ap().rearrange(
                                        "p (q c) -> p q c", c=128)
                                    dh_.dinc(sy.dma_start(
                                        out=ap3[:, 32 * mo:32 * mo + 16, :],
                                        in_=q_bt[:, mo * D:(mo + 1) * D],
                                        transpose=True))
                                    dh_.dinc(sy.dma_start(
                                        out=ap3[:, 32 * mo + 16:32 * mo + 32, :],
                                        in_=k_bt[:, mo * D:(mo + 1) * D],
                                        transpose=True))
                            Y(p3_tr)

                            def qfm(h, mo):
                                base = mo * 4096 + h * 128
                                return qk_fm[:, base:base + 128]

                            def kfm(h, mo):
                                base = mo * 4096 + 2048 + h * 128
                                return qk_fm[:, base:base + 128]

                            for h in range(H):
                                def p3_mm_s(t, blk=blk, h=h):
                                    if h == 0:
                                        dh_.wait(t)
                                    qrhs = qk_fm.ap().rearrange(
                                        "p (mo r) -> p mo r", mo=2)[
                                        :, :, h * 128:h * 128 + 128]
                                    for ko in range(2):
                                        ps_s[ko].guard(t)
                                        t.matmul(ps_s[ko].ap[:, 0:256],
                                                 kfm(h, ko), qrhs, start=True,
                                                 stop=True)
                                    st.inc(t.last)
                                T(p3_mm_s)

                                def p3_vm(v, blk=blk, h=h):
                                    st.wait(v)
                                    off = (h * 2) * 256
                                    v.tensor_mul(stbf[:, 0:256],
                                                 ps_s[0].ap[:, 0:256],
                                                 dmask[:, off:off + 256])
                                    v.tensor_mul(stbf[:, 256:512],
                                                 ps_s[1].ap[:, 0:256],
                                                 dmask[:, off + 256:off + 512])
                                    for ko in range(2):
                                        vsl = v_bt[:, ko * D + h * 128:
                                                   ko * D + h * 128 + 128]
                                        v.tensor_scalar(
                                            vkd2[:, ko * 128:ko * 128 + 128],
                                            vsl,
                                            kd_s[:, 2 * h + ko:2 * h + ko + 1],
                                            None, ALU.mult)
                                    sv.inc(v.engine_nop())
                                    ps_s[0].consumed(sv)
                                    ps_s[1].consumed(sv)
                                V(p3_vm)

                                def p3_mm2(t, blk=blk, h=h):
                                    sv.wait(t)
                                    for qo in range(2):
                                        ps_oi[qo].guard(t)
                                        ps_ii[qo].guard(t)
                                        for ko in range(2):
                                            vsl = v_bt[:, ko * D + h * 128:
                                                       ko * D + h * 128 + 128]
                                            t.matmul(
                                                ps_oi[qo].ap[:, 0:128],
                                                stbf[:, ko * 256 + qo * 128:
                                                     ko * 256 + qo * 128 + 128],
                                                vsl, start=(ko == 0),
                                                stop=(ko == 1))
                                        t.matmul(ps_ii[qo].ap[:, 0:128],
                                                 qfm(h, qo),
                                                 kvloc_bf[:, h * 128:
                                                          h * 128 + 128],
                                                 start=True, stop=True)
                                    ps_cc.guard(t)
                                    for ko in range(2):
                                        ksl = k_bt[:, ko * D + h * 128:
                                                   ko * D + h * 128 + 128]
                                        t.matmul(ps_cc.ap[:, 0:128], ksl,
                                                 vkd2[:, ko * 128:ko * 128 + 128],
                                                 start=(ko == 0), stop=(ko == 1))
                                    st.inc(t.last)
                                T(p3_mm2)

                                def p3_ev(v, blk=blk, h=h):
                                    st.wait(v)
                                    for qo in range(2):
                                        m = 2 * blk + qo
                                        asl = attn_o[:, m * D + h * 128:
                                                     m * D + h * 128 + 128]
                                        v.tensor_scalar(
                                            itmp[:, :], ps_ii[qo].ap[:, 0:128],
                                            qd_s[:, 2 * h + qo:2 * h + qo + 1],
                                            None, ALU.mult)
                                        v.tensor_add(asl,
                                                     ps_oi[qo].ap[:, 0:128],
                                                     itmp[:, :])
                                    ksl = kvloc[:, h * 128:h * 128 + 128]
                                    v.scalar_tensor_tensor(
                                        ksl, ksl, bd_s[:, h:h + 1],
                                        ps_cc.ap[:, 0:128], ALU.mult, ALU.add)
                                    v.tensor_copy(
                                        kvloc_bf[:, h * 128:h * 128 + 128], ksl)
                                    sv.inc(v.engine_nop())
                                    for sl in (ps_oi[0], ps_oi[1], ps_ii[0],
                                               ps_ii[1], ps_cc):
                                        sl.consumed(sv)
                                V(p3_ev)

                    _ps7.close()
                    def p35_g(g):
                        sv.wait(g)
                        o3 = cc_a_in.ap().rearrange("(h d) e -> d h e", h=H)
                        i3 = kvloc.ap().rearrange("p (h e) -> p h e", h=H)
                        dg.dinc(g.dma_start(out=o3, in_=i3))
                    G(p35_g)

                    def p35_y(g):
                        g.wait_ge(dg.sem, dg.n)
                        cc.inc(g.collective_compute(
                            "AllGather", ALU.bypass,
                            replica_groups=[[0, 1, 2, 3], [4, 5, 6, 7]],
                            ins=[cc_a_in.ap().opt()],
                            outs=[cc_a_out.ap().opt()]))
                    G(p35_y)
                    kvst3.close()

                _ps5.close()
                _ps8 = ExitStack()
                g0 = _ps8.enter_context(nc.sbuf_tensor("g0", [128, MT * D], BF16))
                if True:
                    _ps9 = ExitStack()
                    wbufB = _ps9.enter_context(nc.sbuf_tensor("wbufB", [128, 2 * KC * 512], BF16))
                    if True:
                        def evac_g0(m, n, slot):
                            def ev(s, m=m, n=n, slot=slot):
                                st.wait(s)
                                ss.inc(s.activation(
                                    g0[:, m * D + n * 512:m * D + n * 512 + 512],
                                    slot.ap[:, 0:512], AF.Sigmoid))
                            S(ev)
                            slot.consumed(ss)
                        emit_proj(tiled_src(wgate, KC), 4, xn0_lhs, evac_g0,
                                  wbufB)

                    _ps9.close()
                    _ps10 = ExitStack()
                    tmpA = _ps10.enter_context(nc.sbuf_tensor("tmpA", [128, H * 128], F32))
                    kvin = _ps10.enter_context(nc.sbuf_tensor("kvin", [128, H * 128], F32))
                    kvsc = _ps10.enter_context(nc.sbuf_tensor("kvsc", [128, H * 128], BF16))
                    qf_blk = _ps10.enter_context(nc.sbuf_tensor("qf_blk", [128, 2 * D], BF16))
                    qb5 = _ps10.enter_context(nc.sbuf_tensor("qb5", [128, 2 * D], BF16))
                    itmp2 = _ps10.enter_context(nc.sbuf_tensor("itmp2", [128, 128], F32))
                    if True:

                        def p5_v0(v):
                            cc.wait(v)
                            v.memset(kvin[:, :], 0.0)
                            v.memset(bdp_s[:, :], 1.0)
                            sv.inc(v.engine_nop())
                        V(p5_v0)

                        for cp in range(4):
                            def p5_g(g, cp=cp):
                                sv.wait(g)
                                i3 = cc_a_out.ap()[cp * H * 128:
                                                   (cp + 1) * H * 128, :]
                                i3 = i3.rearrange("(h d) e -> d h e", h=H)
                                o3 = tmpA.ap().rearrange("p (h e) -> p h e",
                                                         h=H)
                                dg.dinc(g.dma_start(out=o3, in_=i3))
                            G(p5_g)

                            def p5_v(v, cp=cp):
                                dg.wait(v)
                                for h in range(H):
                                    v.scalar_tensor_tensor(
                                        kvin[:, h * 128:h * 128 + 128],
                                        tmpA[:, h * 128:h * 128 + 128],
                                        coef_s[:, 4 * h + cp:4 * h + cp + 1],
                                        kvin[:, h * 128:h * 128 + 128],
                                        ALU.mult, ALU.add)
                                sv.inc(v.engine_nop())
                            V(p5_v)

                        for blk in range(NB):
                            def p5_vb(v, blk=blk):
                                if blk > 0:
                                    v.tensor_mul(bdp_s[:, :], bdp_s[:, :],
                                                 bd_s[:, :])
                                for h in range(H):
                                    v.tensor_scalar(
                                        kvsc[:, h * 128:h * 128 + 128],
                                        kvin[:, h * 128:h * 128 + 128],
                                        bdp_s[:, h:h + 1], None, ALU.mult)
                                sv.inc(v.engine_nop())
                            V(p5_vb)

                            def p5_ld(g, blk=blk):
                                sv.wait(g)
                                if blk > 0:
                                    st.wait(g)
                                r0 = 256 * blk
                                dg.dinc(g.dma_start(
                                    out=qb5.ap().rearrange("p (t c) -> p t c",
                                                           t=2),
                                    in_=qd_d.ap()[r0:r0 + 256, :].rearrange(
                                        "(t p) c -> p t c", p=128)))
                            G(p5_ld)

                            def p5_tr(sy, blk=blk):
                                dg.wait(sy)
                                if blk > 0:
                                    st.wait(sy)
                                for mo in range(2):
                                    o3 = qf_blk.ap().rearrange(
                                        "p (q c) -> p q c", c=128)[
                                        :, KC * mo:KC * mo + KC, :]
                                    dh_.dinc(sy.dma_start(
                                        out=o3,
                                        in_=qb5[:, mo * D:(mo + 1) * D],
                                        transpose=True))
                            Y(p5_tr)

                            for h in range(H):
                                def p5_mm(t, blk=blk, h=h):
                                    if h == 0:
                                        dh_.wait(t)
                                        sv.wait(t)
                                    for qo in range(2):
                                        ps_ii[qo].guard(t)
                                        t.matmul(
                                            ps_ii[qo].ap[:, 0:128],
                                            qf_blk[:, qo * D + h * 128:
                                                   qo * D + h * 128 + 128],
                                            kvsc[:, h * 128:h * 128 + 128],
                                            start=True, stop=True)
                                    st.inc(t.last)
                                T(p5_mm)

                                def p5_ev(v, blk=blk, h=h):
                                    st.wait(v)
                                    for qo in range(2):
                                        m = 2 * blk + qo
                                        asl = attn_o[:, m * D + h * 128:
                                                     m * D + h * 128 + 128]
                                        v.tensor_scalar(
                                            itmp2[:, :],
                                            ps_ii[qo].ap[:, 0:128],
                                            qd_s[:, 2 * h + qo:2 * h + qo + 1],
                                            None, ALU.mult)
                                        v.tensor_add(asl, itmp2[:, :], asl)
                                    sv.inc(v.engine_nop())
                                    ps_ii[0].consumed(sv)
                                    ps_ii[1].consumed(sv)
                                V(p5_ev)
                    _ps10.close()
                    tap_out("attn0", src_sb=attn_o[:, :])

                    _ps11 = ExitStack()
                    nwg = _ps11.enter_context(nc.sbuf_tensor("nwg", [128, D], BF16))
                    z_fm = _ps11.enter_context(nc.sbuf_tensor("z_fm", [128, MT * D], BF16))
                    l0nw_s = _ps11.enter_context(
                        nc.sbuf_tensor("l0nw_s", [128, D], F32))
                    if True:
                        def p6_ld(g):
                            sv.wait(g)
                            dg.dinc(g.dma_start(out=l0nw_s[:, :],
                                                in_=l0nw_r[:, :]))
                        G(p6_ld)
                        for m in range(MT):
                            def p6_s(s, m=m):
                                sv.wait(s)
                                if m > 0:
                                    dh_.wait(s)
                                ss.inc(s.activation(
                                    sqs[:, :], attn_o[:, m * D:(m + 1) * D],
                                    AF.Square, accum_out=colB[:, 0:1]))
                            S(p6_s)

                            def p6_v(v, m=m):
                                ss.wait(v)
                                sv.inc(v.tensor_scalar(colC[:, :], colB[:, :],
                                                       1.0 / D, EPS_RMS,
                                                       ALU.mult, ALU.add))
                            V(p6_v)

                            def p6_s2(s, m=m):
                                sv.wait(s)
                                ss.inc(s.activation(colC[:, :], colC[:, :],
                                                    AF.Sqrt))
                            S(p6_s2)

                            def p6_v2(v, m=m):
                                ss.wait(v)
                                if m == 0:
                                    dg.wait(v)
                                v.reciprocal(colD[:, :], colC[:, :])
                                v.drain()
                                v.tensor_mul(nwg[:, :],
                                             g0[:, m * D:(m + 1) * D],
                                             l0nw_s[:, :])
                                v.scalar_tensor_tensor(
                                    attn_o[:, m * D:(m + 1) * D],
                                    attn_o[:, m * D:(m + 1) * D],
                                    colD[:, 0:1], nwg[:, :], ALU.mult, ALU.mult)
                                sv.inc(v.engine_nop())
                            V(p6_v2)

                            def p6_y(sy, m=m):
                                sv.wait(sy)
                                o3 = z_fm.ap().rearrange("p (q c) -> p q c",
                                                         c=128)[
                                    :, KC * m:KC * (m + 1), :]
                                dh_.dinc(sy.dma_start(
                                    out=o3, in_=attn_o[:, m * D:(m + 1) * D],
                                    transpose=True))
                            Y(p6_y)

                        def t_wt2(t):
                            dh_.wait(t)
                        T(t_wt2)
                        wbufC = _ps11.enter_context(
                            nc.sbuf_tensor("wbufC", [128, 2 * KC * 512], BF16))
                        s0t = _ps11.enter_context(
                            nc.sbuf_tensor("s0t", [128, 512], BF16))

                        def z_lhs(m, k):
                            return z_fm[:, (m * KC + k) * 128:
                                        (m * KC + k) * 128 + 128]

                        def evac_s0(m, n, slot):
                            def ev(v, m=m, n=n, slot=slot):
                                st.wait(v)
                                dg.wait(v)
                                sv.inc(v.tensor_copy(s0t[:, :],
                                                     slot.ap[:, 0:512]))
                            V(ev)
                            slot.consumed(sv)

                            def stg(g, m=m, n=n):
                                sv.wait(g)
                                dg.dinc(g.dma_start(
                                    out=s0d[128 * m:128 * (m + 1),
                                            512 * n:512 * (n + 1)],
                                    in_=s0t[:, :]))
                            G(stg)
                        emit_proj(tiled_src(wout, KC), 4, z_lhs, evac_s0,
                                  wbufC)
                    _ps11.close()
                _ps8.close()
            _ps4.close()

            aost.close()
            fence()
            _ps12 = ExitStack()
            wbufD = _ps12.enter_context(nc.sbuf_tensor("wbufD", [128, 2 * KC * 512], BF16))
            wbufE = _ps12.enter_context(nc.sbuf_tensor("wbufE", [128, 2 * KC * 512], BF16))
            mid_fm = _ps12.enter_context(nc.sbuf_tensor("mid_fm", [128, MT * F], BF16))
            midc = _ps12.enter_context(nc.sbuf_tensor("midc", [128, 2 * 512], BF16))
            silg = _ps12.enter_context(nc.sbuf_tensor("silg", [128, 512], BF16))
            rtile = _ps12.enter_context(nc.sbuf_tensor("rtile", [128, E], F32))
            raux = _ps12.enter_context(nc.sbuf_tensor("raux", [128, E], F32))
            raux2 = _ps12.enter_context(nc.sbuf_tensor("raux2", [128, E], F32))
            zaux = _ps12.enter_context(nc.sbuf_tensor("zaux", [128, 2 * E], F32))
            t1g = _ps12.enter_context(nc.sbuf_tensor("t1g", [128, MT * 128], BF16))
            t1u = _ps12.enter_context(nc.sbuf_tensor("t1u", [128, MT * 128], BF16))
            t1gf = _ps12.enter_context(nc.sbuf_tensor("t1gf", [128, MT * 128], BF16))
            t1uf = _ps12.enter_context(nc.sbuf_tensor("t1uf", [128, MT * 128], BF16))
            s0tt = _ps12.enter_context(nc.sbuf_tensor("s0tt", [128, D], BF16))
            x0tt = _ps12.enter_context(nc.sbuf_tensor("x0tt", [128, D], F32))
            dstg = _ps12.enter_context(nc.sbuf_tensor("dstg", [128, 2 * 512], F32))
            accb = _ps12.enter_context(nc.sbuf_tensor("accb", [128, D], F32))
            fftmp = _ps12.enter_context(nc.sbuf_tensor("fftmp", [128, D], F32))
            wr_s = _ps12.enter_context(nc.sbuf_tensor("wr_s", [128, KC * E], BF16))
            lag = _ps12.enter_context(nc.sbuf_tensor("lag", [128, KC * 64], BF16))
            lau = _ps12.enter_context(nc.sbuf_tensor("lau", [128, KC * 64], BF16))
            lad = _ps12.enter_context(nc.sbuf_tensor("lad", [128, KC * 64], BF16))
            lbg = _ps12.enter_context(nc.sbuf_tensor("lbg", [128, F], BF16))
            lbu = _ps12.enter_context(nc.sbuf_tensor("lbu", [128, F], BF16))
            lbd = _ps12.enter_context(nc.sbuf_tensor("lbd", [128, D], BF16))
            if True:
                def p7_ld(g):
                    sg.wait(g)
                    dg.dinc(g.dma_start(out=wr_s[:, :], in_=wr[:, :]))
                G(p7_ld)

                for m in range(MT):
                    slot = pbanks[m % 2]

                    def r_mm(t, m=m, slot=slot):
                        slot.guard(t)
                        if m == 0:
                            dg.wait(t)
                        wv = wr_s.ap().rearrange("p (k n) -> p k n", k=KC)
                        for k in range(KC):
                            t.matmul(slot.ap[:, 0:E], xn0_lhs(m, k), wv[:, k],
                                     start=(k == 0), stop=(k == KC - 1))
                        st.inc(t.last)
                    T(r_mm)

                    def r_s(s, m=m, slot=slot):
                        st.wait(s)
                        ss.inc(s.activation(rtile[:, :], slot.ap[:, 0:E],
                                            AF.Sigmoid, scale=invT[:, 0:1]))
                    S(r_s)
                    slot.consumed(ss)

                    def r_s2(s, m=m):
                        ss.inc(s.activation(raux[:, :], rtile[:, :], AF.Abs,
                                            bias=cgc[:, 0:1]))
                    S(r_s2)

                    def r_v1(v, m=m):
                        ss.wait(v)
                        z1 = zaux[:, 0:E]
                        z2 = zaux[:, E:2 * E]
                        v.tensor_scalar(z1, rtile[:, :], GL_, None, ALU.is_ge)
                        v.tensor_scalar(z2, rtile[:, :], GU_, None, ALU.is_le)
                        sv.inc(v.tensor_mul(z1, z1, z2))
                    V(r_v1)

                    def r_s3(s, m=m):
                        sv.wait(s)
                        s.activation(raux2[:, :], raux[:, :], AF.Exp,
                                     scale=-10.0)
                        ss.inc(s.activation(raux[:, :], raux[:, :], AF.Exp,
                                            scale=-1.0 / 0.3))
                    S(r_s3)

                    def r_v2(v, m=m):
                        ss.wait(v)
                        wsl = wcols[:, m * E:(m + 1) * E]
                        z1 = zaux[:, 0:E]
                        z2 = zaux[:, E:2 * E]
                        v.tensor_mul(wsl, raux2[:, :], z1)
                        v.tensor_reduce(colA[:, :], wsl, AX.X, ALU.add)
                        v.tensor_scalar(colB[:, :], colA[:, :], 1e-8, None,
                                        ALU.is_lt)
                        v.tensor_reduce(colC[:, :], raux[:, :], AX.X, ALU.max)
                        v.tensor_scalar(z2, raux[:, :], colC[:, 0:1], None,
                                        ALU.is_ge)
                        v.scalar_tensor_tensor(raux2[:, :], z2, -2.0,
                                               raux[:, :], ALU.mult, ALU.add)
                        v.tensor_reduce(colC[:, :], raux2[:, :], AX.X, ALU.max)
                        v.tensor_scalar(raux2[:, :], raux2[:, :], colC[:, 0:1],
                                        None, ALU.is_ge)
                        v.tensor_add(z2, z2, raux2[:, :])
                        v.tensor_mul(raux[:, :], raux[:, :], z2)
                        v.tensor_reduce(colC[:, :], raux[:, :], AX.X, ALU.add)
                        v.tensor_scalar_max(colC[:, :], colC[:, :], 1e-8)
                        v.reciprocal(colD[:, :], colC[:, :])
                        v.drain()
                        v.tensor_scalar(raux[:, :], raux[:, :], colD[:, 0:1],
                                        None, ALU.mult)
                        v.tensor_scalar(colC[:, :], colB[:, :], -1.0, 1.0,
                                        ALU.mult, ALU.add)
                        v.tensor_scalar(wsl, wsl, colC[:, 0:1], None, ALU.mult)
                        v.scalar_tensor_tensor(wsl, raux[:, :], colB[:, 0:1],
                                               wsl, ALU.mult, ALU.add)
                        v.tensor_reduce(colC[:, :], wsl, AX.X, ALU.add)
                        v.tensor_scalar_max(colC[:, :], colC[:, :], 1e-8)
                        v.reciprocal(colD[:, :], colC[:, :])
                        v.drain()
                        v.tensor_scalar(wsl, wsl, colD[:, 0:1], None, ALU.mult)
                        sv.inc(v.engine_nop())
                    V(r_v2)

                def mid_lhs(m, k):
                    return mid_fm[:, (m * KC + k) * 128:(m * KC + k) * 128 + 128]

                pl = pbanks[6]
                wlastD = [0, 0]
                wlastE = [0, 0]
                prev_d_end = [0]
                for e in range(E):
                    lora = e < 2
                    if lora:
                        def l_ld(g, e=e):
                            st.wait(g)
                            for dst, srcv in ((lag, ga), (lau, ua), (lad, da)):
                                dg.dinc(g.dma_start(out=dst[:, :],
                                                    in_=srcv.ap()[e]))
                            for dst, srcv in ((lbg, gbp), (lbu, ubp),
                                              (lbd, dbp)):
                                dg.dinc(g.dma_start(out=dst[:, :],
                                                    in_=srcv.ap()[e]))
                        G(l_ld)

                        def l_v0(v, e=e):
                            dh_.wait(v)
                            v.memset(t1g[:, :], 0.0)
                            v.memset(t1u[:, :], 0.0)
                            sv.inc(v.engine_nop())
                        V(l_v0)
                        for m in range(MT):
                            for asrc, tdst in ((lag, t1g), (lau, t1u)):
                                def l_mm(t, m=m, e=e, asrc=asrc):
                                    if m == 0:
                                        dg.wait(t)
                                    pl.guard(t)
                                    src = asrc.ap().rearrange(
                                        "p (k n) -> p k n", k=KC)
                                    for k in range(KC):
                                        t.matmul(pl.ap[:, 0:64],
                                                 xn0_lhs(m, k), src[:, k],
                                                 start=(k == 0),
                                                 stop=(k == KC - 1))
                                    st.inc(t.last)
                                T(l_mm)

                                def l_ev(v, m=m, tdst=tdst):
                                    st.wait(v)
                                    sv.inc(v.tensor_copy(
                                        tdst[:, m * 128:m * 128 + 64],
                                        pl.ap[:, 0:64]))
                                V(l_ev)
                                pl.consumed(sv)

                        def l_tr(sy, e=e):
                            sv.wait(sy)
                            st.wait(sy)
                            for tsrc, tdst in ((t1g, t1gf), (t1u, t1uf)):
                                o3 = tdst.ap().rearrange("p (q c) -> p q c",
                                                         c=128)
                                dh_.dinc(sy.dma_start(out=o3, in_=tsrc[:, :],
                                                      transpose=True))
                        Y(l_tr)

                        def t_wt3(t, e=e):
                            dh_.wait(t)
                        T(t_wt3)

                    for n in range(4):
                        wb = n % 2

                        def wload_gu(g, e=e, n=n, wb=wb,
                                     wl=max(wlastD[wb], wlastE[wb])):
                            if wl:
                                st.wait(g, wl)
                            for wsrc, wbuf in ((wg, wbufD), (wu, wbufE)):
                                dst = wbuf.ap().rearrange("p (w q) -> p w q",
                                                          w=2)[:, wb]
                                src = wsrc.ap()[e].rearrange(
                                    "p (n r) -> p n r", r=KC * 512)[:, n]
                                dg.dinc(g.dma_start(out=dst, in_=src))
                        G(wload_gu)

                        for m in range(MT):
                            sg_ = pbanks[m % 2]
                            su_ = pbanks[2 + m % 2]

                            def gu_mm(t, e=e, n=n, m=m, wb=wb, sg_=sg_,
                                      su_=su_, lora=lora):
                                if m == 0:
                                    dg.wait(t)
                                for slot, wbuf, lf, bsrc in (
                                        (sg_, wbufD, t1gf, lbg),
                                        (su_, wbufE, t1uf, lbu)):
                                    slot.guard(t)
                                    wv = wbuf.ap().rearrange(
                                        "p (w q) -> p w q", w=2)[:, wb]
                                    wv = wv.rearrange("p (k n) -> p k n", k=KC)
                                    for k in range(KC):
                                        t.matmul(slot.ap[:, 0:512],
                                                 xn0_lhs(m, k), wv[:, k],
                                                 start=(k == 0),
                                                 stop=(k == KC - 1 and
                                                       not lora))
                                    if lora:
                                        t.matmul(
                                            slot.ap[:, 0:512],
                                            lf[:, m * 128:m * 128 + 128],
                                            bsrc[:, 512 * n:512 * (n + 1)],
                                            start=False, stop=True)
                                st.inc(t.last)
                            T(gu_mm)
                            wlastD[wb] = st.n
                            wlastE[wb] = st.n

                            mb = m % 2

                            def gu_ev_s(s, e=e, n=n, m=m, sg_=sg_):
                                st.wait(s)
                                sv.wait(s)
                                ss.inc(s.activation(silg[:, :],
                                                    sg_.ap[:, 0:512], AF.Silu))
                            S(gu_ev_s)
                            sg_.consumed(ss)

                            def gu_ev_v(v, e=e, n=n, m=m, mb=mb, su_=su_):
                                ss.wait(v)
                                if not (e == 0 and n == 0 and m < 2):
                                    dh_.wait(v, dh_.n)
                                sv.inc(v.tensor_mul(midc[:, mb * 512:mb * 512 + 512], silg[:, :],
                                                    su_.ap[:, 0:512]))
                            V(gu_ev_v)
                            su_.consumed(sv)

                            def gu_tr(sy, e=e, n=n, m=m, mb=mb,
                                      pde=prev_d_end[0]):
                                sv.wait(sy)
                                if e > 0 and n == 0 and m == 0:
                                    st.wait(sy, pde)
                                o3 = mid_fm.ap().rearrange(
                                    "p (q c) -> p q c", c=128)[
                                    :, m * KC + 4 * n:m * KC + 4 * n + 4, :]
                                dh_.dinc(sy.dma_start(out=o3,
                                                      in_=midc[:, mb * 512:mb * 512 + 512],
                                                      transpose=True))
                            Y(gu_tr)

                    if lora:
                        def t2_v0(v, e=e):
                            dh_.wait(v)
                            v.memset(t1g[:, :], 0.0)
                            sv.inc(v.engine_nop())
                        V(t2_v0)
                        for m in range(MT):
                            def t2_mm(t, m=m, e=e):
                                if m == 0:
                                    dh_.wait(t)
                                pl.guard(t)
                                src = lad.ap().rearrange(
                                    "p (k n) -> p k n", k=KC)
                                for k in range(KC):
                                    t.matmul(pl.ap[:, 0:64], mid_lhs(m, k),
                                             src[:, k], start=(k == 0),
                                             stop=(k == KC - 1))
                                st.inc(t.last)
                            T(t2_mm)

                            def t2_ev(v, m=m):
                                st.wait(v)
                                sv.inc(v.tensor_copy(
                                    t1g[:, m * 128:m * 128 + 64],
                                    pl.ap[:, 0:64]))
                            V(t2_ev)
                            pl.consumed(sv)

                        def t2_tr(sy, e=e):
                            sv.wait(sy)
                            o3 = t1gf.ap().rearrange("p (q c) -> p q c", c=128)
                            dh_.dinc(sy.dma_start(out=o3, in_=t1g[:, :],
                                                  transpose=True))
                        Y(t2_tr)

                    for n in range(4):
                        wb = n % 2

                        def wload_d(g, e=e, n=n, wb=wb, wl=wlastD[wb]):
                            if wl:
                                st.wait(g, wl)
                            dst = wbufD.ap().rearrange("p (w q) -> p w q",
                                                       w=2)[:, wb]
                            src = wd.ap()[e].rearrange(
                                "p (n r) -> p n r", r=KC * 512)[:, n]
                            dg.dinc(g.dma_start(out=dst, in_=src))
                        G(wload_d)

                        for m in range(MT):
                            sd_ = pbanks[4 + m % 2]

                            def d_mm(t, e=e, n=n, m=m, wb=wb, sd_=sd_,
                                     lora=lora):
                                if m == 0:
                                    dg.wait(t)
                                    if n == 0:
                                        dh_.wait(t)
                                sd_.guard(t)
                                wv = wbufD.ap().rearrange("p (w q) -> p w q",
                                                          w=2)[:, wb]
                                wv = wv.rearrange("p (k n) -> p k n", k=KC)
                                for k in range(KC):
                                    t.matmul(sd_.ap[:, 0:512], mid_lhs(m, k),
                                             wv[:, k], start=(k == 0),
                                             stop=(k == KC - 1 and not lora))
                                if lora:
                                    t.matmul(sd_.ap[:, 0:512],
                                             t1gf[:, m * 128:m * 128 + 128],
                                             lbd[:, 512 * n:512 * (n + 1)],
                                             start=False, stop=True)
                                st.inc(t.last)
                            T(d_mm)

                            dsh = (n * MT + m) % 2

                            def d_ev(v, e=e, n=n, m=m, sd_=sd_, dsh=dsh):
                                st.wait(v)
                                dg.wait(v)
                                sv.inc(v.tensor_scalar(
                                    dstg[:, dsh * 512:dsh * 512 + 512],
                                    sd_.ap[:, 0:512],
                                    wcols[:, m * E + e:m * E + e + 1], None,
                                    ALU.mult))
                            V(d_ev)
                            sd_.consumed(sv)

                            def d_evg(g, e=e, n=n, m=m, dsh=dsh):
                                sv.wait(g)
                                dg.dinc(g.dma_start(
                                    out=accd[128 * m:128 * (m + 1),
                                             512 * n:512 * (n + 1)],
                                    in_=dstg[:, dsh * 512:dsh * 512 + 512],
                                    accum_op=(ALU.bypass if e == 0
                                              else ALU.add)))
                            G(d_evg)
                        wlastD[wb] = st.n
                    prev_d_end[0] = st.n

                for m in range(MT):
                    def c_g(g, m=m):
                        sv.wait(g)
                        dg.dinc(g.dma_start(out=s0tt[:, :],
                                            in_=s0d[128 * m:128 * (m + 1), :]))
                        dg.dinc(g.dma_start(out=x0tt[:, :],
                                            in_=x0[128 * m:128 * (m + 1), :]))
                        dg.dinc(g.dma_start(out=accb[:, :],
                                            in_=accd[128 * m:128 * (m + 1), :]))
                    G(c_g)

                    def c_s(s, m=m):
                        dg.wait(s)
                        ss.inc(s.activation(accb[:, :], accb[:, :], AF.Silu))
                    S(c_s)

                    def c_v(v, m=m):
                        dg.wait(v)
                        ss.wait(v)
                        v.tensor_mul(fftmp[:, :], s0tt[:, :], accb[:, :])
                        sv.inc(v.tensor_add(accb[:, :], fftmp[:, :],
                                            x0tt[:, :]))
                    V(c_v)

                    def c_g2(g, m=m):
                        sv.wait(g)
                        dg.dinc(g.dma_start(out=x1d[128 * m:128 * (m + 1), :],
                                            in_=accb[:, :]))
                    G(c_g2)
            _ps12.close()
            tap_out("x1", src_dram=x1d.ap())
        _ps2.close()

        xfst = ExitStack()
        xn1_fm = xfst.enter_context(nc.sbuf_tensor("xn1_fm", [128, MT * D],
                                                   BF16))

        def xn1_lhs(m, k):
            return xn1_fm[:, (m * KC + k) * 128:(m * KC + k) * 128 + 128]

        _ps13 = ExitStack()
        xnt1 = _ps13.enter_context(nc.sbuf_tensor("xnt1", [128, D], BF16))
        xt8 = _ps13.enter_context(nc.sbuf_tensor("xt8", [128, D], F32))
        if True:
            for m in range(MT):
                def p8_g(g, m=m):
                    sv.wait(g)
                    dg.dinc(g.dma_start(out=xt8[:, :],
                                        in_=x1d[128 * m:128 * (m + 1), :]))
                G(p8_g)

                def p8_s(s, m=m):
                    sv.wait(s)
                    dg.wait(s)
                    if m > 0:
                        dh_.wait(s)
                    ss.inc(s.activation(sqs[:, :], xt8[:, :],
                                        AF.Square, accum_out=colB[:, 0:1]))
                S(p8_s)

                def p8_v(v, m=m):
                    ss.wait(v)
                    v.tensor_reduce(colA[:, :], xt8[:, :], AX.X, ALU.add)
                    v.tensor_scalar(mu1[:, m:m + 1], colA[:, :], 1.0 / D,
                                    None, ALU.mult)
                    v.tensor_mul(colC[:, :], mu1[:, m:m + 1], mu1[:, m:m + 1])
                    v.tensor_scalar(colD[:, :], colB[:, :], 1.0 / D, None,
                                    ALU.mult)
                    v.tensor_sub(colC[:, :], colD[:, :], colC[:, :])
                    sv.inc(v.tensor_scalar(colC[:, :], colC[:, :], EPS_LN,
                                           None, ALU.add))
                V(p8_v)

                def p8_s2(s, m=m):
                    sv.wait(s)
                    ss.inc(s.activation(colC[:, :], colC[:, :], AF.Sqrt))
                S(p8_s2)

                def p8_v2(v, m=m):
                    ss.wait(v)
                    v.reciprocal(rstd1[:, m:m + 1], colC[:, :])
                    v.drain()
                    v.scalar_tensor_tensor(
                        sqs[:, :], xt8[:, :], mu1[:, m:m + 1],
                        rstd1[:, m:m + 1].broadcast_to((128, D)),
                        ALU.subtract, ALU.mult)
                    v.tensor_reduce(colA[:, :], sqs[:, :], AX.X, ALU.max,
                                    apply_absolute_value=True)
                    v.tensor_scalar_max(gx1[:, m:m + 1], colA[:, :], 1e-5)
                    v.reciprocal(colB[:, :], gx1[:, m:m + 1])
                    v.drain()
                    v.tensor_scalar(sx1[:, m:m + 1], colB[:, :], 127.0, None,
                                    ALU.mult)
                    sv.inc(v.tensor_copy(xnt1[:, :], sqs[:, :]))
                V(p8_v2)

                def p8_y(sy, m=m):
                    sv.wait(sy)
                    o3 = xn1_fm.ap().rearrange("p (q c) -> p q c", c=128)[
                        :, KC * m:KC * (m + 1), :]
                    dh_.dinc(sy.dma_start(out=o3, in_=xnt1[:, :],
                                          transpose=True))
                Y(p8_y)

        _ps13.close()
        def p8_rel(t):
            dh_.wait(t)
        T(p8_rel)

        _ps14 = ExitStack()
        gws = _ps14.enter_context(nc.sbuf_tensor("gws", [128, 4], F32))
        if True:
            def gf_g(g):
                cc.wait(g, 1)
                dg.dinc(g.dma_start(out=gws[:, :], in_=cc_g_out[:, :]))
            G(gf_g)

            def gf_t(t):
                dg.wait(t)
                pbanks[7].guard(t)
                t.matmul(pbanks[7].ap[:, 0:4], onesf[:, :], gws[:, :],
                         start=True, stop=True)
                st.inc(t.last)
            T(gf_t)

            def gf_v(v):
                st.wait(v)
                for j, nm in enumerate(("g", "v", "o")):
                    v.tensor_scalar(gamv[nm][:, :], pbanks[7].ap[:, j:j + 1],
                                    1.0 / (D * HID), None, ALU.mult)
                    v.tensor_scalar_max(gamv[nm][:, :], gamv[nm][:, :], 1e-5)
                    v.tensor_scalar(thr[nm][:, :], gamv[nm][:, :], 0.5, None,
                                    ALU.mult)
                    v.tensor_scalar(nthr[nm][:, :], gamv[nm][:, :], -0.5, None,
                                    ALU.mult)
                sv.inc(v.engine_nop())
            V(gf_v)
            pbanks[7].consumed(sv)

        _ps14.close()
        def p8b_v(v):
            for m in range(MT):
                v.tensor_mul(colA[:, :], gx1[:, m:m + 1], gamv["g"][:, :])
                v.tensor_scalar(nsc1g[:, m:m + 1], colA[:, :], -1.0 / 127.0,
                                None, ALU.mult)
                v.tensor_mul(colA[:, :], gx1[:, m:m + 1], gamv["v"][:, :])
                v.tensor_scalar(nsc1v[:, m:m + 1], colA[:, :], -1.0 / 127.0,
                                None, ALU.mult)
            sv.inc(v.engine_nop())
        V(p8b_v)

        kvst = ExitStack()
        v1_tm = kvst.enter_context(nc.sbuf_tensor("v1_tm", [128, MT * D], BF16))
        q_fm1 = kvst.enter_context(nc.sbuf_tensor("q_fm1", [128, H * TOK], BF16))
        k_fm1 = kvst.enter_context(nc.sbuf_tensor("k_fm1", [128, H * TOK], BF16))
        nwst = ExitStack()
        qnw_s = nwst.enter_context(nc.sbuf_tensor("qnw_s", [128, D], BF16))
        knw_s = nwst.enter_context(nc.sbuf_tensor("knw_s", [128, D], BF16))

        def p9_ld(g):
            sv.wait(g)
            dg.dinc(g.dma_start(out=qnw_s[:, :], in_=qnw_r[:, :]))
            dg.dinc(g.dma_start(out=knw_s[:, :], in_=knw_r[:, :]))
        G(p9_ld)

        _ps15 = ExitStack()
        angb = _ps15.enter_context(nc.sbuf_tensor("angb", [128, 64], F32))
        rr = _ps15.enter_context(nc.sbuf_tensor("rr", [128, 64], F32))
        ki = _ps15.enter_context(nc.sbuf_tensor("ki", [128, 64], I32))
        kif = _ps15.enter_context(nc.sbuf_tensor("kif", [128, 64], F32))
        rra = _ps15.enter_context(nc.sbuf_tensor("rra", [128, 64], F32))
        if True:
            for m in range(MT):
                def rp_v(v, m=m):
                    if m > 0:
                        ss.wait(v)
                    v.tensor_scalar(colA[:, :], io_pf[:, :], float(128 * m),
                                    None, ALU.add)
                    v.tensor_add(colA[:, :], colA[:, :], posb_s[:, :])
                    v.drain()
                    v.tensor_scalar(angb[:, :], invfh_s[:, :], colA[:, 0:1],
                                    None, ALU.mult)
                    v.scalar_tensor_tensor(angb[:, :], invfl_s[:, :],
                                           colA[:, 0:1], angb[:, :], ALU.mult,
                                           ALU.add)
                    v.tensor_scalar(ki[:, :], angb[:, :], INV2PI, None,
                                    ALU.mult)
                    v.tensor_copy(kif[:, :], ki[:, :])
                    v.scalar_tensor_tensor(rr[:, :], kif[:, :], -T2PI_HI,
                                           angb[:, :], ALU.mult, ALU.add)
                    v.scalar_tensor_tensor(rr[:, :], kif[:, :], -T2PI_LO,
                                           rr[:, :], ALU.mult, ALU.add)
                    sv.inc(v.engine_nop())
                V(rp_v)

                def rp_s(s, m=m):
                    sv.wait(s)
                    s.activation(nsin8[:, m * 64:(m + 1) * 64],
                                 rr[:, :], AF.Sin, bias=0.0, scale=-1.0)
                    s.activation(rra[:, :], rr[:, :], AF.Abs)
                    ss.inc(s.activation(ncos8[:, m * 64:(m + 1) * 64],
                                        rra[:, :], AF.Sin, bias=cpi2[:, 0:1]))
                S(rp_s)

        _ps15.close()
        tap_out("ncos", src_sb=ncos8[:, :])
        tap_out("nsin", src_sb=nsin8[:, :])
        def rope_tile(v, tm, m, h, nw_s, rsc):
            x1a = tm[:, m * D + h * 128:m * D + h * 128 + 64]
            x2a = tm[:, m * D + h * 128 + 64:m * D + h * 128 + 128]
            ncm = ncos8[:, m * 64:(m + 1) * 64]
            nsm = nsin8[:, m * 64:(m + 1) * 64]
            v.scalar_tensor_tensor(x1a, x1a, rsc,
                                   nw_s[:, h * 128:h * 128 + 64], ALU.mult,
                                   ALU.mult)
            v.scalar_tensor_tensor(x2a, x2a, rsc,
                                   nw_s[:, h * 128 + 64:h * 128 + 128],
                                   ALU.mult, ALU.mult)
            v.tensor_scalar(x1a, x1a, -1.0, None, ALU.mult)
            v.tensor_scalar(x2a, x2a, -1.0, None, ALU.mult)
            t1 = sqs[:, 0:64]
            t2 = sqs[:, 64:128]
            t3 = sqs[:, 128:192]
            t4 = sqs[:, 192:256]
            v.tensor_mul(t1, x1a, ncm)
            v.tensor_mul(t2, x2a, nsm)
            v.tensor_mul(t3, x2a, ncm)
            v.tensor_mul(t4, x1a, nsm)
            v.tensor_sub(x1a, t1, t2)
            v.tensor_add(x2a, t3, t4)

        for which, (wsrc, nw_s) in enumerate([(wq1, qnw_s), (wk1, knw_s)]):
            _ps16 = ExitStack()
            qk1_tm = _ps16.enter_context(nc.sbuf_tensor(
                "qk1_tm%d" % which, [128, MT * D], BF16))
            wbufF = _ps16.enter_context(nc.sbuf_tensor(
                "wbufF%d" % which, [128, KC * 512], BF16))
            if True:
                def evac_cp(m, n, slot):
                    def ev(v, m=m, n=n, slot=slot):
                        st.wait(v)
                        sv.inc(v.tensor_copy(
                            qk1_tm[:, m * D + n * 512:m * D + n * 512 + 512],
                            slot.ap[:, 0:512]))
                    V(ev)
                    slot.consumed(sv)

                def t_wt4(t):
                    dh_.wait(t)
                T(t_wt4)
                emit_proj(tiled_src(wsrc, KC), 4, xn1_lhs, evac_cp, wbufF,
                          dbl=False)

                for m in range(MT):
                    for h in range(H):
                        def nr_s(s, m=m, h=h, qk1_tm=qk1_tm):
                            if m == 0 and h == 0:
                                sv.wait(s)
                            ss.inc(s.activation(
                                sqs[:, 256:384],
                                qk1_tm[:, m * D + h * 128:m * D + h * 128 + 128],
                                AF.Square, accum_out=colB[:, 0:1]))
                        S(nr_s)

                        def nr_v(v, m=m, h=h):
                            ss.wait(v)
                            sv.inc(v.tensor_scalar(colC[:, :], colB[:, :],
                                                   1.0 / DH, EPS_RMS, ALU.mult,
                                                   ALU.add))
                        V(nr_v)

                        def nr_s2(s, m=m, h=h):
                            sv.wait(s)
                            ss.inc(s.activation(colC[:, :], colC[:, :],
                                                AF.Sqrt))
                        S(nr_s2)

                        def nr_v2(v, m=m, h=h, qk1_tm=qk1_tm, nw_s=nw_s):
                            ss.wait(v)
                            if m == 0 and h == 0:
                                dg.wait(v)
                            v.reciprocal(colD[:, :], colC[:, :])
                            v.drain()
                            rope_tile(v, qk1_tm, m, h, nw_s, colD[:, 0:1])
                            sv.inc(v.engine_nop())
                        V(nr_v2)

                if which == 1:
                    def kc_g(g, qk1_tm=qk1_tm):
                        sv.wait(g)
                        o3 = cc_kv_in.ap()[0:TOK, :].rearrange(
                            "(t p) c -> p t c", p=128)
                        i3 = qk1_tm.ap().rearrange("p (t c) -> p t c", t=MT)
                        dg.dinc(g.dma_start(out=o3, in_=i3))
                    G(kc_g)

                dstf = q_fm1 if which == 0 else k_fm1

                def qk_tr(sy, qk1_tm=qk1_tm, dstf=dstf, which=which):
                    sv.wait(sy)
                    if which == 1:
                        dg.wait(sy)
                    for t_ in range(MT):
                        o3 = dstf.ap().rearrange("p (h q) -> p h q", h=H)[
                            :, :, 128 * t_:128 * t_ + 128]
                        dh_.dinc(sy.dma_start(
                            out=o3, in_=qk1_tm[:, t_ * D:(t_ + 1) * D],
                            transpose=True))
                Y(qk_tr)

                def t_wt5(t):
                    dh_.wait(t)
                T(t_wt5)

            _ps16.close()
        nwst.close()
        fence()
        _ps17 = ExitStack()
        wbufG = _ps17.enter_context(nc.sbuf_tensor("wbufG", [128, 2 * KC * 512], BF16))
        if True:
            def evac_v1(m, n, slot):
                def ev(v, m=m, n=n, slot=slot):
                    st.wait(v)
                    sv.inc(v.tensor_copy(
                        v1_tm[:, m * D + n * 512:m * D + n * 512 + 512],
                        slot.ap[:, 0:512]))
                V(ev)
                slot.consumed(sv)
            emit_proj(tiled_src(wv1, KC), 4, xn1_lhs, evac_v1, wbufG)

        _ps17.close()
        def vc_g(g):
            sv.wait(g)
            o3 = cc_kv_in.ap()[TOK:2 * TOK, :].rearrange("(t p) c -> p t c",
                                                         p=128)
            i3 = v1_tm.ap().rearrange("p (t c) -> p t c", t=MT)
            dg.dinc(g.dma_start(out=o3, in_=i3))
        G(vc_g)

        def kv_ag(g):
            g.wait_ge(dg.sem, dg.n)
            cc.inc(g.collective_compute(
                "AllGather", ALU.bypass,
                replica_groups=[[0, 1, 2, 3], [4, 5, 6, 7]],
                ins=[cc_kv_in.ap().opt()], outs=[cc_kv_out.ap().opt()]))
        G(kv_ag)

        if _DEV and "q1f" in taps:
            def q1tap(g):
                sv.wait(g); dh_.wait(g); cc.wait(g)
                dg.dinc(g.dma_start(out=taps["q1f"].ap(), in_=q_fm1[:, :]))
                dg.dinc(g.dma_start(out=taps["k1f"].ap(), in_=k_fm1[:, :]))
                dg.dinc(g.dma_start(out=taps["v1t"].ap(), in_=v1_tm[:, :]))
            G(q1tap)

        a1st = ExitStack()
        attn1 = a1st.enter_context(nc.sbuf_tensor("attn1", [128, MT * D],
                                                  BF16))

        _ps18 = ExitStack()
        mask8 = _ps18.enter_context(nc.sbuf_tensor("mask8", [128, MT * 1152], BF16))
        khalo = _ps18.enter_context(nc.sbuf_tensor("khalo", [128, TOK], BF16))
        vhalo = _ps18.enter_context(nc.sbuf_tensor("vhalo", [128, MT * 128], BF16))
        pbf = _ps18.enter_context(nc.sbuf_tensor("pbf", [128, 1152], BF16))
        pt = _ps18.enter_context(nc.sbuf_tensor("pt", [128, 9 * 128], BF16))
        if True:
            def mk_g(g):
                sv.wait(g)
                dg.dinc(g.dma_start(out=mask8[:, :], in_=mask8_r[:, :]))
            G(mk_g)

            ps_sc = [pbanks[0], pbanks[1], pbanks[2]]
            ps_pv = pbanks[3]

            creg = [None]
            for h in range(H):
                def halo_y(sy, h=h):
                    if h == 0:
                        cc.wait(sy)
                    sv.wait(sy)
                    st.wait(sy)
                    if creg[0] is None:
                        creg[0] = sy.alloc_register("cid")
                        sy.reg_load(creg[0], chid_s[0:1, 0:1])
                    cid = creg[0]
                    if True:
                        for p_ in range(4):
                            base = max(p_ - 1, 0) * 2 * TOK
                            with sy.If_eq(cid, p_):
                                dh_.dinc(sy.dma_start(
                                    out=khalo[:, :],
                                    in_=cc_kv_out.ap()[base:base + TOK,
                                                       h * 128:(h + 1) * 128],
                                    transpose=True))
                                dh_.dinc(sy.dma_start(
                                    out=vhalo.ap().rearrange(
                                        "p (t c) -> p t c", t=MT),
                                    in_=cc_kv_out.ap()[
                                        base + TOK:base + 2 * TOK,
                                        h * 128:(h + 1) * 128].rearrange(
                                        "(t p) c -> p t c", p=128)))
                    dh_.n += 0
                Y(halo_y)
                dh_.n -= 3 * 32

                for t_ in range(MT):
                    hl = TOK - 128 * t_

                    def sc_mm(t, h=h, t_=t_, hl=hl):
                        if t_ == 0:
                            dh_.wait(t)
                        for ci, wlo in enumerate((0, 512, 1024)):
                            wid = 512 if ci < 2 else 128
                            ps_sc[ci].guard(t)
                            segs = []
                            j0 = wlo
                            while j0 < wlo + wid:
                                if j0 < hl:
                                    seg = min(hl - j0, wlo + wid - j0)
                                    src = khalo[:, 128 * t_ + j0:
                                                128 * t_ + j0 + seg]
                                else:
                                    seg = wlo + wid - j0
                                    src = k_fm1[:, h * TOK + (j0 - hl):
                                                h * TOK + (j0 - hl) + seg]
                                segs.append((j0 - wlo, seg, src))
                                j0 += seg
                            for off, seg, src in segs:
                                t.matmul(ps_sc[ci].ap[:, off:off + seg],
                                         q_fm1[:, h * TOK + 128 * t_:
                                               h * TOK + 128 * t_ + 128],
                                         src, start=True, stop=True)
                        st.inc(t.last)
                    T(sc_mm)

                    def sm_v(v, h=h, t_=t_):
                        st.wait(v)
                        v.tensor_reduce(colA[:, :], ps_sc[0].ap[:, 0:512],
                                        AX.X, ALU.max, negate=True)
                        v.tensor_reduce(colB[:, :], ps_sc[1].ap[:, 0:512],
                                        AX.X, ALU.max, negate=True)
                        v.tensor_reduce(colC[:, :], ps_sc[2].ap[:, 0:128],
                                        AX.X, ALU.max, negate=True)
                        v.tensor_tensor(colA[:, :], colA[:, :], colB[:, :],
                                        ALU.min)
                        v.tensor_tensor(colA[:, :], colA[:, :], colC[:, :],
                                        ALU.min)
                        sv.inc(v.tensor_scalar(colD[:, :], colA[:, :], ISQD,
                                               None, ALU.mult))
                    V(sm_v)

                    def sm_s(s, h=h, t_=t_):
                        sv.wait(s)
                        dh_.wait(s, dh_.n)
                        s.activation(pbf[:, 0:512], ps_sc[0].ap[:, 0:512],
                                     AF.Exp, bias=colD[:, 0:1], scale=ISQD)
                        s.activation(pbf[:, 512:1024], ps_sc[1].ap[:, 0:512],
                                     AF.Exp, bias=colD[:, 0:1], scale=ISQD)
                        ss.inc(s.activation(pbf[:, 1024:1152],
                                            ps_sc[2].ap[:, 0:128], AF.Exp,
                                            bias=colD[:, 0:1], scale=ISQD))
                    S(sm_s)
                    for sl in ps_sc:
                        sl.consumed(ss)

                    def pm_v(v, h=h, t_=t_):
                        ss.wait(v)
                        if h == 0 and t_ == 0:
                            dg.wait(v)
                        v.tensor_mul(pbf[:, :], pbf[:, :],
                                     mask8[:, t_ * 1152:(t_ + 1) * 1152])
                        v.tensor_reduce(colA[:, :], pbf[:, :], AX.X, ALU.add)
                        v.reciprocal(colB[:, :], colA[:, :])
                        v.drain()
                        sv.inc(v.engine_nop())
                    V(pm_v)

                    def pt_y(sy, h=h, t_=t_):
                        sv.wait(sy)
                        o3 = pt.ap().rearrange("p (q c) -> p q c", c=128)
                        dh_.dinc(sy.dma_start(out=o3, in_=pbf[:, :],
                                              transpose=True))
                    Y(pt_y)

                    def pv_mm(t, h=h, t_=t_, hl=hl):
                        dh_.wait(t)
                        ps_pv.guard(t)
                        for kb in range(9):
                            j0 = 128 * kb
                            if j0 < hl:
                                vsl = vhalo[:, (t_ + kb) * 128:
                                            (t_ + kb) * 128 + 128]
                            else:
                                mo = kb - (MT - t_)
                                vsl = v1_tm[:, mo * D + h * 128:
                                            mo * D + h * 128 + 128]
                            t.matmul(ps_pv.ap[:, 0:128],
                                     pt[:, kb * 128:kb * 128 + 128], vsl,
                                     start=(kb == 0), stop=(kb == 8))
                        st.inc(t.last)
                    T(pv_mm)

                    def pv_ev(v, h=h, t_=t_):
                        st.wait(v)
                        sv.inc(v.tensor_scalar(
                            attn1[:, t_ * D + h * 128:t_ * D + h * 128 + 128],
                            ps_pv.ap[:, 0:128], colB[:, 0:1], None, ALU.mult))
                    V(pv_ev)
                    ps_pv.consumed(sv)

        _ps18.close()
        def a1_st(g):
            sv.wait(g)
            dg.dinc(g.dma_start(
                out=attn1d.ap().rearrange("(t p) c -> p t c", p=128),
                in_=attn1.ap().rearrange("p (t c) -> p t c", t=MT)))
        G(a1_st)

        def a1_rel(t):
            dg.wait(t)
        T(a1_rel)
        a1st.close()
        fence()

        _ps19 = ExitStack()
        a1_fm = _ps19.enter_context(nc.sbuf_tensor("a1_fm", [128, MT * D], BF16))
        wbufH = _ps19.enter_context(nc.sbuf_tensor("wbufH", [128, KC * 512], BF16))
        s1t = _ps19.enter_context(nc.sbuf_tensor("s1t", [128, 512], BF16))
        if True:
            def a1_tr(sy):
                dg.wait(sy)
                sg.wait(sy)
                for m in range(MT):
                    o3 = a1_fm.ap().rearrange("p (q c) -> p q c", c=128)[
                        :, KC * m:KC * (m + 1), :]
                    dh_.dinc(sy.dma_start(
                        out=o3, in_=attn1d.ap()[128 * m:128 * (m + 1), :],
                        transpose=True))
            Y(a1_tr)

            def t_wt6(t):
                dh_.wait(t)
            T(t_wt6)

            def a1_lhs(m, k):
                return a1_fm[:, (m * KC + k) * 128:(m * KC + k) * 128 + 128]

            def evac_s1(m, n, slot):
                def ev(v, m=m, n=n, slot=slot):
                    st.wait(v)
                    dg.wait(v)
                    sv.inc(v.tensor_copy(s1t[:, :], slot.ap[:, 0:512]))
                V(ev)
                slot.consumed(sv)

                def stg(g, m=m, n=n):
                    sv.wait(g)
                    dg.dinc(g.dma_start(
                        out=s1d[128 * m:128 * (m + 1), 512 * n:512 * (n + 1)],
                        in_=s1t[:, :]))
                G(stg)
            emit_proj(tiled_src(wo1, KC), 4, a1_lhs, evac_s1, wbufH,
                      dbl=False)

        _ps19.close()
        kvst.close()
        xfst.close()
        fence()
        _ps20 = ExitStack()
        xq_fm = _ps20.enter_context(nc.sbuf_tensor("xq_fm", [128, MT * D], BF16))
        xti = _ps20.enter_context(nc.sbuf_tensor("xti", [128, D], I32))
        xtb = _ps20.enter_context(nc.sbuf_tensor("xtb", [128, D], BF16))
        x1tt = _ps20.enter_context(nc.sbuf_tensor("x1tt", [128, D], F32))
        if True:
            for m in range(MT):
                def qa_g(g, m=m):
                    if m > 0:
                        sv.wait(g)
                    dg.dinc(g.dma_start(out=x1tt[:, :],
                                        in_=x1d[128 * m:128 * (m + 1), :]))
                G(qa_g)

                def qa_v(v, m=m):
                    dg.wait(v)
                    if m > 0:
                        dh_.wait(v)
                    v.scalar_tensor_tensor(
                        sqs[:, :], x1tt[:, :], mu1[:, m:m + 1],
                        rstd1[:, m:m + 1].broadcast_to((128, D)),
                        ALU.subtract, ALU.mult)
                    v.tensor_scalar(xti[:, :], sqs[:, :], sx1[:, m:m + 1],
                                    None, ALU.mult)
                    sv.inc(v.tensor_copy(xtb[:, :], xti[:, :]))
                V(qa_v)

                def qa_y(sy, m=m):
                    sv.wait(sy)
                    o3 = xq_fm.ap().rearrange("p (q c) -> p q c", c=128)[
                        :, KC * m:KC * (m + 1), :]
                    dh_.dinc(sy.dma_start(out=o3, in_=xtb[:, :],
                                          transpose=True))
                Y(qa_y)

            def xq_lhs(m, k):
                return xq_fm[:, (m * KC + k) * 128:(m * KC + k) * 128 + 128]

            _ps21 = ExitStack()
            wfq = _ps21.enter_context(nc.sbuf_tensor("wfq", [128, KC * 512], F32))
            wqn = _ps21.enter_context(nc.sbuf_tensor("wqn", [128, 2 * KC * 512], BF16))
            gfull = _ps21.enter_context(nc.sbuf_tensor("gfull", [128, MT * 512], F32))
            sgf = _ps21.enter_context(nc.sbuf_tensor("sgf", [128, MT * 512], BF16))
            bsl = _ps21.enter_context(nc.sbuf_tensor("bsl", [128, 512], F32))
            mct = _ps21.enter_context(nc.sbuf_tensor("mct", [128, 2 * 512], BF16))
            if True:
                def t_wt7(t):
                    dh_.wait(t)
                T(t_wt7)

                def p13_v0(v):
                    v.memset(gx2[:, :], 0.0)
                    sv.inc(v.engine_nop())
                V(p13_v0)

                for n in range(HID // 512):
                    for wi, (wsrc, nm, bias_r, nsc) in enumerate(
                            [(wgt1, "g", gb_r, nsc1g), (wvt1, "v", vb_r,
                                                        nsc1v)]):
                        wb = (2 * n + wi) % 2

                        def qw_g(g, n=n, wi=wi, wsrc=wsrc, bias_r=bias_r):
                            if n + wi > 0:
                                sv.wait(g)
                            src = wsrc.ap().rearrange(
                                "p (n r) -> p n r", r=KC * 512)[:, n]
                            dg.dinc(g.dma_start(out=wfq.ap(), in_=src))
                            dg.dinc(g.dma_start(
                                out=bsl[:, :],
                                in_=bias_r[:, 512 * n:512 * (n + 1)]))
                        G(qw_g)

                        def qw_v(v, n=n, wi=wi, nm=nm, wb=wb):
                            dg.wait(v)
                            if 2 * n + wi >= 2:
                                st.wait(v)
                            v.tensor_scalar(wqn[:, wb * KC * 512:(wb + 1) * KC * 512], wfq[:, :],
                                            thr[nm][:, 0:1], None, ALU.is_ge)
                            v.scalar_tensor_tensor(wqn[:, wb * KC * 512:(wb + 1) * KC * 512], wfq[:, :],
                                                   nthr[nm][:, 0:1],
                                                   wqn[:, wb * KC * 512:(wb + 1) * KC * 512], ALU.is_le,
                                                   ALU.subtract)
                            sv.inc(v.engine_nop())
                        V(qw_v)

                        for m in range(MT):
                            slot = pbanks[4 + m % 2] if wi else pbanks[m % 2]

                            def bq_mm(t, n=n, wi=wi, m=m, wb=wb, slot=slot):
                                if m == 0:
                                    sv.wait(t)
                                slot.guard(t)
                                wv = wqn[:, wb * KC * 512:(wb + 1) * KC * 512].rearrange(
                                    "p (k n) -> p k n", k=KC)
                                for k in range(KC):
                                    t.matmul(slot.ap[:, 0:512], xq_lhs(m, k),
                                             wv[:, k], start=(k == 0),
                                             stop=(k == KC - 1))
                                st.inc(t.last)
                            T(bq_mm)

                            if wi == 0:
                                def g_ev(v, n=n, m=m, slot=slot):
                                    st.wait(v)
                                    sv.inc(v.scalar_tensor_tensor(
                                        gfull[:, m * 512:(m + 1) * 512],
                                        slot.ap[:, 0:512],
                                        nsc1g[:, m:m + 1], bsl[:, :],
                                        ALU.mult, ALU.add))
                                V(g_ev)
                                slot.consumed(sv)

                                def g_sil(s, n=n, m=m):
                                    sv.wait(s)
                                    ss.inc(s.activation(
                                        sgf[:, m * 512:(m + 1) * 512],
                                        gfull[:, m * 512:(m + 1) * 512],
                                        AF.Silu))
                                S(g_sil)
                            else:
                                mb = m % 2

                                def v_ev(v, n=n, m=m, mb=mb, slot=slot):
                                    st.wait(v)
                                    ss.wait(v)
                                    if not (n == 0 and m < 2):
                                        dg.wait(v, dg.n)
                                    v.scalar_tensor_tensor(
                                        sqs[:, 0:512], slot.ap[:, 0:512],
                                        nsc1v[:, m:m + 1], bsl[:, :],
                                        ALU.mult, ALU.add)
                                    v.tensor_mul(mct[:, mb * 512:mb * 512 + 512],
                                                 sgf[:, m * 512:(m + 1) * 512],
                                                 sqs[:, 0:512])
                                    v.tensor_reduce(colA[:, :], mct[:, mb * 512:mb * 512 + 512],
                                                    AX.X, ALU.max,
                                                    apply_absolute_value=True)
                                    sv.inc(v.tensor_tensor(
                                        gx2[:, m:m + 1], gx2[:, m:m + 1],
                                        colA[:, :], ALU.max))
                                V(v_ev)
                                slot.consumed(sv)

                                def mc_g(g, n=n, m=m, mb=mb):
                                    sv.wait(g)
                                    dg.dinc(g.dma_start(
                                        out=midd[128 * m:128 * (m + 1),
                                                 512 * n:512 * (n + 1)],
                                        in_=mct[:, mb * 512:mb * 512 + 512]))
                                G(mc_g)

                def p13b_v(v):
                    sv.wait(v) if False else None
                    for m in range(MT):
                        v.tensor_scalar_max(gx2[:, m:m + 1], gx2[:, m:m + 1],
                                            1e-5)
                        v.reciprocal(colA[:, :], gx2[:, m:m + 1])
                        v.drain()
                        v.tensor_scalar(sx2[:, m:m + 1], colA[:, :], 127.0,
                                        None, ALU.mult)
                        v.tensor_mul(colA[:, :], gx2[:, m:m + 1],
                                     gamv["o"][:, :])
                        v.tensor_scalar(nsc3[:, m:m + 1], colA[:, :],
                                        -1.0 / 127.0, None, ALU.mult)
                    sv.inc(v.engine_nop())
                V(p13b_v)

            _ps21.close()
        _ps20.close()
        _ps22 = ExitStack()
        midq_fm = _ps22.enter_context(nc.sbuf_tensor("midq_fm", [128, 4 * HID], BF16))
        mbq = _ps22.enter_context(nc.sbuf_tensor("mbq", [128, HID], BF16))
        mbi = _ps22.enter_context(nc.sbuf_tensor("mbi", [128, D], I32))
        mbb = _ps22.enter_context(nc.sbuf_tensor("mbb", [128, HID], BF16))
        wfq2 = _ps22.enter_context(nc.sbuf_tensor("wfq2", [128, KC * 512], F32))
        wqn2 = _ps22.enter_context(nc.sbuf_tensor("wqn2", [128, 2 * KC * 512], BF16))
        obsl = _ps22.enter_context(nc.sbuf_tensor("obsl", [128, 512], F32))
        s1tt = _ps22.enter_context(nc.sbuf_tensor("s1tt", [128, 512], BF16))
        x1tt2 = _ps22.enter_context(nc.sbuf_tensor("x1tt2", [128, 512], F32))
        ot = _ps22.enter_context(nc.sbuf_tensor("ot", [128, 512], F32))
        tf2 = _ps22.enter_context(nc.sbuf_tensor("tf2", [128, 512], F32))
        if True:
            for half in range(2):
                for mi in range(4):
                    m = 4 * half + mi

                    def mq_g(g, m=m, mi=mi, half=half):
                        sv.wait(g)
                        if half > 0 and mi == 0:
                            st.wait(g)
                        dg.dinc(g.dma_start(
                            out=mbq[:, :],
                            in_=midd[128 * m:128 * (m + 1), :]))
                    G(mq_g)

                    def mq_v(v, m=m, mi=mi, half=half):
                        dg.wait(v)
                        if half + mi > 0:
                            dh_.wait(v)
                        for cq in range(HID // D):
                            v.tensor_scalar(mbi[:, :],
                                            mbq[:, cq * D:(cq + 1) * D],
                                            sx2[:, m:m + 1], None, ALU.mult)
                            v.tensor_copy(mbb[:, cq * D:(cq + 1) * D],
                                          mbi[:, :])
                        sv.inc(v.engine_nop())
                    V(mq_v)

                    def mq_y(sy, m=m, mi=mi):
                        sv.wait(sy)
                        o3 = midq_fm.ap().rearrange("p (q c) -> p q c",
                                                    c=128)[
                            :, 64 * mi:64 * (mi + 1), :]
                        dh_.dinc(sy.dma_start(out=o3, in_=mbb[:, :],
                                              transpose=True))
                    Y(mq_y)

                def mq_lhs(mi, kk):
                    return midq_fm[:, (mi * 64 + kk) * 128:
                                   (mi * 64 + kk) * 128 + 128]

                def t_wt8(t, half=half):
                    dh_.wait(t)
                T(t_wt8)

                for n in range(4):
                    for kg in range(4):
                        wb = kg % 2

                        def dw_g(g, n=n, kg=kg, wb=wb, half=half):
                            if half + n + kg > 0:
                                sv.wait(g)
                            src = wot1.ap().rearrange(
                                "p (c r) -> p c r", r=KC * 512)[:, n * 4 + kg]
                            dg.dinc(g.dma_start(out=wfq2.ap(), in_=src))
                        G(dw_g)

                        def dw_v(v, n=n, kg=kg, wb=wb, half=half):
                            dg.wait(v)
                            if half + n + kg >= 1:
                                st.wait(v)
                            v.tensor_scalar(wqn2[:, wb * KC * 512:(wb + 1) * KC * 512], wfq2[:, :],
                                            thr["o"][:, 0:1], None, ALU.is_ge)
                            v.scalar_tensor_tensor(wqn2[:, wb * KC * 512:(wb + 1) * KC * 512], wfq2[:, :],
                                                   nthr["o"][:, 0:1],
                                                   wqn2[:, wb * KC * 512:(wb + 1) * KC * 512], ALU.is_le,
                                                   ALU.subtract)
                            sv.inc(v.engine_nop())
                        V(dw_v)

                        for mi in range(4):
                            slot = pbanks[mi]

                            def dn_mm(t, n=n, kg=kg, mi=mi, wb=wb, slot=slot):
                                if mi == 0:
                                    sv.wait(t)
                                if kg == 0:
                                    slot.guard(t)
                                wv = wqn2[:, wb * KC * 512:(wb + 1) * KC * 512].rearrange(
                                    "p (k n) -> p k n", k=KC)
                                for k in range(KC):
                                    t.matmul(slot.ap[:, 0:512],
                                             mq_lhs(mi, kg * 16 + k),
                                             wv[:, k],
                                             start=(kg == 0 and k == 0),
                                             stop=(kg == 3 and k == KC - 1))
                                st.inc(t.last)
                            T(dn_mm)

                    for mi in range(4):
                        m = 4 * half + mi
                        slot = pbanks[mi]

                        def fin_g(g, n=n, m=m):
                            sv.wait(g)
                            dg.dinc(g.dma_start(
                                out=obsl[:, :],
                                in_=ob_r[:, 512 * n:512 * (n + 1)]))
                            dg.dinc(g.dma_start(
                                out=s1tt[:, :],
                                in_=s1d[128 * m:128 * (m + 1),
                                        512 * n:512 * (n + 1)]))
                            dg.dinc(g.dma_start(
                                out=x1tt2[:, :],
                                in_=x1d[128 * m:128 * (m + 1),
                                        512 * n:512 * (n + 1)]))
                        G(fin_g)

                        def fin_v(v, n=n, m=m, slot=slot):
                            st.wait(v)
                            dg.wait(v)
                            sv.inc(v.scalar_tensor_tensor(
                                tf2[:, :], slot.ap[:, 0:512],
                                nsc3[:, m:m + 1], obsl[:, :], ALU.mult,
                                ALU.add))
                        V(fin_v)
                        slot.consumed(sv)

                        def fin_s(s, n=n, m=m):
                            sv.wait(s)
                            ss.inc(s.activation(tf2[:, :], tf2[:, :],
                                                AF.Silu))
                        S(fin_s)

                        def fin_v2(v, n=n, m=m):
                            ss.wait(v)
                            v.tensor_mul(ot[:, :], tf2[:, :], s1tt[:, :])
                            sv.inc(v.tensor_add(ot[:, :], ot[:, :],
                                                x1tt2[:, :]))
                        V(fin_v2)

                        def fin_g2(g, n=n, m=m):
                            sv.wait(g)
                            dg.dinc(g.dma_start(
                                out=out[128 * m:128 * (m + 1),
                                        512 * n:512 * (n + 1)],
                                in_=ot[:, :]))
                        G(fin_g2)

        _ps22.close()
        if _DEV:
            dram_map = {"qd": qd_d, "kd": kd_d, "vd": vd_d, "s0": s0d,
                        "acc": accd, "x1": x1d, "a1": attn1d, "s1": s1d,
                        "mid": midd, "ccg": cc_g_out, "cca": cc_a_out,
                        "cckv": cc_kv_out}
            def tg(g):
                sv.wait(g); ss.wait(g); st.wait(g); dh_.wait(g)
                for tname in taps:
                    if tname in dram_map:
                        dg.dinc(g.dma_start(out=taps[tname].ap(),
                                            in_=dram_map[tname].ap()))
            G(tg)

        def fin_wait(g):
            g.wait_ge(dg.sem, dg.n)
            g.wait_ge(dh_.sem, dh_.n)
        G(fin_wait)

    return nc


_NC_CACHE = {}


def _pretile(w):
    K, N = w.shape
    return np.ascontiguousarray(
        w.reshape(K // 128, 128, N).transpose(1, 0, 2).reshape(128, -1))


def _pretile_c(w, nb=512):
    K, N = w.shape
    kc = K // 128
    t = w.reshape(kc, 128, N // nb, nb).transpose(1, 2, 0, 3)
    return np.ascontiguousarray(t.reshape(128, -1))


def _pretile_g(w, kgs=4, nb=512):
    K, N = w.shape
    kc = K // 128
    kk = kc // kgs
    t = w.reshape(kgs, kk, 128, N // nb, nb).transpose(2, 3, 0, 1, 4)
    return np.ascontiguousarray(t.reshape(128, -1))


def kernel(L0_attn_hidden_states, L0_attn_slope_rate, L0_attn_w_qkv,
           L0_attn_w_gate, L0_attn_w_out, L0_attn_norm_w, L0_ffn_Wg,
           L0_ffn_Wu, L0_ffn_Wd, L0_ffn_Ga, L0_ffn_Gb, L0_ffn_Ua, L0_ffn_Ub,
           L0_ffn_Da, L0_ffn_Db, L0_ffn_Wr, L0_ffn_temperature, L1_attn_Wq,
           L1_attn_Wk, L1_attn_Wv, L1_attn_Wo, L1_attn_q_norm_w,
           L1_attn_k_norm_w, L1_ffn_gate_w, L1_ffn_gate_b, L1_ffn_val_w,
           L1_ffn_val_b, L1_ffn_out_w, L1_ffn_out_b):
    f32 = np.float32
    asf = lambda a: np.ascontiguousarray(a, dtype=f32)

    x = asf(L0_attn_hidden_states)
    rep = lambda vec: np.ascontiguousarray(
        np.tile(np.asarray(vec, f32).reshape(1, -1), (128, 1)))

    shared = {
        "wqkv": _pretile_c(asf(L0_attn_w_qkv).T),
        "wgate": _pretile_c(asf(L0_attn_w_gate).T),
        "wout": _pretile_c(asf(L0_attn_w_out).T),
        "wg": np.stack([_pretile_c(asf(L0_ffn_Wg[e])) for e in range(E)]),
        "wu": np.stack([_pretile_c(asf(L0_ffn_Wu[e])) for e in range(E)]),
        "wd": np.stack([_pretile_c(asf(L0_ffn_Wd[e])) for e in range(E)]),
        "ga": np.stack([_pretile(asf(L0_ffn_Ga[l])) for l in range(2)]),
        "ua": np.stack([_pretile(asf(L0_ffn_Ua[l])) for l in range(2)]),
        "da": np.stack([_pretile(asf(L0_ffn_Da[l])) for l in range(2)]),
        "gbp": np.stack([np.concatenate(
            [asf(L0_ffn_Gb[l]), np.zeros((64, F), f32)], axis=0)
            for l in range(2)]),
        "ubp": np.stack([np.concatenate(
            [asf(L0_ffn_Ub[l]), np.zeros((64, F), f32)], axis=0)
            for l in range(2)]),
        "dbp": np.stack([np.concatenate(
            [asf(L0_ffn_Db[l]), np.zeros((64, D), f32)], axis=0)
            for l in range(2)]),
        "wr": _pretile(asf(L0_ffn_Wr)),
        "wq1": _pretile_c(asf(L1_attn_Wq).T),
        "wk1": _pretile_c(asf(L1_attn_Wk).T),
        "wv1": _pretile_c(asf(L1_attn_Wv).T),
        "wo1": _pretile_c(asf(L1_attn_Wo).T),
        "wgt1": _pretile_c(asf(L1_ffn_gate_w).T),
        "wvt1": _pretile_c(asf(L1_ffn_val_w).T),
        "wot1": _pretile_g(asf(L1_ffn_out_w).T),
        "gb_r": rep(L1_ffn_gate_b),
        "vb_r": rep(L1_ffn_val_b),
        "ob_r": rep(L1_ffn_out_b),
        "l0nw_r": rep(L0_attn_norm_w),
        "qnw_r": rep(np.tile(np.asarray(L1_attn_q_norm_w, f32), H)),
        "knw_r": rep(np.tile(np.asarray(L1_attn_k_norm_w, f32), H)),
        "slope_r": rep(L0_attn_slope_rate),
        "temp_r": rep(L0_ffn_temperature),
        "arq": np.ascontiguousarray(
            np.arange(128, dtype=f32)[:, None] + np.array([1.0, 129.0], f32)),
        "ark": np.ascontiguousarray(
            np.array([255.0, 127.0], f32) - np.arange(128, dtype=f32)[:, None]),
        "invf_hi": None, "invf_lo": None,
    }
    inv64 = 1.0 / (10000.0 ** (np.arange(0, 128, 2, dtype=np.float64) / 128.0))
    hi = inv64.astype(f32)
    lo = (inv64 - hi.astype(np.float64)).astype(f32)
    shared["invf_hi"] = np.ascontiguousarray(np.tile(hi, (128, 1)))
    shared["invf_lo"] = np.ascontiguousarray(np.tile(lo, (128, 1)))
    shared.pop("invf_hi", None) if False else None
    ii = np.arange(128)[:, None]
    jj = np.arange(1152)[None, :]
    band = ((jj > ii) & (jj <= ii + 1024))

    gwT = asf(L1_ffn_gate_w).T
    vwT = asf(L1_ffn_val_w).T
    owT = asf(L1_ffn_out_w).T

    in_maps = []
    for c in range(8):
        b, p = divmod(c, 4)
        im = dict(shared)
        im["x0"] = np.ascontiguousarray(x[b, 1024 * p:1024 * (p + 1), :])
        im["posb"] = np.full((128, 1), 1024.0 * p, f32)
        im["chunkid"] = np.array([[p]], np.int32)
        dist = np.zeros((128, 4), f32)
        msk = np.zeros((128, 4), f32)
        for cp in range(4):
            if cp < p:
                dist[:, cp] = 256.0 * (4 * (p - 1 - cp))
                msk[:, cp] = 1.0
        im["ldist"] = dist
        im["lmask"] = msk
        m8 = np.zeros((128, MT * 1152), ml_dtypes.bfloat16)
        for t_ in range(MT):
            valid = (jj >= 1024 - 128 * t_ - 1024 * p)
            m8[:, t_ * 1152:(t_ + 1) * 1152] = (band & valid).astype(
                ml_dtypes.bfloat16)
        im["mask8_r"] = m8
        im["gw_sl"] = np.ascontiguousarray(gwT[256 * c:256 * (c + 1), :])
        im["vw_sl"] = np.ascontiguousarray(vwT[256 * c:256 * (c + 1), :])
        im["ow_sl"] = np.ascontiguousarray(owT[1024 * c:1024 * (c + 1), :])
        in_maps.append(im)

    key = "main"
    if key not in _NC_CACHE:
        _NC_CACHE[key] = build()
    nc = _NC_CACHE[key]
    res = run_bass_kernel_spmd(nc, in_maps, core_ids=list(range(8)))

    outp = np.empty((2, 4096, D), f32)
    for c in range(8):
        b, p = divmod(c, 4)
        outp[b, 1024 * p:1024 * (p + 1), :] = res.results[c]["out"]
    return outp

